# version 07
# baseline (speedup 1.0000x reference)
import numpy as np
import ml_dtypes
import concourse.bass as bass
import concourse.mybir as mybir
from concourse.bass_utils import run_bass_kernel_spmd

F32 = mybir.dt.float32
BF16 = mybir.dt.bfloat16
AF = mybir.ActivationFunctionType
ALU = mybir.AluOpType

NCORES = 8
D = 1024
DC = 8
DI = 2048
IC = 16
NS = 16
DTR = 64
T = 2048
I_FFN = 2816
ICF = 22
EPS = 1e-5
RND = 12582912.0
BF = ml_dtypes.bfloat16

ENGS = ("sync", "pe", "act", "dve", "pool")
DBG = False


class Prog:

    def __init__(self):
        self.ops = {e: [] for e in ENGS}
        self.cnt = {e: 0 for e in ENGS}
        self._pending = {e: [] for e in ENGS}

    def do(self, eng, fn, waits=(), inc=1):
        self.cnt[eng] += inc
        w = tuple(self._pending[eng]) + tuple(x for x in waits if x is not None)
        self._pending[eng] = []
        self.ops[eng].append((fn, w, inc))
        return (eng, self.cnt[eng])

    def ev(self, eng):
        return (eng, self.cnt[eng])

    def barrier(self):
        evs = [(e, self.cnt[e]) for e in ENGS if self.cnt[e] > 0]
        for e in ENGS:
            self._pending[e].extend(evs)

    def emit(self, block, sems):
        def mk(eng_name):
            def body(eng_obj):
                for fn, waits, inc in self.ops[eng_name]:
                    best = {}
                    for (en, c) in waits:
                        if c > best.get(en, 0):
                            best[en] = c
                    for en, c in best.items():
                        eng_obj.wait_ge(sems[en], c)
                    fn(eng_obj).then_inc(sems[eng_name], inc)
            return body

        block.sync(mk("sync"))
        block.tensor(mk("pe"))
        block.scalar(mk("act"))
        block.vector(mk("dve"))
        block.gpsimd(mk("pool"))


def _emit_all(nc, p):
    with (
        nc.semaphore("s_sync") as s_sync,
        nc.semaphore("s_pe") as s_pe,
        nc.semaphore("s_act") as s_act,
        nc.semaphore("s_dve") as s_dve,
        nc.semaphore("s_pool") as s_pool,
        nc.Block() as block,
    ):
        sems = dict(sync=s_sync, pe=s_pe, act=s_act, dve=s_dve, pool=s_pool)
        p.emit(block, sems)


def DMA(out, in_):
    return lambda e: e.dma_start(out=out, in_=in_)



def build_l1(TT, NT):
    nc = bass.Bass("TRN2", target_bir_lowering=False)
    xT = nc.dram_tensor("xT", [D, T], F32, kind="ExternalInput").ap()
    c_col = nc.dram_tensor("c_col", [128, DC], F32, kind="ExternalInput").ap()
    wmod = nc.dram_tensor("wmod", [D, 6 * D], F32, kind="ExternalInput").ap()
    w_in = nc.dram_tensor("w_in", [D, 2 * DI], F32, kind="ExternalInput").ap()
    w_x = nc.dram_tensor("w_x", [DI, 96], F32, kind="ExternalInput").ap()
    w_dt = nc.dram_tensor("w_dt", [DTR, DI], F32, kind="ExternalInput").ap()
    w_out = nc.dram_tensor("w_out", [DI, D], F32, kind="ExternalInput").ap()
    alog_col = nc.dram_tensor("alog_col", [128, IC * NS], F32, kind="ExternalInput").ap()
    normw_col = nc.dram_tensor("normw_col", [128, DC], F32, kind="ExternalInput").ap()
    convw_col = nc.dram_tensor("convw_col", [128, IC * 4], F32, kind="ExternalInput").ap()
    convb_col = nc.dram_tensor("convb_col", [128, IC], F32, kind="ExternalInput").ap()
    bdt_col = nc.dram_tensor("bdt_col", [128, IC], F32, kind="ExternalInput").ap()
    dvec_col = nc.dram_tensor("dvec_col", [128, IC], F32, kind="ExternalInput").ap()
    onehot = nc.dram_tensor("onehot", [32, 32 * 128], BF16, kind="ExternalInput").ap()
    identf = nc.dram_tensor("identf", [128, 128], F32, kind="ExternalInput").ap()
    o_xmid = nc.dram_tensor("o_xmid", [D, T], F32, kind="ExternalOutput").ap()
    o_h2 = nc.dram_tensor("o_h2", [D, T], F32, kind="ExternalOutput").ap()
    o_max = nc.dram_tensor("o_max", [1, 1], F32, kind="ExternalOutput").ap()
    o_gb = nc.dram_tensor("o_gb", [128, DC], F32, kind="ExternalOutput").ap()
    o_mc = nc.dram_tensor("o_mc", [128, 48], F32, kind="ExternalOutput").ap()
    if DBG:
        o_hid = nc.dram_tensor("o_hid", [128, DC * TT], BF16, kind="ExternalOutput").ap()
        o_xs = nc.dram_tensor("o_xs", [128, IC * TT], BF16, kind="ExternalOutput").ap()
        o_zb = nc.dram_tensor("o_zb", [128, IC * TT], BF16, kind="ExternalOutput").ap()
        o_dtb = nc.dram_tensor("o_dtb", [128, IC * TT], BF16, kind="ExternalOutput").ap()
        o_ya = nc.dram_tensor("o_ya", [128, IC * TT], BF16, kind="ExternalOutput").ap()
        o_ga2 = nc.dram_tensor("o_ga2", [128, DC], F32, kind="ExternalOutput").ap()
        o_ball = nc.dram_tensor("o_ball", [128, IC * (TT + 1)], BF16, kind="ExternalOutput").ap()
        o_hall = nc.dram_tensor("o_hall", [128, IC * (TT + 1)], BF16, kind="ExternalOutput").ap()
        o_bcsb = nc.dram_tensor("o_bcsb", [128, 2 * TT], BF16, kind="ExternalOutput").ap()
        o_cdbg = nc.dram_tensor("o_cdbg", [128, 6 * DC], F32, kind="ExternalOutput").ap()
        o_ga_pre = nc.dram_tensor("o_ga_pre", [128, DC], F32, kind="ExternalOutput").ap()
        o_acl = nc.dram_tensor("o_acl", [128, IC * NS], F32, kind="ExternalOutput").ap()
    wout_bf = nc.dram_tensor("wout_bf", [IC, 128, D], BF16, kind="Internal").ap()

    a = nc.alloc_sbuf_tensor
    x_t = a("x_t", [128, DC, TT], F32)
    h1 = a("h1", [128, DC, TT], F32)
    h1b = a("h1b", [128, DC, TT], BF16)
    hid = a("hid", [128, DC, TT], BF16)
    x_raw = a("x_raw", [128, IC, TT + 3], BF16)
    xs_bf = a("xs_bf", [128, IC, TT], BF16)
    z_bf = a("z_bf", [128, IC, TT], BF16)
    dt_bf = a("dt_bf", [128, IC, TT], BF16)
    u_bf = a("u_bf", [128, IC, TT], BF16)
    xdbl = a("xdbl", [96, TT], BF16)
    bcr = a("bcr", [32, TT], BF16)
    dtA = a("dtA", [128, IC, TT], BF16)
    a_all = a("a_all", [128, IC, TT + 1], F32)
    b_all = a("b_all", [128, IC, TT + 1], BF16)
    h_all = a("h_all", [128, IC, TT + 1], BF16)
    y_acc = a("y_acc", [128, IC, TT], BF16)
    bc_sb = a("bc_sb", [128, 2, TT], BF16)
    state = a("state", [128, IC, NS], F32)
    winb = a("winb", [128, DC, 2 * DI], BF16)
    woutb = a("woutb", [128, 4, D], BF16)
    wxb = a("wxb", [128, IC, 96], BF16)
    wdtb = a("wdtb", [64, DI], BF16)
    acol = a("acol", [128, IC * NS], F32)
    mcols = a("mcols", [128, 48], F32)
    gacl = a("gacl", [128, DC], F32)
    gbcl = a("gbcl", [128, DC], F32)
    ca1 = a("ca1", [128, DC], F32)
    cb1 = a("cb1", [128, DC], F32)
    nwc = a("nwc", [128, DC], F32)
    cwc = a("cwc", [128, IC * 4], F32)
    cbc = a("cbc", [128, IC], F32)
    bdc = a("bdc", [128, IC], F32)
    dvc = a("dvc", [128, IC], F32)
    ccol = a("ccol", [128, DC], F32)
    idf = a("idf", [128, 128], F32)
    ones_c = a("ones_c", [128, 1], F32)
    ones_r = a("ones_r", [1, 128], F32)
    rows = a("rows", [1, 2 * TT + 16], F32)
    mxc = a("mxc", [128, 1], F32)
    tmpc = a("tmpc", [128, TT], F32)
    cvac = a("cvac", [128, TT], F32)
    csts = a("csts", [128, 2], F32)
    cdbg = a("cdbg", [128, 6 * DC], F32) if DBG else None
    nc.const_aps.register(F32, -1.0, csts[:, 0:1])
    nc.const_aps.register(F32, -0.5, csts[:, 1:2])

    ohot = woutb[:].rearrange("p a b -> p (a b)")[0:32, 0:32 * 128]

    ps_w0 = nc.alloc_psum_tensor("ps_w0", [128, 512], F32)
    ps_w1 = nc.alloc_psum_tensor("ps_w1", [128, 512], F32)
    ps_w2 = nc.alloc_psum_tensor("ps_w2", [128, 512], F32)
    ps_bc = nc.alloc_psum_tensor("ps_bc", [128, 512], F32)
    ps_row = nc.alloc_psum_tensor("ps_row", [1, 512], F32)
    ps_b2 = nc.alloc_psum_tensor("ps_b2", [128, 512], F32)
    ps_m = nc.alloc_psum_tensor("ps_m", [128, 128], F32)
    psel = [ps_w0, ps_w1, ps_w2]

    p = Prog()

    for tile, src in [(ccol, c_col), (acol, alog_col), (nwc, normw_col),
                      (cwc, convw_col), (cbc, convb_col), (bdc, bdt_col),
                      (dvc, dvec_col), (idf, identf)]:
        p.do("sync", DMA(tile[:], src), inc=16)
    ld_consts = p.ev("sync")

    p.do("dve", lambda e: e.memset(ones_c[:], 1.0))
    p.do("dve", lambda e: e.memset(ones_r[:], 1.0))
    p.do("dve", lambda e: e.memset(state[:].rearrange("p i n -> p (i n)"), 0.0))
    p.do("dve", lambda e: e.memset(mxc[:], 0.0))
    p.do("dve", lambda e: e.memset(x_raw[:, :, 0:3], 0.0))
    p.do("dve", lambda e: e.memset(csts[:, 0:1], -1.0))
    p.do("dve", lambda e: e.memset(csts[:, 1:2], -0.5))
    init_dve = p.ev("dve")
    p._pending["act"].append(init_dve)

    silu_c = p.do("act", lambda e: e.activation(out=ccol[:], in_=ccol[:],
                  func=AF.Silu), waits=[ld_consts])
    a_exp = p.do("act", lambda e: e.activation(out=acol[:], in_=acol[:],
                 func=AF.Exp), waits=[ld_consts])
    a_neg = p.do("dve", lambda e: e.tensor_scalar(out=acol[:], in0=acol[:],
                 scalar1=-1.0, scalar2=None, op0=ALU.mult), waits=[a_exp])

    af32 = a_all[:].rearrange("p i l -> p (i l)")
    last_mm = None
    mm_hist = [None, None]
    first = True
    ci = 0
    for k in range(DC):
        for qq in range(4):
            half = ci % 2
            off = 1536 * half
            ld = p.do("sync", (lambda k=k, qq=qq, off=off: DMA(
                af32[:, off:off + 1536],
                wmod[128 * k:128 * (k + 1), 1536 * qq:1536 * (qq + 1)]))(),
                waits=[silu_c, mm_hist[half]], inc=16)
            for m in range(12):
                mcol = 12 * qq + m
                last_mm = p.do("pe", (lambda m=m, mcol=mcol, k=k, off=off,
                               first=first: (lambda e: e.matmul(
                    out=ps_m[:, mcol:mcol + 1],
                    lhsT=af32[:, off + 128 * m:off + 128 * (m + 1)],
                    rhs=ccol[:, k:k + 1],
                    start=first, stop=(k == DC - 1))))(),
                    waits=[ld] if m == 0 else [])
                first = False
            mm_hist[half] = last_mm
            ci += 1
    mod_cols = p.do("dve", lambda e: e.tensor_copy(out=mcols[:], in_=ps_m[:, 0:48]),
                    waits=[last_mm])
    p.do("dve", lambda e: e.tensor_scalar(out=ca1[:], in0=mcols[:, 8:16],
         scalar1=1.0, scalar2=None, op0=ALU.add), waits=[mod_cols])
    p.do("dve", lambda e: e.tensor_scalar(out=cb1[:], in0=mcols[:, 32:40],
         scalar1=1.0, scalar2=None, op0=ALU.add), waits=[mod_cols])

    def clamp_g(dst, lo, prev=None):
        src = mcols[:, lo:lo + DC]
        sg0 = p.do("act", lambda e: e.activation(out=tmpc[:, 0:DC], in_=src,
                   func=AF.Sign), waits=[mod_cols, prev])
        if DBG and lo == 16:
            p.do("dve", lambda e: e.tensor_copy(out=cdbg[:, 0:DC],
                 in_=tmpc[:, 0:DC]), waits=[sg0])
        e1 = p.do("dve", lambda e: e.tensor_tensor(out=tmpc[:, DC:2 * DC],
             in0=tmpc[:, 0:DC], in1=tmpc[:, 0:DC], op=ALU.mult), waits=[sg0])
        e2 = p.do("dve", lambda e: e.tensor_scalar(out=tmpc[:, DC:2 * DC],
             in0=tmpc[:, DC:2 * DC], scalar1=-1.0, scalar2=1.0, op0=ALU.mult,
             op1=ALU.add), waits=[e1])
        e3 = p.do("dve", lambda e: e.tensor_tensor(out=tmpc[:, 0:DC],
             in0=tmpc[:, 0:DC], in1=tmpc[:, DC:2 * DC], op=ALU.add), waits=[e2])
        if DBG and lo == 16:
            p.do("dve", lambda e: e.tensor_copy(out=cdbg[:, DC:2 * DC],
                 in_=tmpc[:, 0:DC]))
        ab = p.do("act", lambda e: e.activation(out=dst, in_=src, func=AF.Abs),
                  waits=[mod_cols, sg0])
        if DBG and lo == 16:
            p.do("dve", lambda e: e.tensor_copy(out=cdbg[:, 2 * DC:3 * DC],
                 in_=dst), waits=[ab])
        e4 = p.do("dve", lambda e: e.tensor_scalar(out=dst, in0=dst, scalar1=0.25,
             scalar2=None, op0=ALU.max), waits=[ab])
        if DBG and lo == 16:
            p.do("dve", lambda e: e.tensor_copy(out=cdbg[:, 3 * DC:4 * DC],
                 in_=dst), waits=[e4])
        ret = p.do("dve", lambda e: e.tensor_tensor(out=dst, in0=dst,
                   in1=tmpc[:, 0:DC], op=ALU.mult), waits=[e3, e4])
        if DBG and lo == 16:
            p.do("dve", lambda e: e.tensor_copy(out=cdbg[:, 4 * DC:5 * DC],
                 in_=dst))
            p.do("dve", lambda e: e.tensor_copy(out=cdbg[:, 5 * DC:6 * DC],
                 in_=mcols[:, 16:24]))
        return ret

    ga_done = clamp_g(gacl[:], 16)
    gb_done = clamp_g(gbcl[:], 40, prev=ga_done)
    if DBG:
        p.do("sync", DMA(o_ga_pre, gacl[:]), waits=[gb_done], inc=16)
    p.do("sync", DMA(o_gb, gbcl[:]), waits=[gb_done], inc=16)
    p.do("sync", DMA(o_mc, mcols[:]), waits=[gb_done], inc=16)

    dtAf = dtA[:].rearrange("p i l -> p (i l)")
    cst_hist = [last_mm, last_mm]
    ci = 0
    for k in range(DC):
        for hh in range(2):
            half = ci % 2
            off = 2048 * half
            ld = p.do("sync", (lambda k=k, hh=hh, off=off: DMA(
                af32[:, off:off + 2048], w_in[128 * k:128 * (k + 1),
                                              2048 * hh:2048 * (hh + 1)]))(),
                waits=[cst_hist[half]], inc=16)
            cst_hist[half] = p.do("dve", (lambda k=k, hh=hh, off=off: (lambda e: e.tensor_copy(
                out=winb[:, k, 2048 * hh:2048 * (hh + 1)],
                in_=af32[:, off:off + 2048])))(), waits=[ld])
            ci += 1
    cst_prev = cst_hist[(ci + 1) % 2]
    cst_prev = cst_hist[ci % 2] if cst_hist[ci % 2] else cst_prev
    st_prev = None
    for k in range(IC):
        ld = p.do("sync", (lambda k=k: DMA(
            af32[:, 0:D], w_out[128 * k:128 * (k + 1), :]))(),
            waits=[cst_prev, st_prev], inc=16)
        cst_prev = p.do("dve", (lambda k=k: (lambda e: e.tensor_copy(
            out=dtAf[:, 0:D], in_=af32[:, 0:D])))(), waits=[ld, st_prev])
        st_prev = p.do("sync", (lambda k=k: DMA(
            wout_bf[k], dtAf[:, 0:D]))(), waits=[cst_prev], inc=16)
    for icx in range(IC):
        ld = p.do("sync", (lambda icx=icx: DMA(
            af32[:, 0:96], w_x[128 * icx:128 * (icx + 1), :]))(),
            waits=[cst_prev, st_prev], inc=16)
        st_prev = None
        cst_prev = p.do("dve", (lambda icx=icx: (lambda e: e.tensor_copy(
            out=wxb[:, icx, :], in_=af32[:, 0:96])))(), waits=[ld])
    ld = p.do("sync", DMA(af32[0:64, 0:DI], w_dt), waits=[cst_prev], inc=16)
    p.do("dve", lambda e: e.tensor_copy(out=wdtb[:], in_=af32[0:64, 0:DI]),
         waits=[ld])
    p.do("dve", lambda e: e.memset(af32, 0.0))

    p.barrier()

    for it in range(NT):
        t0 = it * TT
        p.barrier()
        ld_x = p.do("sync", (lambda t0=t0: DMA(
            x_t[:], xT.rearrange("(c p) t -> p c t", p=128)[:, :, t0:t0 + TT]))(),
            inc=16)
        oh_ld = p.do("sync", DMA(ohot, onehot), inc=16)
        sq = p.do("act", lambda e: e.activation(
            out=a_all[:, 0:DC, 1:1 + TT],
            in_=x_t[:].rearrange("p c l -> p (c l)"), func=AF.Square), waits=[ld_x])
        for dc in range(DC):
            m1 = p.do("pe", (lambda dc=dc: (lambda e: e.matmul(
                out=ps_row[0:1, 0:TT], lhsT=ones_c[:], rhs=x_t[:, dc, :],
                start=(dc == 0), stop=(dc == DC - 1))))(), waits=[ld_x])
        for dc in range(DC):
            m2 = p.do("pe", (lambda dc=dc: (lambda e: e.matmul(
                out=ps_row[0:1, TT:2 * TT], lhsT=ones_c[:], rhs=a_all[:, dc, 1:1 + TT],
                start=(dc == 0), stop=(dc == DC - 1))))(), waits=[sq])
        mean_r = rows[0:1, 0:TT]
        var_r = rows[0:1, TT:2 * TT]
        msq_r = tmpc[0:1, 0:TT]
        p.do("dve", lambda e: e.tensor_scalar(out=mean_r, in0=ps_row[0:1, 0:TT],
             scalar1=1.0 / D, scalar2=None, op0=ALU.mult), waits=[m1, m2])
        p.do("dve", lambda e: e.tensor_scalar(out=var_r, in0=ps_row[0:1, TT:2 * TT],
             scalar1=1.0 / D, scalar2=EPS, op0=ALU.mult, op1=ALU.add))
        p.do("dve", lambda e: e.tensor_tensor(out=msq_r, in0=mean_r, in1=mean_r,
             op=ALU.mult))
        sub_ev = p.do("dve", lambda e: e.tensor_tensor(out=var_r, in0=var_r,
                      in1=msq_r, op=ALU.subtract))
        p.do("act", lambda e: e.activation(out=var_r, in_=var_r, func=AF.Ln),
             waits=[sub_ev])
        rstd_ev = p.do("act", lambda e: e.activation(out=var_r, in_=var_r,
                       func=AF.Exp, scale=-0.5))
        bc1 = p.do("pe", lambda e: e.matmul(out=ps_b2[:, 0:TT], lhsT=ones_r[:],
                   rhs=mean_r, start=True, stop=True), waits=[sub_ev])
        bc2 = p.do("pe", lambda e: e.matmul(out=ps_b2[:, TT:2 * TT], lhsT=ones_r[:],
                   rhs=var_r, start=True, stop=True), waits=[rstd_ev])
        for dc in range(DC):
            p.do("dve", (lambda dc=dc: (lambda e: e.tensor_tensor(
                out=h1[:, dc, :], in0=x_t[:, dc, :], in1=ps_b2[:, 0:TT],
                op=ALU.subtract)))(), waits=[bc1, bc2] if dc == 0 else [])
            p.do("dve", (lambda dc=dc: (lambda e: e.tensor_tensor(
                out=h1[:, dc, :], in0=h1[:, dc, :], in1=ps_b2[:, TT:2 * TT],
                op=ALU.mult)))())
            h1_done = p.do("dve", (lambda dc=dc: (lambda e: e.tensor_scalar(
                out=h1[:, dc, :], in0=h1[:, dc, :], scalar1=ca1[:, dc:dc + 1],
                scalar2=mcols[:, dc:dc + 1], op0=ALU.mult, op1=ALU.add)))())
        h1b_ev = p.do("dve", lambda e: e.tensor_copy(
            out=h1b[:].rearrange("p c l -> p (c l)"),
            in_=h1[:].rearrange("p c l -> p (c l)")))

        sq2 = p.do("act", lambda e: e.activation(
            out=a_all[:, DC:2 * DC, 1:1 + TT],
            in_=h1[:].rearrange("p c l -> p (c l)"), func=AF.Square),
            waits=[h1_done])
        for dc in range(DC):
            m3 = p.do("pe", (lambda dc=dc: (lambda e: e.matmul(
                out=ps_row[0:1, 0:TT], lhsT=ones_c[:], rhs=a_all[:, DC + dc, 1:1 + TT],
                start=(dc == 0), stop=(dc == DC - 1))))(), waits=[sq2, bc2])
        mr = p.do("dve", lambda e: e.tensor_scalar(out=mean_r, in0=ps_row[0:1, 0:TT],
                  scalar1=1.0 / D, scalar2=EPS, op0=ALU.mult, op1=ALU.add),
                  waits=[m3])
        p.do("act", lambda e: e.activation(out=mean_r, in_=mean_r, func=AF.Ln),
             waits=[mr])
        r2 = p.do("act", lambda e: e.activation(out=mean_r, in_=mean_r, func=AF.Exp,
                  scale=-0.5))
        bc3 = p.do("pe", lambda e: e.matmul(out=ps_b2[:, 0:TT], lhsT=ones_r[:],
                   rhs=mean_r, start=True, stop=True), waits=[r2])
        for dc in range(DC):
            p.do("dve", (lambda dc=dc: (lambda e: e.tensor_tensor(
                out=tmpc[:], in0=h1[:, dc, :], in1=ps_b2[:, 0:TT],
                op=ALU.mult)))(), waits=[bc3] if dc == 0 else [])
            hid_done = p.do("dve", (lambda dc=dc: (lambda e: e.tensor_scalar(
                out=hid[:, dc, :], in0=tmpc[:], scalar1=nwc[:, dc:dc + 1],
                scalar2=None, op0=ALU.mult)))())

        if it > 0:
            tail_ev = p.do("dve", lambda e: e.tensor_copy(
                out=x_raw[:, :, 0:3], in_=x_raw[:, :, TT:TT + 3]))
        else:
            tail_ev = p.ev("dve")

        evac_ev = {0: None, 1: None, 2: None}
        xz_evs = []
        for m in range(32):
            bank = m % 3
            for k in range(DC):
                wd = [hid_done, evac_ev[bank]] if k == 0 else []
                mm_l = p.do("pe", (lambda m=m, k=k, bank=bank: (lambda e: e.matmul(
                    out=psel[bank][:, 0:TT],
                    lhsT=winb[:, k, 128 * m:128 * (m + 1)],
                    rhs=hid[:, k, :], start=(k == 0), stop=(k == DC - 1))))(),
                    waits=wd)
            if m < IC:
                ev = p.do("act", (lambda m=m, bank=bank: (lambda e: e.activation(
                    out=x_raw[:, m, 3:3 + TT], in_=psel[bank][:, 0:TT],
                    func=AF.Copy)))(), waits=[mm_l, tail_ev])
            else:
                ev = p.do("act", (lambda m=m, bank=bank: (lambda e: e.activation(
                    out=z_bf[:, m - IC, :], in_=psel[bank][:, 0:TT],
                    func=AF.Silu)))(), waits=[mm_l])
            evac_ev[bank] = ev
            xz_evs.append(ev)

        conv_evs = []
        for icx in range(IC):
            wd = [xz_evs[icx], tail_ev]
            if conv_evs:
                wd.append(conv_evs[-1])
            p.do("dve", (lambda icx=icx: (lambda e: e.tensor_scalar(
                out=cvac[:], in0=x_raw[:, icx, 0:TT],
                scalar1=cwc[:, 4 * icx:4 * icx + 1], scalar2=None,
                op0=ALU.mult)))(), waits=wd)
            for kk in range(1, 4):
                p.do("dve", (lambda icx=icx, kk=kk: (lambda e: e.scalar_tensor_tensor(
                    out=cvac[:], in0=x_raw[:, icx, kk:kk + TT],
                    scalar=cwc[:, 4 * icx + kk:4 * icx + kk + 1], in1=cvac[:],
                    op0=ALU.mult, op1=ALU.add)))())
            cv = p.ev("dve")
            conv_evs.append(p.do("act", (lambda icx=icx: (lambda e: e.activation(
                out=xs_bf[:, icx, :], in_=cvac[:], func=AF.Silu,
                bias=cbc[:, icx:icx + 1])))(), waits=[cv]))
        conv_done = conv_evs[-1]

        for icx in range(IC):
            mmx = p.do("pe", (lambda icx=icx: (lambda e: e.matmul(
                out=ps_w0[0:96, 0:TT], lhsT=wxb[:, icx, :], rhs=xs_bf[:, icx, :],
                start=(icx == 0), stop=(icx == IC - 1))))(),
                waits=[conv_evs[icx]] + ([evac_ev[0]] if icx == 0 else []))
        xdbl_ev = p.do("act", lambda e: e.activation(out=xdbl[:],
                       in_=ps_w0[0:96, 0:TT], func=AF.Copy), waits=[mmx])
        evac_ev[0] = xdbl_ev
        bcr_ev = p.do("act", lambda e: e.activation(out=bcr[:], in_=xdbl[64:96, :],
                      func=AF.Copy))

        dtp_last = None
        for m in range(IC):
            bank = 1 + (m % 2)
            mmd = p.do("pe", (lambda m=m, bank=bank: (lambda e: e.matmul(
                out=psel[bank][:, 0:TT], lhsT=wdtb[:, 128 * m:128 * (m + 1)],
                rhs=xdbl[0:64, :], start=True, stop=True)))(),
                waits=[xdbl_ev, evac_ev[bank]])
            dtp_last = p.do("dve", (lambda m=m, bank=bank: (lambda e: e.tensor_scalar(
                out=a_all[:, m, 1:1 + TT], in0=psel[bank][:, 0:TT],
                scalar1=bdc[:, m:m + 1], scalar2=None, op0=ALU.add)))(),
                waits=[mmd])
            evac_ev[bank] = dtp_last
        vv = a_all[:, :, 1:1 + TT]
        c1 = p.do("act", lambda e: e.activation(out=dtAf, in_=vv, func=AF.Abs),
                  waits=[dtp_last])
        p.do("act", lambda e: e.activation(out=dtAf, in_=dtAf, func=AF.Exp,
             scale=-1.0))
        c3 = p.do("act", lambda e: e.activation(out=dtAf, in_=dtAf, func=AF.Ln,
                  bias=1.0))
        p.do("dve", lambda e: e.tensor_scalar(
            out=dt_bf[:].rearrange("p i l -> p (i l)"), in0=vv, scalar1=0.0,
            scalar2=None, op0=ALU.max), waits=[c1])
        dt_done = p.do("dve", lambda e: e.tensor_tensor(
            out=dt_bf[:].rearrange("p i l -> p (i l)"),
            in0=dt_bf[:].rearrange("p i l -> p (i l)"), in1=dtAf, op=ALU.add),
            waits=[c3])
        u_done = p.do("dve", lambda e: e.tensor_tensor(
            out=u_bf[:].rearrange("p i l -> p (i l)"),
            in0=dt_bf[:].rearrange("p i l -> p (i l)"),
            in1=xs_bf[:].rearrange("p i l -> p (i l)"), op=ALU.mult))

        g_ev = None
        aH_prev = [None, None]
        stH_prev = [None, None]
        bce_prev = None
        sel2 = None
        for n in range(NS):
            seld = [bcr_ev, bce_prev, g_ev, oh_ld]
            p.do("pe", (lambda n=n: (lambda e: e.matmul(
                out=ps_bc[:, 0:TT], lhsT=ohot[:, 128 * n:128 * (n + 1)],
                rhs=bcr[:], start=True, stop=True)))(), waits=seld)
            sel2 = p.do("pe", (lambda n=n: (lambda e: e.matmul(
                out=ps_bc[:, TT:2 * TT], lhsT=ohot[:, 128 * (16 + n):128 * (17 + n)],
                rhs=bcr[:], start=True, stop=True)))())
            bce_prev = p.do("act", lambda e: e.activation(
                out=bc_sb[:].rearrange("p c l -> p (c l)"), in_=ps_bc[:, 0:2 * TT],
                func=AF.Copy), waits=[sel2])
            dtaH = [None, None]
            for H in (0, 1):
                for j in range(8):
                    icx = 8 * H + j
                    wd = []
                    if j == 0:
                        wd = ([dt_done, a_neg] if n == 0
                              else [aH_prev[H], g_ev])
                    dtaH[H] = p.do("pool", (lambda icx=icx, n=n: (lambda e: e.tensor_scalar(
                        out=dtA[:, icx, :], in0=dt_bf[:, icx, :],
                        scalar1=acol[:, NS * icx + n:NS * icx + n + 1],
                        scalar2=None, op0=ALU.mult)))(), waits=wd)
            for H in (0, 1):
                exp_w = [dtaH[H]]
                exp_w.append(stH_prev[H] if stH_prev[H] is not None else dt_done)
                aH_prev[H] = p.do("act", (lambda H=H: (lambda e: e.activation(
                    out=a_all[:, 8 * H:8 * H + 8, 1:1 + TT],
                    in_=dtA[:, 8 * H:8 * H + 8, :].rearrange("p i l -> p (i l)"),
                    func=AF.Exp)))(), waits=exp_w)
            p.do("pe", lambda e: e.matmul(out=ps_m[:, 0:64], lhsT=ohot[:, 0:128],
                 rhs=bcr[:, 0:64], start=True, stop=True), waits=[dtaH[0]])
            p.do("pe", lambda e: e.matmul(out=ps_m[:, 64:128], lhsT=ohot[:, 0:128],
                 rhs=bcr[:, 0:64], start=True, stop=True), waits=[dtaH[1]])
            yacc_ev = None
            for H in (0, 1):
                lo = 8 * H
                b_w = [bce_prev] + ([u_done] if n == 0 else [])
                p.do("dve", (lambda lo=lo: (lambda e: e.tensor_tensor(
                    out=b_all[:, lo:lo + 8, 1:1 + TT], in0=u_bf[:, lo:lo + 8, :],
                    in1=bc_sb[:, 0:1, :].broadcast_to([128, 8, TT]),
                    op=ALU.mult)))(), waits=b_w)
                rst_ev = p.do("dve", (lambda n=n, lo=lo: (lambda e: e.tensor_copy(
                    out=b_all[:, lo:lo + 8, 0], in_=state[:, lo:lo + 8, n])))())
                scn = p.do("dve", (lambda lo=lo: (lambda e: e.tensor_tensor_scan(
                    out=h_all[:, lo:lo + 8, :].rearrange("p i l -> p (i l)"),
                    data0=a_all[:, lo:lo + 8, :].rearrange("p i l -> p (i l)"),
                    data1=b_all[:, lo:lo + 8, :].rearrange("p i l -> p (i l)"),
                    initial=0.0, op0=ALU.mult, op1=ALU.add)))(),
                    waits=[aH_prev[H], rst_ev])
                stH_prev[H] = p.do("dve", (lambda n=n, lo=lo: (lambda e: e.tensor_copy(
                    out=state[:, lo:lo + 8, n], in_=h_all[:, lo:lo + 8, TT])))(),
                    waits=[scn])
                g_ev = p.do("dve", (lambda lo=lo: (lambda e: e.tensor_tensor(
                    out=dtA[:, lo:lo + 8, :].rearrange("p i l -> p (i l)"),
                    in0=h_all[:, lo:lo + 8, 1:1 + TT],
                    in1=bc_sb[:, 1:2, :].broadcast_to([128, 8, TT]),
                    op=ALU.mult)))(), waits=[aH_prev[H], dtaH[H]])
                if n == 0:
                    yacc_ev = p.do("dve", (lambda lo=lo: (lambda e: e.tensor_copy(
                        out=y_acc[:, lo:lo + 8, :].rearrange("p i l -> p (i l)"),
                        in_=dtA[:, lo:lo + 8, :].rearrange("p i l -> p (i l)"))))())
                else:
                    yacc_ev = p.do("dve", (lambda lo=lo: (lambda e: e.tensor_tensor(
                        out=y_acc[:, lo:lo + 8, :].rearrange("p i l -> p (i l)"),
                        in0=y_acc[:, lo:lo + 8, :].rearrange("p i l -> p (i l)"),
                        in1=dtA[:, lo:lo + 8, :].rearrange("p i l -> p (i l)"),
                        op=ALU.add)))())
                p.do("pe", (lambda H=H: (lambda e: e.matmul(
                    out=ps_m[:, 64 * H:64 * H + 64], lhsT=ohot[:, 0:128],
                    rhs=bcr[:, 0:64], start=True, stop=True)))(),
                    waits=[stH_prev[H]])

        for icx in range(IC):
            p.do("dve", (lambda icx=icx: (lambda e: e.scalar_tensor_tensor(
                out=dtA[:, 0, :], in0=xs_bf[:, icx, :], scalar=dvc[:, icx:icx + 1],
                in1=y_acc[:, icx, :], op0=ALU.mult, op1=ALU.add)))(),
                waits=[yacc_ev] if icx == 0 else [])
            yz_done = p.do("dve", (lambda icx=icx: (lambda e: e.tensor_tensor(
                out=x_raw[:, icx, 0:TT], in0=dtA[:, 0, :], in1=z_bf[:, icx, :],
                op=ALU.mult)))())

        accs = [(psel + [ps_bc])[m // 2][:, (m % 2) * TT:(m % 2 + 1) * TT]
                for m in range(DC)]
        mo = None
        for q in range(4):
            wo_ld = p.do("sync", (lambda q=q: DMA(
                woutb[:], wout_bf[4 * q:4 * q + 4]))(),
                waits=([yz_done, evac_ev[0], evac_ev[1], evac_ev[2], sel2,
                        bce_prev] if q == 0 else [mo]), inc=16)
            for k in range(4 * q, 4 * q + 4):
                for m in range(DC):
                    wd = [wo_ld, yz_done] if m == 0 else []
                    mo = p.do("pe", (lambda m=m, k=k: (lambda e: e.matmul(
                        out=accs[m],
                        lhsT=woutb[:, k % 4, 128 * m:128 * (m + 1)],
                        rhs=x_raw[:, k, 0:TT], start=(k == 0),
                        stop=(k == IC - 1))))(), waits=wd)
        xmid_done = None
        for m in range(DC):
            p.do("dve", (lambda m=m: (lambda e: e.tensor_tensor(
                out=tmpc[:], in0=accs[m], in1=h1b[:, m, :],
                op=ALU.add)))(), waits=[mo, h1b_ev] if m == 0 else [])
            xmid_done = p.do("dve", (lambda m=m: (lambda e: e.scalar_tensor_tensor(
                out=h1[:, m, :], in0=tmpc[:], scalar=gacl[:, m:m + 1],
                in1=x_t[:, m, :], op0=ALU.mult, op1=ALU.add)))())
        evac_ev[0] = evac_ev[1] = evac_ev[2] = xmid_done
        p.do("sync", (lambda t0=t0: DMA(
            o_xmid.rearrange("(c p) t -> p c t", p=128)[:, :, t0:t0 + TT], h1[:]))(),
            waits=[xmid_done], inc=16)

        sq3 = p.do("act", lambda e: e.activation(
            out=a_all[:, DC:2 * DC, 1:1 + TT],
            in_=h1[:].rearrange("p c l -> p (c l)"), func=AF.Square),
            waits=[xmid_done])
        for dc in range(DC):
            s1 = p.do("pe", (lambda dc=dc: (lambda e: e.matmul(
                out=ps_row[0:1, 0:TT], lhsT=ones_c[:], rhs=h1[:, dc, :],
                start=(dc == 0), stop=(dc == DC - 1))))(),
                waits=[xmid_done, bc3] if dc == 0 else [])
        for dc in range(DC):
            s2 = p.do("pe", (lambda dc=dc: (lambda e: e.matmul(
                out=ps_row[0:1, TT:2 * TT], lhsT=ones_c[:], rhs=a_all[:, DC + dc, 1:1 + TT],
                start=(dc == 0), stop=(dc == DC - 1))))(), waits=[sq3])
        p.do("dve", lambda e: e.tensor_scalar(out=mean_r, in0=ps_row[0:1, 0:TT],
             scalar1=1.0 / D, scalar2=None, op0=ALU.mult), waits=[s1, s2])
        p.do("dve", lambda e: e.tensor_scalar(out=var_r, in0=ps_row[0:1, TT:2 * TT],
             scalar1=1.0 / D, scalar2=EPS, op0=ALU.mult, op1=ALU.add))
        p.do("dve", lambda e: e.tensor_tensor(out=msq_r, in0=mean_r, in1=mean_r,
             op=ALU.mult))
        sb2 = p.do("dve", lambda e: e.tensor_tensor(out=var_r, in0=var_r,
                   in1=msq_r, op=ALU.subtract))
        p.do("act", lambda e: e.activation(out=var_r, in_=var_r, func=AF.Ln),
             waits=[sb2])
        rs2 = p.do("act", lambda e: e.activation(out=var_r, in_=var_r, func=AF.Exp,
                   scale=-0.5))
        bc4 = p.do("pe", lambda e: e.matmul(out=ps_b2[:, 0:TT], lhsT=ones_r[:],
                   rhs=mean_r, start=True, stop=True), waits=[sb2])
        bc5 = p.do("pe", lambda e: e.matmul(out=ps_b2[:, TT:2 * TT], lhsT=ones_r[:],
                   rhs=var_r, start=True, stop=True), waits=[rs2])
        mx_prev = None
        for dc in range(DC):
            p.do("dve", (lambda dc=dc: (lambda e: e.tensor_tensor(
                out=a_all[:, dc, 1:1 + TT], in0=h1[:, dc, :], in1=ps_b2[:, 0:TT],
                op=ALU.subtract)))(), waits=[bc4, bc5, sq3] if dc == 0 else [])
            p.do("dve", (lambda dc=dc: (lambda e: e.tensor_tensor(
                out=a_all[:, dc, 1:1 + TT], in0=a_all[:, dc, 1:1 + TT],
                in1=ps_b2[:, TT:2 * TT], op=ALU.mult)))())
            p.do("dve", (lambda dc=dc: (lambda e: e.tensor_scalar(
                out=a_all[:, dc, 1:1 + TT], in0=a_all[:, dc, 1:1 + TT],
                scalar1=cb1[:, dc:dc + 1],
                scalar2=mcols[:, 24 + dc:25 + dc], op0=ALU.mult, op1=ALU.add)))())
            rdev = p.do("dve", (lambda dc=dc: (lambda e: e.reduce_max(
                out=tmpc[:, dc:dc + 1], in_=a_all[:, dc, 1:1 + TT],
                axis=mybir.AxisListType.X,
                apply_absolute_value=True)))())
            mx_ev = p.do("dve", (lambda dc=dc: (lambda e: e.tensor_tensor(
                out=mxc[:], in0=mxc[:], in1=tmpc[:, dc:dc + 1], op=ALU.max)))(),
                waits=[rdev, mx_prev])
            mx_prev = mx_ev
        p.do("sync", (lambda t0=t0: DMA(
            o_h2.rearrange("(c p) t -> p c t", p=128)[:, :, t0:t0 + TT],
            a_all[:, 0:DC, 1:1 + TT]))(), waits=[mx_ev], inc=16)

    if DBG:
        allev = [p.ev("dve"), p.ev("act"), p.ev("pe")]
        p.do("sync", DMA(o_hid, hid[:].rearrange("p c l -> p (c l)")), waits=allev, inc=16)
        p.do("sync", DMA(o_xs, xs_bf[:].rearrange("p i l -> p (i l)")), waits=allev, inc=16)
        p.do("sync", DMA(o_zb, z_bf[:].rearrange("p i l -> p (i l)")), waits=allev, inc=16)
        p.do("sync", DMA(o_dtb, dt_bf[:].rearrange("p i l -> p (i l)")), waits=allev, inc=16)
        p.do("sync", DMA(o_ya, y_acc[:].rearrange("p i l -> p (i l)")), waits=allev, inc=16)
        p.do("sync", DMA(o_ga2, gacl[:]), waits=allev, inc=16)
        p.do("sync", DMA(o_ball, b_all[:].rearrange("p i l -> p (i l)")), waits=allev, inc=16)
        p.do("sync", DMA(o_hall, h_all[:].rearrange("p i l -> p (i l)")), waits=allev, inc=16)
        p.do("sync", DMA(o_bcsb, bc_sb[:].rearrange("p c l -> p (c l)")), waits=allev, inc=16)
        p.do("sync", DMA(o_cdbg, cdbg[:]), waits=allev, inc=16)
        p.do("sync", DMA(o_acl, acol[:]), waits=allev, inc=16)

    trm = p.do("pe", lambda e: e.transpose(out=ps_b2[0:1, 0:128], in_=mxc[:],
               identity=idf[:]), waits=[mx_ev, p.ev("dve")])
    mxr = p.do("dve", lambda e: e.reduce_max(out=rows[0:1, 0:1],
               in_=ps_b2[0:1, 0:128], axis=mybir.AxisListType.X), waits=[trm])
    p.do("sync", DMA(o_max, rows[0:1, 0:1]), waits=[mxr], inc=16)
    p.do("dve", lambda e: e.memset(tmpc[:, 0:1], 0.0), waits=[p.ev("sync")])

    _emit_all(nc, p)
    return nc



def build_l2(TT, NT):
    nc = bass.Bass("TRN2", target_bir_lowering=False)
    h2T = nc.dram_tensor("h2T", [D, T], F32, kind="ExternalInput").ap()
    mx8 = nc.dram_tensor("mx8", [1, 8], F32, kind="ExternalInput").ap()
    clipx = nc.dram_tensor("clipx", [1, 1], F32, kind="ExternalInput").ap()
    w_gate = nc.dram_tensor("w_gate", [I_FFN, D], F32, kind="ExternalInput").ap()
    w_up = nc.dram_tensor("w_up", [I_FFN, D], F32, kind="ExternalInput").ap()
    identb = nc.dram_tensor("identb", [128, 128], BF16, kind="ExternalInput").ap()
    o_gp = nc.dram_tensor("o_gp", [I_FFN, T], F32, kind="ExternalOutput").ap()
    o_maxg = nc.dram_tensor("o_maxg", [1, 1], F32, kind="ExternalOutput").ap()

    a = nc.alloc_sbuf_tensor
    wgT = a("wgT", [128, DC, I_FFN], BF16)
    wuT = a("wuT", [128, DC, I_FFN], BF16)
    swg = a("swg", [128, ICF], F32)
    swu = a("swu", [128, ICF], F32)
    wld = a("wld", [128, 2 * D], F32)
    wq = a("wq", [128, 2 * D], BF16)
    h2t = a("h2t", [128, DC, TT], F32)
    xq = a("xq", [128, DC, TT], BF16)
    gsil = a("gsil", [128, TT], F32)
    upsc = a("upsc", [128, TT], F32)
    gp = a("gp", [128, ICF, TT], F32)
    rowb = a("rowb", [1, 128], F32)
    colb = a("colb", [128, 4], F32)
    mxcl = a("mxcl", [128, 1], F32)
    tmp1 = a("tmp1", [128, TT], F32)
    idb = a("idb", [128, 128], BF16)
    idf2 = a("idf2", [128, 128], F32)
    onesr = a("onesr", [1, 128], F32)
    csts = a("csts", [128, 1], F32)
    nc.const_aps.register(F32, 0.5, csts[:, 0:1])

    ps0 = nc.alloc_psum_tensor("ps0", [128, 512], F32)
    ps1 = nc.alloc_psum_tensor("ps1", [128, 512], F32)
    psb = nc.alloc_psum_tensor("psb", [128, 512], F32)
    psr = nc.alloc_psum_tensor("psr", [128, 512], F32)

    p = Prog()

    ld0 = p.do("sync", DMA(rowb[0:1, 0:8], mx8), inc=16)
    ld1 = p.do("sync", DMA(rowb[0:1, 8:9], clipx), inc=16)
    ldi = p.do("sync", DMA(idb[:], identb), inc=16)
    p.do("dve", lambda e: e.memset(onesr[:], 1.0))
    p.do("dve", lambda e: e.memset(mxcl[:], 0.0))
    p.do("dve", lambda e: e.memset(csts[:, 0:1], 0.5))
    ones_ev = p.ev("dve")
    p._pending["act"].append(ones_ev)
    ic_ev = p.do("dve", lambda e: e.tensor_copy(out=idf2[:], in_=idb[:]),
                 waits=[ldi])

    gm = p.do("dve", lambda e: e.reduce_max(out=rowb[0:1, 9:10],
              in_=rowb[0:1, 0:8], axis=mybir.AxisListType.X), waits=[ld0])
    th = p.do("act", lambda e: e.activation(out=rowb[0:1, 10:11],
              in_=rowb[0:1, 8:9], func=AF.Tanh, scale=0.5), waits=[ld1])
    sg = p.do("dve", lambda e: e.tensor_scalar(out=rowb[0:1, 10:11],
              in0=rowb[0:1, 10:11], scalar1=0.5, scalar2=0.5, op0=ALU.mult,
              op1=ALU.add), waits=[th])
    pm = p.do("dve", lambda e: e.tensor_tensor(out=rowb[0:1, 11:12],
              in0=rowb[0:1, 9:10], in1=rowb[0:1, 10:11], op=ALU.mult),
              waits=[gm, sg])
    bcp = p.do("pe", lambda e: e.matmul(out=psr[:, 0:1], lhsT=onesr[:],
               rhs=rowb[0:1, 11:12], start=True, stop=True), waits=[pm, ones_ev])
    pc0 = p.do("dve", lambda e: e.tensor_copy(out=colb[:, 0:1], in_=psr[:, 0:1]),
               waits=[bcp])
    sxc = p.do("dve", lambda e: e.tensor_scalar(out=colb[:, 1:2], in0=colb[:, 0:1],
               scalar1=1.0 / 127.0, scalar2=None, op0=ALU.mult), waits=[pc0])
    rc0 = p.do("dve", lambda e: e.reciprocal(out=colb[:, 2:3], in_=colb[:, 0:1]),
               waits=[pc0])
    rsc = p.do("dve", lambda e: e.tensor_scalar(out=colb[:, 2:3], in0=colb[:, 2:3],
               scalar1=127.0, scalar2=None, op0=ALU.mult), waits=[rc0])

    def quant_w(src, dstT, swcol, first_ev):
        psbv0 = ps0[:].bitcast(BF16)
        psbv1 = ps1[:].bitcast(BF16)
        psv = [psbv0, psbv1]
        ld_h = [first_ev, None]
        q_h = [first_ev, first_ev]
        ev_h = [None, None]
        tprev = [None, None]
        ret = None
        for icx in range(ICF):
            sl = icx % 2
            ld = p.do("sync", (lambda icx=icx, sl=sl: DMA(
                wld[:, 1024 * sl:1024 * (sl + 1)],
                src[128 * icx:128 * (icx + 1), :]))(),
                waits=[q_h[sl]], inc=16)
            rm = p.do("dve", (lambda icx=icx, sl=sl: (lambda e: e.reduce_max(
                out=swcol[:, icx:icx + 1], in_=wld[:, 1024 * sl:1024 * (sl + 1)],
                axis=mybir.AxisListType.X,
                apply_absolute_value=True)))(), waits=[ld])
            rcp = p.do("dve", (lambda icx=icx, sl=sl: (lambda e: e.reciprocal(
                out=tmp1[:, sl:sl + 1], in_=swcol[:, icx:icx + 1])))(), waits=[rm])
            p.do("dve", (lambda sl=sl: (lambda e: e.tensor_scalar(
                out=wq[:, 1024 * sl:1024 * (sl + 1)],
                in0=wld[:, 1024 * sl:1024 * (sl + 1)],
                scalar1=tmp1[:, sl:sl + 1], scalar2=127.0, op0=ALU.mult,
                op1=ALU.mult)))(), waits=[rcp, ev_h[sl]])
            qv = p.do("dve", (lambda sl=sl: (lambda e: e.tensor_scalar(
                out=wq[:, 1024 * sl:1024 * (sl + 1)],
                in0=wq[:, 1024 * sl:1024 * (sl + 1)],
                scalar1=RND, scalar2=RND, op0=ALU.add, op1=ALU.subtract)))())
            q_h[sl] = qv
            for hc in range(DC):
                ps = hc % 2
                tr = p.do("pe", (lambda hc=hc, sl=sl, ps=ps: (lambda e: e.transpose(
                    out=psv[ps][:, 0:128],
                    in_=wq[:, 1024 * sl + 128 * hc:1024 * sl + 128 * (hc + 1)],
                    identity=idb[:])))(), waits=[qv, ldi, tprev[ps]])
                tprev[ps] = p.do("act", (lambda hc=hc, icx=icx, ps=ps: (lambda e: e.activation(
                    out=dstT[:, hc, 128 * icx:128 * (icx + 1)],
                    in_=psv[ps][:, 0:128],
                    func=AF.Copy)))(), waits=[tr])
            ev_h[sl] = tprev[(DC - 1) % 2]
            ret = p.ev("act")
        return ret

    qg = quant_w(w_gate, wgT, swg, None)
    qu = quant_w(w_up, wuT, swu, qg)
    cg = p.do("dve", lambda e: e.tensor_scalar(out=swg[:], in0=swg[:],
              scalar1=colb[:, 1:2], scalar2=1.0 / 127.0, op0=ALU.mult,
              op1=ALU.mult), waits=[qg, sxc])
    cu = p.do("dve", lambda e: e.tensor_scalar(out=swu[:], in0=swu[:],
              scalar1=colb[:, 1:2], scalar2=1.0 / 127.0, op0=ALU.mult,
              op1=ALU.mult), waits=[qu, sxc])

    p.barrier()

    mx_ev = None
    for it in range(NT):
        t0 = it * TT
        p.barrier()
        ldh = p.do("sync", (lambda t0=t0: DMA(
            h2t[:], h2T.rearrange("(c p) t -> p c t", p=128)[:, :, t0:t0 + TT]))(),
            inc=16)
        p.do("dve", lambda e: e.tensor_scalar(
            out=h2t[:].rearrange("p c l -> p (c l)"),
            in0=h2t[:].rearrange("p c l -> p (c l)"), scalar1=colb[:, 2:3],
            scalar2=None, op0=ALU.mult), waits=[ldh, rsc])
        p.do("dve", lambda e: e.tensor_scalar(
            out=h2t[:].rearrange("p c l -> p (c l)"),
            in0=h2t[:].rearrange("p c l -> p (c l)"), scalar1=RND, scalar2=RND,
            op0=ALU.add, op1=ALU.subtract))
        q3 = p.do("dve", lambda e: e.tensor_scalar(
            out=xq[:].rearrange("p c l -> p (c l)"),
            in0=h2t[:].rearrange("p c l -> p (c l)"), scalar1=127.0,
            scalar2=-128.0, op0=ALU.min, op1=ALU.max))

        evac0 = None
        evac1 = None
        for m in range(ICF):
            for k in range(DC):
                g1 = p.do("pe", (lambda m=m, k=k: (lambda e: e.matmul(
                    out=ps0[:, 0:TT], lhsT=wgT[:, k, 128 * m:128 * (m + 1)],
                    rhs=xq[:, k, :], start=(k == 0), stop=(k == DC - 1))))(),
                    waits=[q3, cg, evac0] if k == 0 else [])
            ge = p.do("act", (lambda m=m: (lambda e: e.activation(
                out=gsil[:], in_=ps0[:, 0:TT], func=AF.Silu,
                scale=swg[:, m:m + 1])))(), waits=[g1])
            evac0 = ge
            for k in range(DC):
                u1 = p.do("pe", (lambda m=m, k=k: (lambda e: e.matmul(
                    out=ps1[:, 0:TT], lhsT=wuT[:, k, 128 * m:128 * (m + 1)],
                    rhs=xq[:, k, :], start=(k == 0), stop=(k == DC - 1))))(),
                    waits=[cu, evac1] if k == 0 else [])
            ue = p.do("act", (lambda m=m: (lambda e: e.activation(
                out=upsc[:], in_=ps1[:, 0:TT], func=AF.Copy,
                scale=swu[:, m:m + 1])))(), waits=[u1])
            evac1 = ue
            gpv = p.do("dve", (lambda m=m: (lambda e: e.tensor_tensor(
                out=gp[:, m, :], in0=gsil[:], in1=upsc[:], op=ALU.mult)))(),
                waits=[ge, ue])
            p._pending["act"].append(gpv)
            rdev = p.do("dve", (lambda m=m: (lambda e: e.reduce_max(
                out=tmp1[:, 1:2], in_=gp[:, m, :], axis=mybir.AxisListType.X,
                apply_absolute_value=True)))(), waits=[mx_ev])
            mx_ev = p.do("dve", lambda e: e.tensor_tensor(
                out=mxcl[:], in0=mxcl[:], in1=tmp1[:, 1:2], op=ALU.max),
                waits=[rdev])
        p.do("sync", (lambda t0=t0: DMA(
            o_gp.rearrange("(c p) t -> p c t", p=128)[:, :, t0:t0 + TT], gp[:]))(),
            waits=[mx_ev], inc=16)

    trm = p.do("pe", lambda e: e.transpose(out=psr[0:1, 0:128], in_=mxcl[:],
               identity=idf2[:]), waits=[mx_ev, ic_ev])
    fm = p.do("dve", lambda e: e.reduce_max(out=rowb[0:1, 12:13],
              in_=psr[0:1, 0:128], axis=mybir.AxisListType.X), waits=[trm])
    p.do("sync", DMA(o_maxg, rowb[0:1, 12:13]), waits=[fm], inc=16)
    p.do("dve", lambda e: e.memset(tmp1[:, 2:3], 0.0), waits=[p.ev("sync")])

    _emit_all(nc, p)
    return nc



def build_l3(TT, NT):
    nc = bass.Bass("TRN2", target_bir_lowering=False)
    gpT = nc.dram_tensor("gpT", [I_FFN, T], F32, kind="ExternalInput").ap()
    xmidT = nc.dram_tensor("xmidT", [D, T], F32, kind="ExternalInput").ap()
    mx8 = nc.dram_tensor("mx8", [1, 8], F32, kind="ExternalInput").ap()
    clipd = nc.dram_tensor("clipd", [1, 1], F32, kind="ExternalInput").ap()
    w_down = nc.dram_tensor("w_down", [D, I_FFN], F32, kind="ExternalInput").ap()
    gb_col = nc.dram_tensor("gb_col", [128, DC], F32, kind="ExternalInput").ap()
    identb = nc.dram_tensor("identb", [128, 128], BF16, kind="ExternalInput").ap()
    o_out = nc.dram_tensor("o_out", [D, T], F32, kind="ExternalOutput").ap()

    a = nc.alloc_sbuf_tensor
    wdT = a("wdT", [128, ICF, D], BF16)
    swd = a("swd", [128, DC], F32)
    wld = a("wld", [128, I_FFN], F32)
    wq = a("wq", [128, I_FFN], BF16)
    gpt = a("gpt", [128, ICF, TT], F32)
    hq = a("hq", [128, ICF, TT], BF16)
    xmt = a("xmt", [128, DC, TT], F32)
    outt = a("outt", [128, DC, TT], F32)
    rowb = a("rowb", [1, 128], F32)
    colb = a("colb", [128, 4], F32)
    gbc = a("gbc", [128, DC], F32)
    comb = a("comb", [128, DC], F32)
    csts = a("csts", [128, 1], F32)
    nc.const_aps.register(F32, 0.5, csts[:, 0:1])
    tmp1 = a("tmp1", [128, TT], F32)
    idb = a("idb", [128, 128], BF16)
    onesr = a("onesr", [1, 128], F32)

    ps0 = nc.alloc_psum_tensor("ps0", [128, 512], F32)
    ps1 = nc.alloc_psum_tensor("ps1", [128, 512], F32)
    psb = nc.alloc_psum_tensor("psb", [128, 512], F32)
    psr = nc.alloc_psum_tensor("psr", [128, 512], F32)

    p = Prog()

    ld0 = p.do("sync", DMA(rowb[0:1, 0:8], mx8), inc=16)
    ld1 = p.do("sync", DMA(rowb[0:1, 8:9], clipd), inc=16)
    ldi = p.do("sync", DMA(idb[:], identb), inc=16)
    ldg = p.do("sync", DMA(gbc[:], gb_col), inc=16)
    p.do("dve", lambda e: e.memset(onesr[:], 1.0))
    p.do("dve", lambda e: e.memset(csts[:, 0:1], 0.5))
    ones_ev = p.ev("dve")
    p._pending["act"].append(ones_ev)

    gm = p.do("dve", lambda e: e.reduce_max(out=rowb[0:1, 9:10],
              in_=rowb[0:1, 0:8], axis=mybir.AxisListType.X), waits=[ld0])
    th = p.do("act", lambda e: e.activation(out=rowb[0:1, 10:11],
              in_=rowb[0:1, 8:9], func=AF.Tanh, scale=0.5), waits=[ld1])
    sg = p.do("dve", lambda e: e.tensor_scalar(out=rowb[0:1, 10:11],
              in0=rowb[0:1, 10:11], scalar1=0.5, scalar2=0.5, op0=ALU.mult,
              op1=ALU.add), waits=[th])
    pm = p.do("dve", lambda e: e.tensor_tensor(out=rowb[0:1, 11:12],
              in0=rowb[0:1, 9:10], in1=rowb[0:1, 10:11], op=ALU.mult),
              waits=[gm, sg])
    bcp = p.do("pe", lambda e: e.matmul(out=psr[:, 0:1], lhsT=onesr[:],
               rhs=rowb[0:1, 11:12], start=True, stop=True), waits=[pm, ones_ev])
    pc0 = p.do("dve", lambda e: e.tensor_copy(out=colb[:, 0:1], in_=psr[:, 0:1]),
               waits=[bcp])
    sxc = p.do("dve", lambda e: e.tensor_scalar(out=colb[:, 1:2], in0=colb[:, 0:1],
               scalar1=1.0 / 127.0, scalar2=None, op0=ALU.mult), waits=[pc0])
    rc0 = p.do("dve", lambda e: e.reciprocal(out=colb[:, 2:3], in_=colb[:, 0:1]),
               waits=[pc0])
    rsc = p.do("dve", lambda e: e.tensor_scalar(out=colb[:, 2:3], in0=colb[:, 2:3],
               scalar1=127.0, scalar2=None, op0=ALU.mult), waits=[rc0])

    prev = None
    for hc in range(DC):
        ld = p.do("sync", (lambda hc=hc: DMA(
            wld[:], w_down[128 * hc:128 * (hc + 1), :]))(), waits=[prev], inc=16)
        rm = p.do("dve", (lambda hc=hc: (lambda e: e.reduce_max(
            out=swd[:, hc:hc + 1], in_=wld[:], axis=mybir.AxisListType.X,
            apply_absolute_value=True)))(), waits=[ld])
        rcp = p.do("dve", (lambda hc=hc: (lambda e: e.reciprocal(
            out=tmp1[:, 0:1], in_=swd[:, hc:hc + 1])))(), waits=[rm])
        p.do("dve", lambda e: e.tensor_scalar(out=wq[:], in0=wld[:],
             scalar1=tmp1[:, 0:1], scalar2=127.0, op0=ALU.mult, op1=ALU.mult),
             waits=[rcp])
        qv = p.do("dve", lambda e: e.tensor_scalar(out=wq[:], in0=wq[:],
                  scalar1=RND, scalar2=RND, op0=ALU.add, op1=ALU.subtract))
        tprev = None
        for icx in range(ICF):
            tr = p.do("pe", (lambda icx=icx: (lambda e: e.transpose(
                out=psb[:].bitcast(BF16)[:, 0:128],
                in_=wq[:, 128 * icx:128 * (icx + 1)],
                identity=idb[:])))(), waits=[qv, ldi, tprev])
            tprev = p.do("act", (lambda icx=icx, hc=hc: (lambda e: e.activation(
                out=wdT[:, icx, 128 * hc:128 * (hc + 1)],
                in_=psb[:].bitcast(BF16)[:, 0:128],
                func=AF.Copy)))(), waits=[tr])
        prev = tprev
    cmb = p.do("dve", lambda e: e.tensor_scalar(out=comb[:], in0=swd[:],
               scalar1=colb[:, 1:2], scalar2=1.0 / 127.0, op0=ALU.mult,
               op1=ALU.mult), waits=[prev, sxc])

    p.barrier()

    for it in range(NT):
        t0 = it * TT
        p.barrier()
        ldg2 = p.do("sync", (lambda t0=t0: DMA(
            gpt[:], gpT.rearrange("(c p) t -> p c t", p=128)[:, :, t0:t0 + TT]))(),
            inc=16)
        ldx = p.do("sync", (lambda t0=t0: DMA(
            xmt[:], xmidT.rearrange("(c p) t -> p c t", p=128)[:, :, t0:t0 + TT]))(),
            inc=16)
        p.do("dve", lambda e: e.tensor_scalar(
            out=gpt[:].rearrange("p c l -> p (c l)"),
            in0=gpt[:].rearrange("p c l -> p (c l)"), scalar1=colb[:, 2:3],
            scalar2=None, op0=ALU.mult), waits=[ldg2, rsc])
        p.do("dve", lambda e: e.tensor_scalar(
            out=gpt[:].rearrange("p c l -> p (c l)"),
            in0=gpt[:].rearrange("p c l -> p (c l)"), scalar1=RND, scalar2=RND,
            op0=ALU.add, op1=ALU.subtract))
        q3 = p.do("dve", lambda e: e.tensor_scalar(
            out=hq[:].rearrange("p c l -> p (c l)"),
            in0=gpt[:].rearrange("p c l -> p (c l)"), scalar1=127.0,
            scalar2=-128.0, op0=ALU.min, op1=ALU.max))

        evac = [None, None]
        out_done = None
        for m in range(DC):
            bank = m % 2
            pst = ps0 if bank == 0 else ps1
            for k in range(ICF):
                mm = p.do("pe", (lambda m=m, k=k, pst=pst: (lambda e: e.matmul(
                    out=pst[:, 0:TT], lhsT=wdT[:, k, 128 * m:128 * (m + 1)],
                    rhs=hq[:, k, :], start=(k == 0), stop=(k == ICF - 1))))(),
                    waits=[q3, cmb, evac[bank]] if k == 0 else [])
            p.do("dve", (lambda m=m, pst=pst: (lambda e: e.tensor_scalar(
                out=tmp1[:], in0=pst[:, 0:TT], scalar1=comb[:, m:m + 1],
                scalar2=None, op0=ALU.mult)))(), waits=[mm])
            out_done = p.do("dve", (lambda m=m: (lambda e: e.scalar_tensor_tensor(
                out=outt[:, m, :], in0=tmp1[:], scalar=gbc[:, m:m + 1],
                in1=xmt[:, m, :], op0=ALU.mult, op1=ALU.add)))(),
                waits=[ldx, ldg])
            evac[bank] = out_done
        p.do("sync", (lambda t0=t0: DMA(
            o_out.rearrange("(c p) t -> p c t", p=128)[:, :, t0:t0 + TT],
            outt[:]))(), waits=[out_done], inc=16)

    p.do("dve", lambda e: e.memset(tmp1[:, 2:3], 0.0), waits=[p.ev("sync")])

    _emit_all(nc, p)
    return nc


_CACHE = {}


def _programs():
    if "l1" not in _CACHE:
        _CACHE["l1"] = build_l1(256, 8)
        _CACHE["l2"] = build_l2(512, 4)
        _CACHE["l3"] = build_l3(256, 8)
    return _CACHE["l1"], _CACHE["l2"], _CACHE["l3"]


def _colize(v, chunks):
    return np.ascontiguousarray(np.asarray(v, np.float32).reshape(chunks, 128).T)


def kernel(L0_attn_residual, L0_attn_norm_w, L0_attn_W_in, L0_attn_conv_w,
           L0_attn_conv_b, L0_attn_W_x, L0_attn_W_dt, L0_attn_b_dt,
           L0_attn_A_log, L0_attn_D, L0_attn_W_out, L0_ffn_clip_x,
           L0_ffn_clip_down, L0_ffn_w_gate, L0_ffn_w_up, L0_ffn_w_down,
           L0_c, L0_wmod):
    x = np.asarray(L0_attn_residual, np.float32)
    l1, l2, l3 = _programs()

    onehot = np.zeros((32, 32 * 128), BF)
    for n in range(32):
        onehot[n, 128 * n:128 * (n + 1)] = 1.0
    identf = np.eye(128, dtype=np.float32)
    identb = np.eye(128, dtype=BF)

    normw_col = _colize(L0_attn_norm_w, DC)
    convb_col = _colize(L0_attn_conv_b, IC)
    bdt_col = _colize(L0_attn_b_dt, IC)
    dvec_col = _colize(L0_attn_D, IC)
    cw = np.asarray(L0_attn_conv_w, np.float32).reshape(IC, 128, 4)
    convw_col = np.ascontiguousarray(cw.transpose(1, 0, 2).reshape(128, IC * 4))
    al = np.asarray(L0_attn_A_log, np.float32).reshape(IC, 128, NS)
    alog_col = np.ascontiguousarray(al.transpose(1, 0, 2).reshape(128, IC * NS))

    in1 = [dict(
        xT=np.ascontiguousarray(x[b].T),
        c_col=np.ascontiguousarray(np.asarray(L0_c, np.float32)[b].reshape(DC, 128).T),
        wmod=np.asarray(L0_wmod, np.float32),
        w_in=np.asarray(L0_attn_W_in, np.float32),
        w_x=np.asarray(L0_attn_W_x, np.float32),
        w_dt=np.asarray(L0_attn_W_dt, np.float32),
        w_out=np.asarray(L0_attn_W_out, np.float32),
        alog_col=alog_col, normw_col=normw_col, convw_col=convw_col,
        convb_col=convb_col, bdt_col=bdt_col, dvec_col=dvec_col,
        onehot=onehot, identf=identf,
    ) for b in range(NCORES)]
    r1 = run_bass_kernel_spmd(l1, in1, core_ids=list(range(NCORES)))

    mx8 = np.ascontiguousarray(np.concatenate(
        [r1.results[b]["o_max"].reshape(1, 1) for b in range(NCORES)], axis=1))
    in2 = [dict(
        h2T=r1.results[b]["o_h2"], mx8=mx8,
        clipx=np.asarray(L0_ffn_clip_x, np.float32).reshape(1, 1),
        w_gate=np.asarray(L0_ffn_w_gate, np.float32),
        w_up=np.asarray(L0_ffn_w_up, np.float32),
        identb=identb,
    ) for b in range(NCORES)]
    r2 = run_bass_kernel_spmd(l2, in2, core_ids=list(range(NCORES)))

    mxg8 = np.ascontiguousarray(np.concatenate(
        [r2.results[b]["o_maxg"].reshape(1, 1) for b in range(NCORES)], axis=1))
    in3 = [dict(
        gpT=r2.results[b]["o_gp"], xmidT=r1.results[b]["o_xmid"], mx8=mxg8,
        clipd=np.asarray(L0_ffn_clip_down, np.float32).reshape(1, 1),
        w_down=np.asarray(L0_ffn_w_down, np.float32),
        gb_col=r1.results[b]["o_gb"], identb=identb,
    ) for b in range(NCORES)]
    r3 = run_bass_kernel_spmd(l3, in3, core_ids=list(range(NCORES)))

    return np.stack([np.ascontiguousarray(r3.results[b]["o_out"].T)
                     for b in range(NCORES)], axis=0)


# version 08
# speedup vs baseline: 2.1411x; 2.1411x over previous
import numpy as np
import ml_dtypes
import concourse.bass as bass
import concourse.mybir as mybir
from concourse.bass_utils import run_bass_kernel_spmd

F32 = mybir.dt.float32
BF16 = mybir.dt.bfloat16
AF = mybir.ActivationFunctionType
ALU = mybir.AluOpType

NCORES = 8
D = 1024
DC = 8
DI = 2048
IC = 16
NS = 16
DTR = 64
T = 2048
I_FFN = 2816
ICF = 22
EPS = 1e-5
RND = 12582912.0
BF = ml_dtypes.bfloat16

ENGS = ("sync", "pe", "act", "dve", "pool")
DBG = False


class Prog:

    def __init__(self):
        self.ops = {e: [] for e in ENGS}
        self.cnt = {e: 0 for e in ENGS}
        self._pending = {e: [] for e in ENGS}

    def do(self, eng, fn, waits=(), inc=1):
        self.cnt[eng] += inc
        w = tuple(self._pending[eng]) + tuple(x for x in waits if x is not None)
        self._pending[eng] = []
        self.ops[eng].append((fn, w, inc))
        return (eng, self.cnt[eng])

    def ev(self, eng):
        return (eng, self.cnt[eng])

    def barrier(self):
        evs = [(e, self.cnt[e]) for e in ENGS if self.cnt[e] > 0]
        for e in ENGS:
            self._pending[e].extend(evs)

    def emit(self, block, sems):
        def mk(eng_name):
            def body(eng_obj):
                for fn, waits, inc in self.ops[eng_name]:
                    best = {}
                    for (en, c) in waits:
                        if c > best.get(en, 0):
                            best[en] = c
                    for en, c in best.items():
                        eng_obj.wait_ge(sems[en], c)
                    fn(eng_obj).then_inc(sems[eng_name], inc)
            return body

        block.sync(mk("sync"))
        block.tensor(mk("pe"))
        block.scalar(mk("act"))
        block.vector(mk("dve"))
        block.gpsimd(mk("pool"))


def _emit_all(nc, p):
    with (
        nc.semaphore("s_sync") as s_sync,
        nc.semaphore("s_pe") as s_pe,
        nc.semaphore("s_act") as s_act,
        nc.semaphore("s_dve") as s_dve,
        nc.semaphore("s_pool") as s_pool,
        nc.Block() as block,
    ):
        sems = dict(sync=s_sync, pe=s_pe, act=s_act, dve=s_dve, pool=s_pool)
        p.emit(block, sems)


def DMA(out, in_):
    return lambda e: e.dma_start(out=out, in_=in_)



def build_l1(TT, NT):
    nc = bass.Bass("TRN2", target_bir_lowering=False)
    xT = nc.dram_tensor("xT", [D, T], F32, kind="ExternalInput").ap()
    c_col = nc.dram_tensor("c_col", [128, DC], F32, kind="ExternalInput").ap()
    wmod = nc.dram_tensor("wmod", [D, 6 * D], F32, kind="ExternalInput").ap()
    w_in = nc.dram_tensor("w_in", [D, 2 * DI], F32, kind="ExternalInput").ap()
    w_x = nc.dram_tensor("w_x", [DI, 96], F32, kind="ExternalInput").ap()
    w_dt = nc.dram_tensor("w_dt", [DTR, DI], F32, kind="ExternalInput").ap()
    w_out = nc.dram_tensor("w_out", [DI, D], F32, kind="ExternalInput").ap()
    alog_col = nc.dram_tensor("alog_col", [128, IC * NS], F32, kind="ExternalInput").ap()
    normw_col = nc.dram_tensor("normw_col", [128, DC], F32, kind="ExternalInput").ap()
    convw_col = nc.dram_tensor("convw_col", [128, IC * 4], F32, kind="ExternalInput").ap()
    convb_col = nc.dram_tensor("convb_col", [128, IC], F32, kind="ExternalInput").ap()
    bdt_col = nc.dram_tensor("bdt_col", [128, IC], F32, kind="ExternalInput").ap()
    dvec_col = nc.dram_tensor("dvec_col", [128, IC], F32, kind="ExternalInput").ap()
    onehot = nc.dram_tensor("onehot", [32, 32 * 128], BF16, kind="ExternalInput").ap()
    identf = nc.dram_tensor("identf", [128, 128], F32, kind="ExternalInput").ap()
    o_xmid = nc.dram_tensor("o_xmid", [D, T], F32, kind="ExternalOutput").ap()
    o_h2 = nc.dram_tensor("o_h2", [D, T], F32, kind="ExternalOutput").ap()
    o_max = nc.dram_tensor("o_max", [1, 1], F32, kind="ExternalOutput").ap()
    o_gb = nc.dram_tensor("o_gb", [128, DC], F32, kind="ExternalOutput").ap()
    o_mc = nc.dram_tensor("o_mc", [128, 48], F32, kind="ExternalOutput").ap()
    if DBG:
        o_hid = nc.dram_tensor("o_hid", [128, DC * TT], BF16, kind="ExternalOutput").ap()
        o_xs = nc.dram_tensor("o_xs", [128, IC * TT], BF16, kind="ExternalOutput").ap()
        o_zb = nc.dram_tensor("o_zb", [128, IC * TT], BF16, kind="ExternalOutput").ap()
        o_dtb = nc.dram_tensor("o_dtb", [128, IC * TT], BF16, kind="ExternalOutput").ap()
        o_ya = nc.dram_tensor("o_ya", [128, IC * TT], BF16, kind="ExternalOutput").ap()
        o_ga2 = nc.dram_tensor("o_ga2", [128, DC], F32, kind="ExternalOutput").ap()
        o_ball = nc.dram_tensor("o_ball", [128, IC * (TT + 1)], BF16, kind="ExternalOutput").ap()
        o_hall = nc.dram_tensor("o_hall", [128, IC * (TT + 1)], BF16, kind="ExternalOutput").ap()
        o_bcsb = nc.dram_tensor("o_bcsb", [128, 2 * TT], BF16, kind="ExternalOutput").ap()
        o_cdbg = nc.dram_tensor("o_cdbg", [128, 6 * DC], F32, kind="ExternalOutput").ap()
        o_ga_pre = nc.dram_tensor("o_ga_pre", [128, DC], F32, kind="ExternalOutput").ap()
        o_acl = nc.dram_tensor("o_acl", [128, IC * NS], F32, kind="ExternalOutput").ap()
    wout_bf = nc.dram_tensor("wout_bf", [IC, 128, D], BF16, kind="Internal").ap()

    a = nc.alloc_sbuf_tensor
    x_t = a("x_t", [128, DC, TT], F32)
    h1 = a("h1", [128, DC, TT], F32)
    h1b = a("h1b", [128, DC, TT], BF16)
    hid = a("hid", [128, DC, TT], BF16)
    x_raw = a("x_raw", [128, IC, TT + 3], BF16)
    xs_bf = a("xs_bf", [128, IC, TT], BF16)
    z_bf = a("z_bf", [128, IC, TT], BF16)
    dt_bf = a("dt_bf", [128, IC, TT], BF16)
    u_bf = a("u_bf", [128, IC, TT], BF16)
    xdbl = a("xdbl", [96, TT], BF16)
    bcr = a("bcr", [32, TT], BF16)
    dtA = a("dtA", [128, IC, TT], BF16)
    a_all = a("a_all", [128, IC, TT + 1], F32)
    b_all = a("b_all", [128, IC, TT + 1], BF16)
    h_all = a("h_all", [128, IC, TT + 1], BF16)
    y_acc = a("y_acc", [128, IC, TT], BF16)
    bc_sb = a("bc_sb", [128, 2, TT], BF16)
    state = a("state", [128, IC, NS], F32)
    winb = a("winb", [128, DC, 2 * DI], BF16)
    woutb = a("woutb", [128, 4, D], BF16)
    wxb = a("wxb", [128, IC, 96], BF16)
    wdtb = a("wdtb", [64, DI], BF16)
    acol = a("acol", [128, IC * NS], F32)
    mcols = a("mcols", [128, 48], F32)
    gacl = a("gacl", [128, DC], F32)
    gbcl = a("gbcl", [128, DC], F32)
    ca1 = a("ca1", [128, DC], F32)
    cb1 = a("cb1", [128, DC], F32)
    nwc = a("nwc", [128, DC], F32)
    cwc = a("cwc", [128, IC * 4], F32)
    cbc = a("cbc", [128, IC], F32)
    bdc = a("bdc", [128, IC], F32)
    dvc = a("dvc", [128, IC], F32)
    ccol = a("ccol", [128, DC], F32)
    idf = a("idf", [128, 128], F32)
    ones_c = a("ones_c", [128, 1], F32)
    ones_r = a("ones_r", [1, 128], F32)
    rows = a("rows", [1, 2 * TT + 16], F32)
    mxc = a("mxc", [128, 1], F32)
    tmpc = a("tmpc", [128, TT], F32)
    cvac = a("cvac", [128, TT], F32)
    csts = a("csts", [128, 2], F32)
    cdbg = a("cdbg", [128, 6 * DC], F32) if DBG else None
    nc.const_aps.register(F32, -1.0, csts[:, 0:1])
    nc.const_aps.register(F32, -0.5, csts[:, 1:2])

    ohot = woutb[:].rearrange("p a b -> p (a b)")[0:32, 0:32 * 128]

    ps_w0 = nc.alloc_psum_tensor("ps_w0", [128, 512], F32)
    ps_w1 = nc.alloc_psum_tensor("ps_w1", [128, 512], F32)
    ps_w2 = nc.alloc_psum_tensor("ps_w2", [128, 512], F32)
    ps_bc = nc.alloc_psum_tensor("ps_bc", [128, 512], F32)
    ps_row = nc.alloc_psum_tensor("ps_row", [1, 512], F32)
    ps_b2 = nc.alloc_psum_tensor("ps_b2", [128, 512], F32)
    ps_m = nc.alloc_psum_tensor("ps_m", [128, 128], F32)
    psel = [ps_w0, ps_w1, ps_w2]

    p = Prog()

    for tile, src in [(ccol, c_col), (acol, alog_col), (nwc, normw_col),
                      (cwc, convw_col), (cbc, convb_col), (bdc, bdt_col),
                      (dvc, dvec_col), (idf, identf)]:
        p.do("sync", DMA(tile[:], src), inc=16)
    ld_consts = p.ev("sync")

    p.do("dve", lambda e: e.memset(ones_c[:], 1.0))
    p.do("dve", lambda e: e.memset(ones_r[:], 1.0))
    p.do("dve", lambda e: e.memset(state[:].rearrange("p i n -> p (i n)"), 0.0))
    p.do("dve", lambda e: e.memset(mxc[:], 0.0))
    p.do("dve", lambda e: e.memset(x_raw[:, :, 0:3], 0.0))
    p.do("dve", lambda e: e.memset(csts[:, 0:1], -1.0))
    p.do("dve", lambda e: e.memset(csts[:, 1:2], -0.5))
    init_dve = p.ev("dve")
    p._pending["act"].append(init_dve)

    silu_c = p.do("act", lambda e: e.activation(out=ccol[:], in_=ccol[:],
                  func=AF.Silu), waits=[ld_consts])
    a_exp = p.do("act", lambda e: e.activation(out=acol[:], in_=acol[:],
                 func=AF.Exp), waits=[ld_consts])
    a_neg = p.do("dve", lambda e: e.tensor_scalar(out=acol[:], in0=acol[:],
                 scalar1=-1.0, scalar2=None, op0=ALU.mult), waits=[a_exp])

    af32 = a_all[:].rearrange("p i l -> p (i l)")
    last_mm = None
    mm_hist = [None, None]
    first = True
    ci = 0
    for k in range(DC):
        for qq in range(4):
            half = ci % 2
            off = 1536 * half
            ld = p.do("sync", (lambda k=k, qq=qq, off=off: DMA(
                af32[:, off:off + 1536],
                wmod[128 * k:128 * (k + 1), 1536 * qq:1536 * (qq + 1)]))(),
                waits=[silu_c, mm_hist[half]], inc=16)
            for m in range(12):
                mcol = 12 * qq + m
                last_mm = p.do("pe", (lambda m=m, mcol=mcol, k=k, off=off,
                               first=first: (lambda e: e.matmul(
                    out=ps_m[:, mcol:mcol + 1],
                    lhsT=af32[:, off + 128 * m:off + 128 * (m + 1)],
                    rhs=ccol[:, k:k + 1],
                    start=first, stop=(k == DC - 1))))(),
                    waits=[ld] if m == 0 else [])
                first = False
            mm_hist[half] = last_mm
            ci += 1
    mod_cols = p.do("dve", lambda e: e.tensor_copy(out=mcols[:], in_=ps_m[:, 0:48]),
                    waits=[last_mm])
    p.do("dve", lambda e: e.tensor_scalar(out=ca1[:], in0=mcols[:, 8:16],
         scalar1=1.0, scalar2=None, op0=ALU.add), waits=[mod_cols])
    p.do("dve", lambda e: e.tensor_scalar(out=cb1[:], in0=mcols[:, 32:40],
         scalar1=1.0, scalar2=None, op0=ALU.add), waits=[mod_cols])

    def clamp_g(dst, lo, prev=None):
        src = mcols[:, lo:lo + DC]
        sg0 = p.do("act", lambda e: e.activation(out=tmpc[:, 0:DC], in_=src,
                   func=AF.Sign), waits=[mod_cols, prev])
        if DBG and lo == 16:
            p.do("dve", lambda e: e.tensor_copy(out=cdbg[:, 0:DC],
                 in_=tmpc[:, 0:DC]), waits=[sg0])
        e1 = p.do("dve", lambda e: e.tensor_tensor(out=tmpc[:, DC:2 * DC],
             in0=tmpc[:, 0:DC], in1=tmpc[:, 0:DC], op=ALU.mult), waits=[sg0])
        e2 = p.do("dve", lambda e: e.tensor_scalar(out=tmpc[:, DC:2 * DC],
             in0=tmpc[:, DC:2 * DC], scalar1=-1.0, scalar2=1.0, op0=ALU.mult,
             op1=ALU.add), waits=[e1])
        e3 = p.do("dve", lambda e: e.tensor_tensor(out=tmpc[:, 0:DC],
             in0=tmpc[:, 0:DC], in1=tmpc[:, DC:2 * DC], op=ALU.add), waits=[e2])
        if DBG and lo == 16:
            p.do("dve", lambda e: e.tensor_copy(out=cdbg[:, DC:2 * DC],
                 in_=tmpc[:, 0:DC]))
        ab = p.do("act", lambda e: e.activation(out=dst, in_=src, func=AF.Abs),
                  waits=[mod_cols, sg0])
        if DBG and lo == 16:
            p.do("dve", lambda e: e.tensor_copy(out=cdbg[:, 2 * DC:3 * DC],
                 in_=dst), waits=[ab])
        e4 = p.do("dve", lambda e: e.tensor_scalar(out=dst, in0=dst, scalar1=0.25,
             scalar2=None, op0=ALU.max), waits=[ab])
        if DBG and lo == 16:
            p.do("dve", lambda e: e.tensor_copy(out=cdbg[:, 3 * DC:4 * DC],
                 in_=dst), waits=[e4])
        ret = p.do("dve", lambda e: e.tensor_tensor(out=dst, in0=dst,
                   in1=tmpc[:, 0:DC], op=ALU.mult), waits=[e3, e4])
        if DBG and lo == 16:
            p.do("dve", lambda e: e.tensor_copy(out=cdbg[:, 4 * DC:5 * DC],
                 in_=dst))
            p.do("dve", lambda e: e.tensor_copy(out=cdbg[:, 5 * DC:6 * DC],
                 in_=mcols[:, 16:24]))
        return ret

    ga_done = clamp_g(gacl[:], 16)
    gb_done = clamp_g(gbcl[:], 40, prev=ga_done)
    if DBG:
        p.do("sync", DMA(o_ga_pre, gacl[:]), waits=[gb_done], inc=16)
    p.do("sync", DMA(o_gb, gbcl[:]), waits=[gb_done], inc=16)
    p.do("sync", DMA(o_mc, mcols[:]), waits=[gb_done], inc=16)

    dtAf = dtA[:].rearrange("p i l -> p (i l)")
    cst_hist = [last_mm, last_mm]
    ci = 0
    for k in range(DC):
        for hh in range(2):
            half = ci % 2
            off = 2048 * half
            ld = p.do("sync", (lambda k=k, hh=hh, off=off: DMA(
                af32[:, off:off + 2048], w_in[128 * k:128 * (k + 1),
                                              2048 * hh:2048 * (hh + 1)]))(),
                waits=[cst_hist[half]], inc=16)
            cst_hist[half] = p.do("dve", (lambda k=k, hh=hh, off=off: (lambda e: e.tensor_copy(
                out=winb[:, k, 2048 * hh:2048 * (hh + 1)],
                in_=af32[:, off:off + 2048])))(), waits=[ld])
            ci += 1
    cst_prev = cst_hist[(ci + 1) % 2]
    cst_prev = cst_hist[ci % 2] if cst_hist[ci % 2] else cst_prev
    st_prev = None
    for k in range(IC):
        ld = p.do("sync", (lambda k=k: DMA(
            af32[:, 0:D], w_out[128 * k:128 * (k + 1), :]))(),
            waits=[cst_prev, st_prev], inc=16)
        cst_prev = p.do("dve", (lambda k=k: (lambda e: e.tensor_copy(
            out=dtAf[:, 0:D], in_=af32[:, 0:D])))(), waits=[ld, st_prev])
        st_prev = p.do("sync", (lambda k=k: DMA(
            wout_bf[k], dtAf[:, 0:D]))(), waits=[cst_prev], inc=16)
    for icx in range(IC):
        ld = p.do("sync", (lambda icx=icx: DMA(
            af32[:, 0:96], w_x[128 * icx:128 * (icx + 1), :]))(),
            waits=[cst_prev, st_prev], inc=16)
        st_prev = None
        cst_prev = p.do("dve", (lambda icx=icx: (lambda e: e.tensor_copy(
            out=wxb[:, icx, :], in_=af32[:, 0:96])))(), waits=[ld])
    ld = p.do("sync", DMA(af32[0:64, 0:DI], w_dt), waits=[cst_prev], inc=16)
    p.do("dve", lambda e: e.tensor_copy(out=wdtb[:], in_=af32[0:64, 0:DI]),
         waits=[ld])
    p.do("dve", lambda e: e.memset(af32, 0.0))

    p.barrier()

    for it in range(NT):
        t0 = it * TT
        p.barrier()
        ld_x = p.do("sync", (lambda t0=t0: DMA(
            x_t[:], xT.rearrange("(c p) t -> p c t", p=128)[:, :, t0:t0 + TT]))(),
            inc=16)
        oh_ld = p.do("sync", DMA(ohot, onehot), inc=16)
        sq = p.do("act", lambda e: e.activation(
            out=a_all[:, 0:DC, 1:1 + TT],
            in_=x_t[:].rearrange("p c l -> p (c l)"), func=AF.Square), waits=[ld_x])
        for dc in range(DC):
            m1 = p.do("pe", (lambda dc=dc: (lambda e: e.matmul(
                out=ps_row[0:1, 0:TT], lhsT=ones_c[:], rhs=x_t[:, dc, :],
                start=(dc == 0), stop=(dc == DC - 1))))(), waits=[ld_x])
        for dc in range(DC):
            m2 = p.do("pe", (lambda dc=dc: (lambda e: e.matmul(
                out=ps_row[0:1, TT:2 * TT], lhsT=ones_c[:], rhs=a_all[:, dc, 1:1 + TT],
                start=(dc == 0), stop=(dc == DC - 1))))(), waits=[sq])
        mean_r = rows[0:1, 0:TT]
        var_r = rows[0:1, TT:2 * TT]
        msq_r = tmpc[0:1, 0:TT]
        p.do("dve", lambda e: e.tensor_scalar(out=mean_r, in0=ps_row[0:1, 0:TT],
             scalar1=1.0 / D, scalar2=None, op0=ALU.mult), waits=[m1, m2])
        p.do("dve", lambda e: e.tensor_scalar(out=var_r, in0=ps_row[0:1, TT:2 * TT],
             scalar1=1.0 / D, scalar2=EPS, op0=ALU.mult, op1=ALU.add))
        p.do("dve", lambda e: e.tensor_tensor(out=msq_r, in0=mean_r, in1=mean_r,
             op=ALU.mult))
        sub_ev = p.do("dve", lambda e: e.tensor_tensor(out=var_r, in0=var_r,
                      in1=msq_r, op=ALU.subtract))
        p.do("act", lambda e: e.activation(out=var_r, in_=var_r, func=AF.Ln),
             waits=[sub_ev])
        rstd_ev = p.do("act", lambda e: e.activation(out=var_r, in_=var_r,
                       func=AF.Exp, scale=-0.5))
        bc1 = p.do("pe", lambda e: e.matmul(out=ps_b2[:, 0:TT], lhsT=ones_r[:],
                   rhs=mean_r, start=True, stop=True), waits=[sub_ev])
        bc2 = p.do("pe", lambda e: e.matmul(out=ps_b2[:, TT:2 * TT], lhsT=ones_r[:],
                   rhs=var_r, start=True, stop=True), waits=[rstd_ev])
        for dc in range(DC):
            p.do("dve", (lambda dc=dc: (lambda e: e.tensor_tensor(
                out=h1[:, dc, :], in0=x_t[:, dc, :], in1=ps_b2[:, 0:TT],
                op=ALU.subtract)))(), waits=[bc1, bc2] if dc == 0 else [])
            p.do("dve", (lambda dc=dc: (lambda e: e.tensor_tensor(
                out=h1[:, dc, :], in0=h1[:, dc, :], in1=ps_b2[:, TT:2 * TT],
                op=ALU.mult)))())
            h1_done = p.do("dve", (lambda dc=dc: (lambda e: e.tensor_scalar(
                out=h1[:, dc, :], in0=h1[:, dc, :], scalar1=ca1[:, dc:dc + 1],
                scalar2=mcols[:, dc:dc + 1], op0=ALU.mult, op1=ALU.add)))())
        h1b_ev = p.do("dve", lambda e: e.tensor_copy(
            out=h1b[:].rearrange("p c l -> p (c l)"),
            in_=h1[:].rearrange("p c l -> p (c l)")))

        sq2 = p.do("act", lambda e: e.activation(
            out=a_all[:, DC:2 * DC, 1:1 + TT],
            in_=h1[:].rearrange("p c l -> p (c l)"), func=AF.Square),
            waits=[h1_done])
        for dc in range(DC):
            m3 = p.do("pe", (lambda dc=dc: (lambda e: e.matmul(
                out=ps_row[0:1, 0:TT], lhsT=ones_c[:], rhs=a_all[:, DC + dc, 1:1 + TT],
                start=(dc == 0), stop=(dc == DC - 1))))(), waits=[sq2, bc2])
        mr = p.do("dve", lambda e: e.tensor_scalar(out=mean_r, in0=ps_row[0:1, 0:TT],
                  scalar1=1.0 / D, scalar2=EPS, op0=ALU.mult, op1=ALU.add),
                  waits=[m3])
        p.do("act", lambda e: e.activation(out=mean_r, in_=mean_r, func=AF.Ln),
             waits=[mr])
        r2 = p.do("act", lambda e: e.activation(out=mean_r, in_=mean_r, func=AF.Exp,
                  scale=-0.5))
        bc3 = p.do("pe", lambda e: e.matmul(out=ps_b2[:, 0:TT], lhsT=ones_r[:],
                   rhs=mean_r, start=True, stop=True), waits=[r2])
        for dc in range(DC):
            p.do("dve", (lambda dc=dc: (lambda e: e.tensor_tensor(
                out=tmpc[:], in0=h1[:, dc, :], in1=ps_b2[:, 0:TT],
                op=ALU.mult)))(), waits=[bc3] if dc == 0 else [])
            hid_done = p.do("dve", (lambda dc=dc: (lambda e: e.tensor_scalar(
                out=hid[:, dc, :], in0=tmpc[:], scalar1=nwc[:, dc:dc + 1],
                scalar2=None, op0=ALU.mult)))())

        if it > 0:
            tail_ev = p.do("dve", lambda e: e.tensor_copy(
                out=x_raw[:, :, 0:3], in_=x_raw[:, :, TT:TT + 3]))
        else:
            tail_ev = p.ev("dve")

        evac_ev = {0: None, 1: None, 2: None}
        xz_evs = []
        for m in range(32):
            bank = m % 3
            for k in range(DC):
                wd = [hid_done, evac_ev[bank]] if k == 0 else []
                mm_l = p.do("pe", (lambda m=m, k=k, bank=bank: (lambda e: e.matmul(
                    out=psel[bank][:, 0:TT],
                    lhsT=winb[:, k, 128 * m:128 * (m + 1)],
                    rhs=hid[:, k, :], start=(k == 0), stop=(k == DC - 1))))(),
                    waits=wd)
            if m < IC:
                ev = p.do("act", (lambda m=m, bank=bank: (lambda e: e.activation(
                    out=x_raw[:, m, 3:3 + TT], in_=psel[bank][:, 0:TT],
                    func=AF.Copy)))(), waits=[mm_l, tail_ev])
            else:
                ev = p.do("act", (lambda m=m, bank=bank: (lambda e: e.activation(
                    out=z_bf[:, m - IC, :], in_=psel[bank][:, 0:TT],
                    func=AF.Silu)))(), waits=[mm_l])
            evac_ev[bank] = ev
            xz_evs.append(ev)

        conv_evs = []
        for icx in range(IC):
            wd = [xz_evs[icx], tail_ev]
            if conv_evs:
                wd.append(conv_evs[-1])
            p.do("dve", (lambda icx=icx: (lambda e: e.tensor_scalar(
                out=cvac[:], in0=x_raw[:, icx, 0:TT],
                scalar1=cwc[:, 4 * icx:4 * icx + 1], scalar2=None,
                op0=ALU.mult)))(), waits=wd)
            for kk in range(1, 4):
                p.do("dve", (lambda icx=icx, kk=kk: (lambda e: e.scalar_tensor_tensor(
                    out=cvac[:], in0=x_raw[:, icx, kk:kk + TT],
                    scalar=cwc[:, 4 * icx + kk:4 * icx + kk + 1], in1=cvac[:],
                    op0=ALU.mult, op1=ALU.add)))())
            cv = p.ev("dve")
            conv_evs.append(p.do("act", (lambda icx=icx: (lambda e: e.activation(
                out=xs_bf[:, icx, :], in_=cvac[:], func=AF.Silu,
                bias=cbc[:, icx:icx + 1])))(), waits=[cv]))
        conv_done = conv_evs[-1]

        for icx in range(IC):
            mmx = p.do("pe", (lambda icx=icx: (lambda e: e.matmul(
                out=ps_w0[0:96, 0:TT], lhsT=wxb[:, icx, :], rhs=xs_bf[:, icx, :],
                start=(icx == 0), stop=(icx == IC - 1))))(),
                waits=[conv_evs[icx]] + ([evac_ev[0]] if icx == 0 else []))
        xdbl_ev = p.do("act", lambda e: e.activation(out=xdbl[:],
                       in_=ps_w0[0:96, 0:TT], func=AF.Copy), waits=[mmx])
        evac_ev[0] = xdbl_ev
        bcr_ev = p.do("act", lambda e: e.activation(out=bcr[:], in_=xdbl[64:96, :],
                      func=AF.Copy))

        dtp_last = None
        for m in range(IC):
            bank = 1 + (m % 2)
            mmd = p.do("pe", (lambda m=m, bank=bank: (lambda e: e.matmul(
                out=psel[bank][:, 0:TT], lhsT=wdtb[:, 128 * m:128 * (m + 1)],
                rhs=xdbl[0:64, :], start=True, stop=True)))(),
                waits=[xdbl_ev, evac_ev[bank]])
            dtp_last = p.do("dve", (lambda m=m, bank=bank: (lambda e: e.tensor_scalar(
                out=a_all[:, m, 1:1 + TT], in0=psel[bank][:, 0:TT],
                scalar1=bdc[:, m:m + 1], scalar2=None, op0=ALU.add)))(),
                waits=[mmd])
            evac_ev[bank] = dtp_last
        vv = a_all[:, :, 1:1 + TT]
        c1 = p.do("act", lambda e: e.activation(out=dtAf, in_=vv, func=AF.Abs),
                  waits=[dtp_last])
        p.do("act", lambda e: e.activation(out=dtAf, in_=dtAf, func=AF.Exp,
             scale=-1.0))
        c3 = p.do("act", lambda e: e.activation(out=dtAf, in_=dtAf, func=AF.Ln,
                  bias=1.0))
        p.do("dve", lambda e: e.tensor_scalar(
            out=dt_bf[:].rearrange("p i l -> p (i l)"), in0=vv, scalar1=0.0,
            scalar2=None, op0=ALU.max), waits=[c1])
        dt_done = p.do("dve", lambda e: e.tensor_tensor(
            out=dt_bf[:].rearrange("p i l -> p (i l)"),
            in0=dt_bf[:].rearrange("p i l -> p (i l)"), in1=dtAf, op=ALU.add),
            waits=[c3])
        u_done = p.do("dve", lambda e: e.tensor_tensor(
            out=u_bf[:].rearrange("p i l -> p (i l)"),
            in0=dt_bf[:].rearrange("p i l -> p (i l)"),
            in1=xs_bf[:].rearrange("p i l -> p (i l)"), op=ALU.mult))

        g_ev = None
        aH_prev = [None, None]
        stH_prev = [None, None]
        bce_prev = None
        sel2 = None
        for n in range(NS):
            seld = [bcr_ev, bce_prev, g_ev, oh_ld]
            p.do("pe", (lambda n=n: (lambda e: e.matmul(
                out=ps_bc[:, 0:TT], lhsT=ohot[:, 128 * n:128 * (n + 1)],
                rhs=bcr[:], start=True, stop=True)))(), waits=seld)
            sel2 = p.do("pe", (lambda n=n: (lambda e: e.matmul(
                out=ps_bc[:, TT:2 * TT], lhsT=ohot[:, 128 * (16 + n):128 * (17 + n)],
                rhs=bcr[:], start=True, stop=True)))())
            bce_prev = p.do("act", lambda e: e.activation(
                out=bc_sb[:].rearrange("p c l -> p (c l)"), in_=ps_bc[:, 0:2 * TT],
                func=AF.Copy), waits=[sel2])
            dtaH = [None, None]
            for H in (0, 1):
                for j in range(8):
                    icx = 8 * H + j
                    wd = []
                    if j == 0:
                        wd = ([dt_done, a_neg] if n == 0
                              else [aH_prev[H], g_ev])
                    dtaH[H] = p.do("dve", (lambda icx=icx, n=n: (lambda e: e.tensor_scalar(
                        out=dtA[:, icx, :], in0=dt_bf[:, icx, :],
                        scalar1=acol[:, NS * icx + n:NS * icx + n + 1],
                        scalar2=None, op0=ALU.mult)))(), waits=wd)
            for H in (0, 1):
                exp_w = [dtaH[H]]
                exp_w.append(stH_prev[H] if stH_prev[H] is not None else dt_done)
                aH_prev[H] = p.do("act", (lambda H=H: (lambda e: e.activation(
                    out=a_all[:, 8 * H:8 * H + 8, 1:1 + TT],
                    in_=dtA[:, 8 * H:8 * H + 8, :].rearrange("p i l -> p (i l)"),
                    func=AF.Exp)))(), waits=exp_w)
            p.do("pe", lambda e: e.matmul(out=ps_m[:, 0:64], lhsT=ohot[:, 0:128],
                 rhs=bcr[:, 0:64], start=True, stop=True), waits=[dtaH[0]])
            p.do("pe", lambda e: e.matmul(out=ps_m[:, 64:128], lhsT=ohot[:, 0:128],
                 rhs=bcr[:, 0:64], start=True, stop=True), waits=[dtaH[1]])
            yacc_ev = None
            for H in (0, 1):
                lo = 8 * H
                b_w = [bce_prev] + ([u_done] if n == 0 else [])
                p.do("dve", (lambda lo=lo: (lambda e: e.tensor_tensor(
                    out=b_all[:, lo:lo + 8, 1:1 + TT], in0=u_bf[:, lo:lo + 8, :],
                    in1=bc_sb[:, 0:1, :].broadcast_to([128, 8, TT]),
                    op=ALU.mult)))(), waits=b_w)
                rst_ev = p.do("dve", (lambda n=n, lo=lo: (lambda e: e.tensor_copy(
                    out=b_all[:, lo:lo + 8, 0], in_=state[:, lo:lo + 8, n])))())
                scn = p.do("dve", (lambda lo=lo: (lambda e: e.tensor_tensor_scan(
                    out=h_all[:, lo:lo + 8, :].rearrange("p i l -> p (i l)"),
                    data0=a_all[:, lo:lo + 8, :].rearrange("p i l -> p (i l)"),
                    data1=b_all[:, lo:lo + 8, :].rearrange("p i l -> p (i l)"),
                    initial=0.0, op0=ALU.mult, op1=ALU.add)))(),
                    waits=[aH_prev[H], rst_ev])
                stH_prev[H] = p.do("dve", (lambda n=n, lo=lo: (lambda e: e.tensor_copy(
                    out=state[:, lo:lo + 8, n], in_=h_all[:, lo:lo + 8, TT])))(),
                    waits=[scn])
                g_ev = p.do("dve", (lambda lo=lo: (lambda e: e.tensor_tensor(
                    out=dtA[:, lo:lo + 8, :].rearrange("p i l -> p (i l)"),
                    in0=h_all[:, lo:lo + 8, 1:1 + TT],
                    in1=bc_sb[:, 1:2, :].broadcast_to([128, 8, TT]),
                    op=ALU.mult)))(), waits=[aH_prev[H], dtaH[H]])
                if n == 0:
                    yacc_ev = p.do("dve", (lambda lo=lo: (lambda e: e.tensor_copy(
                        out=y_acc[:, lo:lo + 8, :].rearrange("p i l -> p (i l)"),
                        in_=dtA[:, lo:lo + 8, :].rearrange("p i l -> p (i l)"))))())
                else:
                    yacc_ev = p.do("dve", (lambda lo=lo: (lambda e: e.tensor_tensor(
                        out=y_acc[:, lo:lo + 8, :].rearrange("p i l -> p (i l)"),
                        in0=y_acc[:, lo:lo + 8, :].rearrange("p i l -> p (i l)"),
                        in1=dtA[:, lo:lo + 8, :].rearrange("p i l -> p (i l)"),
                        op=ALU.add)))())
                p.do("pe", (lambda H=H: (lambda e: e.matmul(
                    out=ps_m[:, 64 * H:64 * H + 64], lhsT=ohot[:, 0:128],
                    rhs=bcr[:, 0:64], start=True, stop=True)))(),
                    waits=[stH_prev[H]])

        for icx in range(IC):
            p.do("dve", (lambda icx=icx: (lambda e: e.scalar_tensor_tensor(
                out=dtA[:, 0, :], in0=xs_bf[:, icx, :], scalar=dvc[:, icx:icx + 1],
                in1=y_acc[:, icx, :], op0=ALU.mult, op1=ALU.add)))(),
                waits=[yacc_ev] if icx == 0 else [])
            yz_done = p.do("dve", (lambda icx=icx: (lambda e: e.tensor_tensor(
                out=x_raw[:, icx, 0:TT], in0=dtA[:, 0, :], in1=z_bf[:, icx, :],
                op=ALU.mult)))())

        accs = [(psel + [ps_bc])[m // 2][:, (m % 2) * TT:(m % 2 + 1) * TT]
                for m in range(DC)]
        mo = None
        for q in range(4):
            wo_ld = p.do("sync", (lambda q=q: DMA(
                woutb[:], wout_bf[4 * q:4 * q + 4]))(),
                waits=([yz_done, evac_ev[0], evac_ev[1], evac_ev[2], sel2,
                        bce_prev] if q == 0 else [mo]), inc=16)
            for k in range(4 * q, 4 * q + 4):
                for m in range(DC):
                    wd = [wo_ld, yz_done] if m == 0 else []
                    mo = p.do("pe", (lambda m=m, k=k: (lambda e: e.matmul(
                        out=accs[m],
                        lhsT=woutb[:, k % 4, 128 * m:128 * (m + 1)],
                        rhs=x_raw[:, k, 0:TT], start=(k == 0),
                        stop=(k == IC - 1))))(), waits=wd)
        xmid_done = None
        for m in range(DC):
            p.do("dve", (lambda m=m: (lambda e: e.tensor_tensor(
                out=tmpc[:], in0=accs[m], in1=h1b[:, m, :],
                op=ALU.add)))(), waits=[mo, h1b_ev] if m == 0 else [])
            xmid_done = p.do("dve", (lambda m=m: (lambda e: e.scalar_tensor_tensor(
                out=h1[:, m, :], in0=tmpc[:], scalar=gacl[:, m:m + 1],
                in1=x_t[:, m, :], op0=ALU.mult, op1=ALU.add)))())
        evac_ev[0] = evac_ev[1] = evac_ev[2] = xmid_done
        p.do("sync", (lambda t0=t0: DMA(
            o_xmid.rearrange("(c p) t -> p c t", p=128)[:, :, t0:t0 + TT], h1[:]))(),
            waits=[xmid_done], inc=16)

        sq3 = p.do("act", lambda e: e.activation(
            out=a_all[:, DC:2 * DC, 1:1 + TT],
            in_=h1[:].rearrange("p c l -> p (c l)"), func=AF.Square),
            waits=[xmid_done])
        for dc in range(DC):
            s1 = p.do("pe", (lambda dc=dc: (lambda e: e.matmul(
                out=ps_row[0:1, 0:TT], lhsT=ones_c[:], rhs=h1[:, dc, :],
                start=(dc == 0), stop=(dc == DC - 1))))(),
                waits=[xmid_done, bc3] if dc == 0 else [])
        for dc in range(DC):
            s2 = p.do("pe", (lambda dc=dc: (lambda e: e.matmul(
                out=ps_row[0:1, TT:2 * TT], lhsT=ones_c[:], rhs=a_all[:, DC + dc, 1:1 + TT],
                start=(dc == 0), stop=(dc == DC - 1))))(), waits=[sq3])
        p.do("dve", lambda e: e.tensor_scalar(out=mean_r, in0=ps_row[0:1, 0:TT],
             scalar1=1.0 / D, scalar2=None, op0=ALU.mult), waits=[s1, s2])
        p.do("dve", lambda e: e.tensor_scalar(out=var_r, in0=ps_row[0:1, TT:2 * TT],
             scalar1=1.0 / D, scalar2=EPS, op0=ALU.mult, op1=ALU.add))
        p.do("dve", lambda e: e.tensor_tensor(out=msq_r, in0=mean_r, in1=mean_r,
             op=ALU.mult))
        sb2 = p.do("dve", lambda e: e.tensor_tensor(out=var_r, in0=var_r,
                   in1=msq_r, op=ALU.subtract))
        p.do("act", lambda e: e.activation(out=var_r, in_=var_r, func=AF.Ln),
             waits=[sb2])
        rs2 = p.do("act", lambda e: e.activation(out=var_r, in_=var_r, func=AF.Exp,
                   scale=-0.5))
        bc4 = p.do("pe", lambda e: e.matmul(out=ps_b2[:, 0:TT], lhsT=ones_r[:],
                   rhs=mean_r, start=True, stop=True), waits=[sb2])
        bc5 = p.do("pe", lambda e: e.matmul(out=ps_b2[:, TT:2 * TT], lhsT=ones_r[:],
                   rhs=var_r, start=True, stop=True), waits=[rs2])
        mx_prev = None
        for dc in range(DC):
            p.do("dve", (lambda dc=dc: (lambda e: e.tensor_tensor(
                out=a_all[:, dc, 1:1 + TT], in0=h1[:, dc, :], in1=ps_b2[:, 0:TT],
                op=ALU.subtract)))(), waits=[bc4, bc5, sq3] if dc == 0 else [])
            p.do("dve", (lambda dc=dc: (lambda e: e.tensor_tensor(
                out=a_all[:, dc, 1:1 + TT], in0=a_all[:, dc, 1:1 + TT],
                in1=ps_b2[:, TT:2 * TT], op=ALU.mult)))())
            p.do("dve", (lambda dc=dc: (lambda e: e.tensor_scalar(
                out=a_all[:, dc, 1:1 + TT], in0=a_all[:, dc, 1:1 + TT],
                scalar1=cb1[:, dc:dc + 1],
                scalar2=mcols[:, 24 + dc:25 + dc], op0=ALU.mult, op1=ALU.add)))())
            rdev = p.do("dve", (lambda dc=dc: (lambda e: e.reduce_max(
                out=tmpc[:, dc:dc + 1], in_=a_all[:, dc, 1:1 + TT],
                axis=mybir.AxisListType.X,
                apply_absolute_value=True)))())
            mx_ev = p.do("dve", (lambda dc=dc: (lambda e: e.tensor_tensor(
                out=mxc[:], in0=mxc[:], in1=tmpc[:, dc:dc + 1], op=ALU.max)))(),
                waits=[rdev, mx_prev])
            mx_prev = mx_ev
        p.do("sync", (lambda t0=t0: DMA(
            o_h2.rearrange("(c p) t -> p c t", p=128)[:, :, t0:t0 + TT],
            a_all[:, 0:DC, 1:1 + TT]))(), waits=[mx_ev], inc=16)

    if DBG:
        allev = [p.ev("dve"), p.ev("act"), p.ev("pe")]
        p.do("sync", DMA(o_hid, hid[:].rearrange("p c l -> p (c l)")), waits=allev, inc=16)
        p.do("sync", DMA(o_xs, xs_bf[:].rearrange("p i l -> p (i l)")), waits=allev, inc=16)
        p.do("sync", DMA(o_zb, z_bf[:].rearrange("p i l -> p (i l)")), waits=allev, inc=16)
        p.do("sync", DMA(o_dtb, dt_bf[:].rearrange("p i l -> p (i l)")), waits=allev, inc=16)
        p.do("sync", DMA(o_ya, y_acc[:].rearrange("p i l -> p (i l)")), waits=allev, inc=16)
        p.do("sync", DMA(o_ga2, gacl[:]), waits=allev, inc=16)
        p.do("sync", DMA(o_ball, b_all[:].rearrange("p i l -> p (i l)")), waits=allev, inc=16)
        p.do("sync", DMA(o_hall, h_all[:].rearrange("p i l -> p (i l)")), waits=allev, inc=16)
        p.do("sync", DMA(o_bcsb, bc_sb[:].rearrange("p c l -> p (c l)")), waits=allev, inc=16)
        p.do("sync", DMA(o_cdbg, cdbg[:]), waits=allev, inc=16)
        p.do("sync", DMA(o_acl, acol[:]), waits=allev, inc=16)

    trm = p.do("pe", lambda e: e.transpose(out=ps_b2[0:1, 0:128], in_=mxc[:],
               identity=idf[:]), waits=[mx_ev, p.ev("dve")])
    mxr = p.do("dve", lambda e: e.reduce_max(out=rows[0:1, 0:1],
               in_=ps_b2[0:1, 0:128], axis=mybir.AxisListType.X), waits=[trm])
    p.do("sync", DMA(o_max, rows[0:1, 0:1]), waits=[mxr], inc=16)
    p.do("dve", lambda e: e.memset(tmpc[:, 0:1], 0.0), waits=[p.ev("sync")])

    _emit_all(nc, p)
    return nc



def build_l2(TT, NT):
    nc = bass.Bass("TRN2", target_bir_lowering=False)
    h2T = nc.dram_tensor("h2T", [D, T], F32, kind="ExternalInput").ap()
    mx8 = nc.dram_tensor("mx8", [1, 8], F32, kind="ExternalInput").ap()
    clipx = nc.dram_tensor("clipx", [1, 1], F32, kind="ExternalInput").ap()
    w_gate = nc.dram_tensor("w_gate", [I_FFN, D], F32, kind="ExternalInput").ap()
    w_up = nc.dram_tensor("w_up", [I_FFN, D], F32, kind="ExternalInput").ap()
    identb = nc.dram_tensor("identb", [128, 128], BF16, kind="ExternalInput").ap()
    o_gp = nc.dram_tensor("o_gp", [I_FFN, T], F32, kind="ExternalOutput").ap()
    o_maxg = nc.dram_tensor("o_maxg", [1, 1], F32, kind="ExternalOutput").ap()

    a = nc.alloc_sbuf_tensor
    wgT = a("wgT", [128, DC, I_FFN], BF16)
    wuT = a("wuT", [128, DC, I_FFN], BF16)
    swg = a("swg", [128, ICF], F32)
    swu = a("swu", [128, ICF], F32)
    wld = a("wld", [128, 2 * D], F32)
    wq = a("wq", [128, 2 * D], BF16)
    h2t = a("h2t", [128, DC, TT], F32)
    xq = a("xq", [128, DC, TT], BF16)
    gsil = a("gsil", [128, TT], F32)
    upsc = a("upsc", [128, TT], F32)
    gp = a("gp", [128, ICF, TT], F32)
    rowb = a("rowb", [1, 128], F32)
    colb = a("colb", [128, 4], F32)
    mxcl = a("mxcl", [128, 1], F32)
    tmp1 = a("tmp1", [128, TT], F32)
    idb = a("idb", [128, 128], BF16)
    idf2 = a("idf2", [128, 128], F32)
    onesr = a("onesr", [1, 128], F32)
    csts = a("csts", [128, 1], F32)
    nc.const_aps.register(F32, 0.5, csts[:, 0:1])

    ps0 = nc.alloc_psum_tensor("ps0", [128, 512], F32)
    ps1 = nc.alloc_psum_tensor("ps1", [128, 512], F32)
    psb = nc.alloc_psum_tensor("psb", [128, 512], F32)
    psr = nc.alloc_psum_tensor("psr", [128, 512], F32)

    p = Prog()

    ld0 = p.do("sync", DMA(rowb[0:1, 0:8], mx8), inc=16)
    ld1 = p.do("sync", DMA(rowb[0:1, 8:9], clipx), inc=16)
    ldi = p.do("sync", DMA(idb[:], identb), inc=16)
    p.do("dve", lambda e: e.memset(onesr[:], 1.0))
    p.do("dve", lambda e: e.memset(mxcl[:], 0.0))
    p.do("dve", lambda e: e.memset(csts[:, 0:1], 0.5))
    ones_ev = p.ev("dve")
    p._pending["act"].append(ones_ev)
    ic_ev = p.do("dve", lambda e: e.tensor_copy(out=idf2[:], in_=idb[:]),
                 waits=[ldi])

    gm = p.do("dve", lambda e: e.reduce_max(out=rowb[0:1, 9:10],
              in_=rowb[0:1, 0:8], axis=mybir.AxisListType.X), waits=[ld0])
    th = p.do("act", lambda e: e.activation(out=rowb[0:1, 10:11],
              in_=rowb[0:1, 8:9], func=AF.Tanh, scale=0.5), waits=[ld1])
    sg = p.do("dve", lambda e: e.tensor_scalar(out=rowb[0:1, 10:11],
              in0=rowb[0:1, 10:11], scalar1=0.5, scalar2=0.5, op0=ALU.mult,
              op1=ALU.add), waits=[th])
    pm = p.do("dve", lambda e: e.tensor_tensor(out=rowb[0:1, 11:12],
              in0=rowb[0:1, 9:10], in1=rowb[0:1, 10:11], op=ALU.mult),
              waits=[gm, sg])
    bcp = p.do("pe", lambda e: e.matmul(out=psr[:, 0:1], lhsT=onesr[:],
               rhs=rowb[0:1, 11:12], start=True, stop=True), waits=[pm, ones_ev])
    pc0 = p.do("dve", lambda e: e.tensor_copy(out=colb[:, 0:1], in_=psr[:, 0:1]),
               waits=[bcp])
    sxc = p.do("dve", lambda e: e.tensor_scalar(out=colb[:, 1:2], in0=colb[:, 0:1],
               scalar1=1.0 / 127.0, scalar2=None, op0=ALU.mult), waits=[pc0])
    rc0 = p.do("dve", lambda e: e.reciprocal(out=colb[:, 2:3], in_=colb[:, 0:1]),
               waits=[pc0])
    rsc = p.do("dve", lambda e: e.tensor_scalar(out=colb[:, 2:3], in0=colb[:, 2:3],
               scalar1=127.0, scalar2=None, op0=ALU.mult), waits=[rc0])

    def quant_w(src, dstT, swcol, first_ev):
        psbv0 = ps0[:].bitcast(BF16)
        psbv1 = ps1[:].bitcast(BF16)
        psv = [psbv0, psbv1]
        ld_h = [first_ev, None]
        q_h = [first_ev, first_ev]
        ev_h = [None, None]
        tprev = [None, None]
        ret = None
        for icx in range(ICF):
            sl = icx % 2
            ld = p.do("sync", (lambda icx=icx, sl=sl: DMA(
                wld[:, 1024 * sl:1024 * (sl + 1)],
                src[128 * icx:128 * (icx + 1), :]))(),
                waits=[q_h[sl]], inc=16)
            rm = p.do("dve", (lambda icx=icx, sl=sl: (lambda e: e.reduce_max(
                out=swcol[:, icx:icx + 1], in_=wld[:, 1024 * sl:1024 * (sl + 1)],
                axis=mybir.AxisListType.X,
                apply_absolute_value=True)))(), waits=[ld])
            rcp = p.do("dve", (lambda icx=icx, sl=sl: (lambda e: e.reciprocal(
                out=tmp1[:, sl:sl + 1], in_=swcol[:, icx:icx + 1])))(), waits=[rm])
            p.do("dve", (lambda sl=sl: (lambda e: e.tensor_scalar(
                out=wq[:, 1024 * sl:1024 * (sl + 1)],
                in0=wld[:, 1024 * sl:1024 * (sl + 1)],
                scalar1=tmp1[:, sl:sl + 1], scalar2=127.0, op0=ALU.mult,
                op1=ALU.mult)))(), waits=[rcp, ev_h[sl]])
            qv = p.do("dve", (lambda sl=sl: (lambda e: e.tensor_scalar(
                out=wq[:, 1024 * sl:1024 * (sl + 1)],
                in0=wq[:, 1024 * sl:1024 * (sl + 1)],
                scalar1=RND, scalar2=RND, op0=ALU.add, op1=ALU.subtract)))())
            q_h[sl] = qv
            for hc in range(DC):
                ps = hc % 2
                tr = p.do("pe", (lambda hc=hc, sl=sl, ps=ps: (lambda e: e.transpose(
                    out=psv[ps][:, 0:128],
                    in_=wq[:, 1024 * sl + 128 * hc:1024 * sl + 128 * (hc + 1)],
                    identity=idb[:])))(), waits=[qv, ldi, tprev[ps]])
                tprev[ps] = p.do("act", (lambda hc=hc, icx=icx, ps=ps: (lambda e: e.activation(
                    out=dstT[:, hc, 128 * icx:128 * (icx + 1)],
                    in_=psv[ps][:, 0:128],
                    func=AF.Copy)))(), waits=[tr])
            ev_h[sl] = tprev[(DC - 1) % 2]
            ret = p.ev("act")
        return ret

    qg = quant_w(w_gate, wgT, swg, None)
    qu = quant_w(w_up, wuT, swu, qg)
    cg = p.do("dve", lambda e: e.tensor_scalar(out=swg[:], in0=swg[:],
              scalar1=colb[:, 1:2], scalar2=1.0 / 127.0, op0=ALU.mult,
              op1=ALU.mult), waits=[qg, sxc])
    cu = p.do("dve", lambda e: e.tensor_scalar(out=swu[:], in0=swu[:],
              scalar1=colb[:, 1:2], scalar2=1.0 / 127.0, op0=ALU.mult,
              op1=ALU.mult), waits=[qu, sxc])

    p.barrier()

    mx_ev = None
    for it in range(NT):
        t0 = it * TT
        p.barrier()
        ldh = p.do("sync", (lambda t0=t0: DMA(
            h2t[:], h2T.rearrange("(c p) t -> p c t", p=128)[:, :, t0:t0 + TT]))(),
            inc=16)
        p.do("dve", lambda e: e.tensor_scalar(
            out=h2t[:].rearrange("p c l -> p (c l)"),
            in0=h2t[:].rearrange("p c l -> p (c l)"), scalar1=colb[:, 2:3],
            scalar2=None, op0=ALU.mult), waits=[ldh, rsc])
        p.do("dve", lambda e: e.tensor_scalar(
            out=h2t[:].rearrange("p c l -> p (c l)"),
            in0=h2t[:].rearrange("p c l -> p (c l)"), scalar1=RND, scalar2=RND,
            op0=ALU.add, op1=ALU.subtract))
        q3 = p.do("dve", lambda e: e.tensor_scalar(
            out=xq[:].rearrange("p c l -> p (c l)"),
            in0=h2t[:].rearrange("p c l -> p (c l)"), scalar1=127.0,
            scalar2=-128.0, op0=ALU.min, op1=ALU.max))

        evac0 = None
        evac1 = None
        for m in range(ICF):
            for k in range(DC):
                g1 = p.do("pe", (lambda m=m, k=k: (lambda e: e.matmul(
                    out=ps0[:, 0:TT], lhsT=wgT[:, k, 128 * m:128 * (m + 1)],
                    rhs=xq[:, k, :], start=(k == 0), stop=(k == DC - 1))))(),
                    waits=[q3, cg, evac0] if k == 0 else [])
            ge = p.do("act", (lambda m=m: (lambda e: e.activation(
                out=gsil[:], in_=ps0[:, 0:TT], func=AF.Silu,
                scale=swg[:, m:m + 1])))(), waits=[g1])
            evac0 = ge
            for k in range(DC):
                u1 = p.do("pe", (lambda m=m, k=k: (lambda e: e.matmul(
                    out=ps1[:, 0:TT], lhsT=wuT[:, k, 128 * m:128 * (m + 1)],
                    rhs=xq[:, k, :], start=(k == 0), stop=(k == DC - 1))))(),
                    waits=[cu, evac1] if k == 0 else [])
            ue = p.do("act", (lambda m=m: (lambda e: e.activation(
                out=upsc[:], in_=ps1[:, 0:TT], func=AF.Copy,
                scale=swu[:, m:m + 1])))(), waits=[u1])
            evac1 = ue
            gpv = p.do("dve", (lambda m=m: (lambda e: e.tensor_tensor(
                out=gp[:, m, :], in0=gsil[:], in1=upsc[:], op=ALU.mult)))(),
                waits=[ge, ue])
            p._pending["act"].append(gpv)
            rdev = p.do("dve", (lambda m=m: (lambda e: e.reduce_max(
                out=tmp1[:, 1:2], in_=gp[:, m, :], axis=mybir.AxisListType.X,
                apply_absolute_value=True)))(), waits=[mx_ev])
            mx_ev = p.do("dve", lambda e: e.tensor_tensor(
                out=mxcl[:], in0=mxcl[:], in1=tmp1[:, 1:2], op=ALU.max),
                waits=[rdev])
        p.do("sync", (lambda t0=t0: DMA(
            o_gp.rearrange("(c p) t -> p c t", p=128)[:, :, t0:t0 + TT], gp[:]))(),
            waits=[mx_ev], inc=16)

    trm = p.do("pe", lambda e: e.transpose(out=psr[0:1, 0:128], in_=mxcl[:],
               identity=idf2[:]), waits=[mx_ev, ic_ev])
    fm = p.do("dve", lambda e: e.reduce_max(out=rowb[0:1, 12:13],
              in_=psr[0:1, 0:128], axis=mybir.AxisListType.X), waits=[trm])
    p.do("sync", DMA(o_maxg, rowb[0:1, 12:13]), waits=[fm], inc=16)
    p.do("dve", lambda e: e.memset(tmp1[:, 2:3], 0.0), waits=[p.ev("sync")])

    _emit_all(nc, p)
    return nc



def build_l3(TT, NT):
    nc = bass.Bass("TRN2", target_bir_lowering=False)
    gpT = nc.dram_tensor("gpT", [I_FFN, T], F32, kind="ExternalInput").ap()
    xmidT = nc.dram_tensor("xmidT", [D, T], F32, kind="ExternalInput").ap()
    mx8 = nc.dram_tensor("mx8", [1, 8], F32, kind="ExternalInput").ap()
    clipd = nc.dram_tensor("clipd", [1, 1], F32, kind="ExternalInput").ap()
    w_down = nc.dram_tensor("w_down", [D, I_FFN], F32, kind="ExternalInput").ap()
    gb_col = nc.dram_tensor("gb_col", [128, DC], F32, kind="ExternalInput").ap()
    identb = nc.dram_tensor("identb", [128, 128], BF16, kind="ExternalInput").ap()
    o_out = nc.dram_tensor("o_out", [D, T], F32, kind="ExternalOutput").ap()

    a = nc.alloc_sbuf_tensor
    wdT = a("wdT", [128, ICF, D], BF16)
    swd = a("swd", [128, DC], F32)
    wld = a("wld", [128, I_FFN], F32)
    wq = a("wq", [128, I_FFN], BF16)
    gpt = a("gpt", [128, ICF, TT], F32)
    hq = a("hq", [128, ICF, TT], BF16)
    xmt = a("xmt", [128, DC, TT], F32)
    outt = a("outt", [128, DC, TT], F32)
    rowb = a("rowb", [1, 128], F32)
    colb = a("colb", [128, 4], F32)
    gbc = a("gbc", [128, DC], F32)
    comb = a("comb", [128, DC], F32)
    csts = a("csts", [128, 1], F32)
    nc.const_aps.register(F32, 0.5, csts[:, 0:1])
    tmp1 = a("tmp1", [128, TT], F32)
    idb = a("idb", [128, 128], BF16)
    onesr = a("onesr", [1, 128], F32)

    ps0 = nc.alloc_psum_tensor("ps0", [128, 512], F32)
    ps1 = nc.alloc_psum_tensor("ps1", [128, 512], F32)
    psb = nc.alloc_psum_tensor("psb", [128, 512], F32)
    psr = nc.alloc_psum_tensor("psr", [128, 512], F32)

    p = Prog()

    ld0 = p.do("sync", DMA(rowb[0:1, 0:8], mx8), inc=16)
    ld1 = p.do("sync", DMA(rowb[0:1, 8:9], clipd), inc=16)
    ldi = p.do("sync", DMA(idb[:], identb), inc=16)
    ldg = p.do("sync", DMA(gbc[:], gb_col), inc=16)
    p.do("dve", lambda e: e.memset(onesr[:], 1.0))
    p.do("dve", lambda e: e.memset(csts[:, 0:1], 0.5))
    ones_ev = p.ev("dve")
    p._pending["act"].append(ones_ev)

    gm = p.do("dve", lambda e: e.reduce_max(out=rowb[0:1, 9:10],
              in_=rowb[0:1, 0:8], axis=mybir.AxisListType.X), waits=[ld0])
    th = p.do("act", lambda e: e.activation(out=rowb[0:1, 10:11],
              in_=rowb[0:1, 8:9], func=AF.Tanh, scale=0.5), waits=[ld1])
    sg = p.do("dve", lambda e: e.tensor_scalar(out=rowb[0:1, 10:11],
              in0=rowb[0:1, 10:11], scalar1=0.5, scalar2=0.5, op0=ALU.mult,
              op1=ALU.add), waits=[th])
    pm = p.do("dve", lambda e: e.tensor_tensor(out=rowb[0:1, 11:12],
              in0=rowb[0:1, 9:10], in1=rowb[0:1, 10:11], op=ALU.mult),
              waits=[gm, sg])
    bcp = p.do("pe", lambda e: e.matmul(out=psr[:, 0:1], lhsT=onesr[:],
               rhs=rowb[0:1, 11:12], start=True, stop=True), waits=[pm, ones_ev])
    pc0 = p.do("dve", lambda e: e.tensor_copy(out=colb[:, 0:1], in_=psr[:, 0:1]),
               waits=[bcp])
    sxc = p.do("dve", lambda e: e.tensor_scalar(out=colb[:, 1:2], in0=colb[:, 0:1],
               scalar1=1.0 / 127.0, scalar2=None, op0=ALU.mult), waits=[pc0])
    rc0 = p.do("dve", lambda e: e.reciprocal(out=colb[:, 2:3], in_=colb[:, 0:1]),
               waits=[pc0])
    rsc = p.do("dve", lambda e: e.tensor_scalar(out=colb[:, 2:3], in0=colb[:, 2:3],
               scalar1=127.0, scalar2=None, op0=ALU.mult), waits=[rc0])

    prev = None
    for hc in range(DC):
        ld = p.do("sync", (lambda hc=hc: DMA(
            wld[:], w_down[128 * hc:128 * (hc + 1), :]))(), waits=[prev], inc=16)
        rm = p.do("dve", (lambda hc=hc: (lambda e: e.reduce_max(
            out=swd[:, hc:hc + 1], in_=wld[:], axis=mybir.AxisListType.X,
            apply_absolute_value=True)))(), waits=[ld])
        rcp = p.do("dve", (lambda hc=hc: (lambda e: e.reciprocal(
            out=tmp1[:, 0:1], in_=swd[:, hc:hc + 1])))(), waits=[rm])
        p.do("dve", lambda e: e.tensor_scalar(out=wq[:], in0=wld[:],
             scalar1=tmp1[:, 0:1], scalar2=127.0, op0=ALU.mult, op1=ALU.mult),
             waits=[rcp])
        qv = p.do("dve", lambda e: e.tensor_scalar(out=wq[:], in0=wq[:],
                  scalar1=RND, scalar2=RND, op0=ALU.add, op1=ALU.subtract))
        tprev = None
        for icx in range(ICF):
            tr = p.do("pe", (lambda icx=icx: (lambda e: e.transpose(
                out=psb[:].bitcast(BF16)[:, 0:128],
                in_=wq[:, 128 * icx:128 * (icx + 1)],
                identity=idb[:])))(), waits=[qv, ldi, tprev])
            tprev = p.do("act", (lambda icx=icx, hc=hc: (lambda e: e.activation(
                out=wdT[:, icx, 128 * hc:128 * (hc + 1)],
                in_=psb[:].bitcast(BF16)[:, 0:128],
                func=AF.Copy)))(), waits=[tr])
        prev = tprev
    cmb = p.do("dve", lambda e: e.tensor_scalar(out=comb[:], in0=swd[:],
               scalar1=colb[:, 1:2], scalar2=1.0 / 127.0, op0=ALU.mult,
               op1=ALU.mult), waits=[prev, sxc])

    p.barrier()

    for it in range(NT):
        t0 = it * TT
        p.barrier()
        ldg2 = p.do("sync", (lambda t0=t0: DMA(
            gpt[:], gpT.rearrange("(c p) t -> p c t", p=128)[:, :, t0:t0 + TT]))(),
            inc=16)
        ldx = p.do("sync", (lambda t0=t0: DMA(
            xmt[:], xmidT.rearrange("(c p) t -> p c t", p=128)[:, :, t0:t0 + TT]))(),
            inc=16)
        p.do("dve", lambda e: e.tensor_scalar(
            out=gpt[:].rearrange("p c l -> p (c l)"),
            in0=gpt[:].rearrange("p c l -> p (c l)"), scalar1=colb[:, 2:3],
            scalar2=None, op0=ALU.mult), waits=[ldg2, rsc])
        p.do("dve", lambda e: e.tensor_scalar(
            out=gpt[:].rearrange("p c l -> p (c l)"),
            in0=gpt[:].rearrange("p c l -> p (c l)"), scalar1=RND, scalar2=RND,
            op0=ALU.add, op1=ALU.subtract))
        q3 = p.do("dve", lambda e: e.tensor_scalar(
            out=hq[:].rearrange("p c l -> p (c l)"),
            in0=gpt[:].rearrange("p c l -> p (c l)"), scalar1=127.0,
            scalar2=-128.0, op0=ALU.min, op1=ALU.max))

        evac = [None, None]
        out_done = None
        for m in range(DC):
            bank = m % 2
            pst = ps0 if bank == 0 else ps1
            for k in range(ICF):
                mm = p.do("pe", (lambda m=m, k=k, pst=pst: (lambda e: e.matmul(
                    out=pst[:, 0:TT], lhsT=wdT[:, k, 128 * m:128 * (m + 1)],
                    rhs=hq[:, k, :], start=(k == 0), stop=(k == ICF - 1))))(),
                    waits=[q3, cmb, evac[bank]] if k == 0 else [])
            p.do("dve", (lambda m=m, pst=pst: (lambda e: e.tensor_scalar(
                out=tmp1[:], in0=pst[:, 0:TT], scalar1=comb[:, m:m + 1],
                scalar2=None, op0=ALU.mult)))(), waits=[mm])
            out_done = p.do("dve", (lambda m=m: (lambda e: e.scalar_tensor_tensor(
                out=outt[:, m, :], in0=tmp1[:], scalar=gbc[:, m:m + 1],
                in1=xmt[:, m, :], op0=ALU.mult, op1=ALU.add)))(),
                waits=[ldx, ldg])
            evac[bank] = out_done
        p.do("sync", (lambda t0=t0: DMA(
            o_out.rearrange("(c p) t -> p c t", p=128)[:, :, t0:t0 + TT],
            outt[:]))(), waits=[out_done], inc=16)

    p.do("dve", lambda e: e.memset(tmp1[:, 2:3], 0.0), waits=[p.ev("sync")])

    _emit_all(nc, p)
    return nc


_CACHE = {}


def _programs():
    if "l1" not in _CACHE:
        _CACHE["l1"] = build_l1(256, 8)
        _CACHE["l2"] = build_l2(512, 4)
        _CACHE["l3"] = build_l3(256, 8)
    return _CACHE["l1"], _CACHE["l2"], _CACHE["l3"]


def _colize(v, chunks):
    return np.ascontiguousarray(np.asarray(v, np.float32).reshape(chunks, 128).T)


def kernel(L0_attn_residual, L0_attn_norm_w, L0_attn_W_in, L0_attn_conv_w,
           L0_attn_conv_b, L0_attn_W_x, L0_attn_W_dt, L0_attn_b_dt,
           L0_attn_A_log, L0_attn_D, L0_attn_W_out, L0_ffn_clip_x,
           L0_ffn_clip_down, L0_ffn_w_gate, L0_ffn_w_up, L0_ffn_w_down,
           L0_c, L0_wmod):
    x = np.asarray(L0_attn_residual, np.float32)
    l1, l2, l3 = _programs()

    onehot = np.zeros((32, 32 * 128), BF)
    for n in range(32):
        onehot[n, 128 * n:128 * (n + 1)] = 1.0
    identf = np.eye(128, dtype=np.float32)
    identb = np.eye(128, dtype=BF)

    normw_col = _colize(L0_attn_norm_w, DC)
    convb_col = _colize(L0_attn_conv_b, IC)
    bdt_col = _colize(L0_attn_b_dt, IC)
    dvec_col = _colize(L0_attn_D, IC)
    cw = np.asarray(L0_attn_conv_w, np.float32).reshape(IC, 128, 4)
    convw_col = np.ascontiguousarray(cw.transpose(1, 0, 2).reshape(128, IC * 4))
    al = np.asarray(L0_attn_A_log, np.float32).reshape(IC, 128, NS)
    alog_col = np.ascontiguousarray(al.transpose(1, 0, 2).reshape(128, IC * NS))

    in1 = [dict(
        xT=np.ascontiguousarray(x[b].T),
        c_col=np.ascontiguousarray(np.asarray(L0_c, np.float32)[b].reshape(DC, 128).T),
        wmod=np.asarray(L0_wmod, np.float32),
        w_in=np.asarray(L0_attn_W_in, np.float32),
        w_x=np.asarray(L0_attn_W_x, np.float32),
        w_dt=np.asarray(L0_attn_W_dt, np.float32),
        w_out=np.asarray(L0_attn_W_out, np.float32),
        alog_col=alog_col, normw_col=normw_col, convw_col=convw_col,
        convb_col=convb_col, bdt_col=bdt_col, dvec_col=dvec_col,
        onehot=onehot, identf=identf,
    ) for b in range(NCORES)]
    r1 = run_bass_kernel_spmd(l1, in1, core_ids=list(range(NCORES)))

    mx8 = np.ascontiguousarray(np.concatenate(
        [r1.results[b]["o_max"].reshape(1, 1) for b in range(NCORES)], axis=1))
    in2 = [dict(
        h2T=r1.results[b]["o_h2"], mx8=mx8,
        clipx=np.asarray(L0_ffn_clip_x, np.float32).reshape(1, 1),
        w_gate=np.asarray(L0_ffn_w_gate, np.float32),
        w_up=np.asarray(L0_ffn_w_up, np.float32),
        identb=identb,
    ) for b in range(NCORES)]
    r2 = run_bass_kernel_spmd(l2, in2, core_ids=list(range(NCORES)))

    mxg8 = np.ascontiguousarray(np.concatenate(
        [r2.results[b]["o_maxg"].reshape(1, 1) for b in range(NCORES)], axis=1))
    in3 = [dict(
        gpT=r2.results[b]["o_gp"], xmidT=r1.results[b]["o_xmid"], mx8=mxg8,
        clipd=np.asarray(L0_ffn_clip_down, np.float32).reshape(1, 1),
        w_down=np.asarray(L0_ffn_w_down, np.float32),
        gb_col=r1.results[b]["o_gb"], identb=identb,
    ) for b in range(NCORES)]
    r3 = run_bass_kernel_spmd(l3, in3, core_ids=list(range(NCORES)))

    return np.stack([np.ascontiguousarray(r3.results[b]["o_out"].T)
                     for b in range(NCORES)], axis=0)


# version 09
# speedup vs baseline: 2.2190x; 1.0364x over previous
import numpy as np
import ml_dtypes
import concourse.bass as bass
import concourse.mybir as mybir
from concourse.bass_utils import run_bass_kernel_spmd

F32 = mybir.dt.float32
BF16 = mybir.dt.bfloat16
AF = mybir.ActivationFunctionType
ALU = mybir.AluOpType

NCORES = 8
D = 1024
DC = 8
DI = 2048
IC = 16
NS = 16
DTR = 64
T = 2048
I_FFN = 2816
ICF = 22
EPS = 1e-5
RND = 12582912.0
BF = ml_dtypes.bfloat16

ENGS = ("sync", "pe", "act", "dve", "pool")
DBG = False


class Prog:

    def __init__(self):
        self.ops = {e: [] for e in ENGS}
        self.cnt = {e: 0 for e in ENGS}
        self._pending = {e: [] for e in ENGS}

    def do(self, eng, fn, waits=(), inc=1):
        self.cnt[eng] += inc
        w = tuple(self._pending[eng]) + tuple(x for x in waits if x is not None)
        self._pending[eng] = []
        self.ops[eng].append((fn, w, inc))
        return (eng, self.cnt[eng])

    def ev(self, eng):
        return (eng, self.cnt[eng])

    def barrier(self):
        evs = [(e, self.cnt[e]) for e in ENGS if self.cnt[e] > 0]
        for e in ENGS:
            self._pending[e].extend(evs)

    def emit(self, block, sems):
        def mk(eng_name):
            def body(eng_obj):
                for fn, waits, inc in self.ops[eng_name]:
                    best = {}
                    for (en, c) in waits:
                        if c > best.get(en, 0):
                            best[en] = c
                    for en, c in best.items():
                        eng_obj.wait_ge(sems[en], c)
                    fn(eng_obj).then_inc(sems[eng_name], inc)
            return body

        block.sync(mk("sync"))
        block.tensor(mk("pe"))
        block.scalar(mk("act"))
        block.vector(mk("dve"))
        block.gpsimd(mk("pool"))


def _emit_all(nc, p):
    with (
        nc.semaphore("s_sync") as s_sync,
        nc.semaphore("s_pe") as s_pe,
        nc.semaphore("s_act") as s_act,
        nc.semaphore("s_dve") as s_dve,
        nc.semaphore("s_pool") as s_pool,
        nc.Block() as block,
    ):
        sems = dict(sync=s_sync, pe=s_pe, act=s_act, dve=s_dve, pool=s_pool)
        p.emit(block, sems)


def DMA(out, in_):
    return lambda e: e.dma_start(out=out, in_=in_)



def build_l1(TT, NT):
    nc = bass.Bass("TRN2", target_bir_lowering=False)
    xT = nc.dram_tensor("xT", [D, T], F32, kind="ExternalInput").ap()
    c_col = nc.dram_tensor("c_col", [128, DC], F32, kind="ExternalInput").ap()
    wmod = nc.dram_tensor("wmod", [D, 6 * D], F32, kind="ExternalInput").ap()
    w_in = nc.dram_tensor("w_in", [D, 2 * DI], F32, kind="ExternalInput").ap()
    w_x = nc.dram_tensor("w_x", [DI, 96], F32, kind="ExternalInput").ap()
    w_dt = nc.dram_tensor("w_dt", [DTR, DI], F32, kind="ExternalInput").ap()
    w_out = nc.dram_tensor("w_out", [DI, D], F32, kind="ExternalInput").ap()
    alog_col = nc.dram_tensor("alog_col", [128, IC * NS], F32, kind="ExternalInput").ap()
    normw_col = nc.dram_tensor("normw_col", [128, DC], F32, kind="ExternalInput").ap()
    convw_col = nc.dram_tensor("convw_col", [128, IC * 4], F32, kind="ExternalInput").ap()
    convb_col = nc.dram_tensor("convb_col", [128, IC], F32, kind="ExternalInput").ap()
    bdt_col = nc.dram_tensor("bdt_col", [128, IC], F32, kind="ExternalInput").ap()
    dvec_col = nc.dram_tensor("dvec_col", [128, IC], F32, kind="ExternalInput").ap()
    onehot = nc.dram_tensor("onehot", [32, 32 * 128], BF16, kind="ExternalInput").ap()
    identf = nc.dram_tensor("identf", [128, 128], F32, kind="ExternalInput").ap()
    o_xmid = nc.dram_tensor("o_xmid", [D, T], F32, kind="ExternalOutput").ap()
    o_h2 = nc.dram_tensor("o_h2", [D, T], F32, kind="ExternalOutput").ap()
    o_max = nc.dram_tensor("o_max", [1, 1], F32, kind="ExternalOutput").ap()
    o_gb = nc.dram_tensor("o_gb", [128, DC], F32, kind="ExternalOutput").ap()
    o_mc = nc.dram_tensor("o_mc", [128, 48], F32, kind="ExternalOutput").ap()
    if DBG:
        o_hid = nc.dram_tensor("o_hid", [128, DC * TT], BF16, kind="ExternalOutput").ap()
        o_xs = nc.dram_tensor("o_xs", [128, IC * TT], BF16, kind="ExternalOutput").ap()
        o_zb = nc.dram_tensor("o_zb", [128, IC * TT], BF16, kind="ExternalOutput").ap()
        o_dtb = nc.dram_tensor("o_dtb", [128, IC * TT], BF16, kind="ExternalOutput").ap()
        o_ya = nc.dram_tensor("o_ya", [128, IC * TT], BF16, kind="ExternalOutput").ap()
        o_ga2 = nc.dram_tensor("o_ga2", [128, DC], F32, kind="ExternalOutput").ap()
        o_ball = nc.dram_tensor("o_ball", [128, IC * (TT + 1)], BF16, kind="ExternalOutput").ap()
        o_hall = nc.dram_tensor("o_hall", [128, IC * (TT + 1)], BF16, kind="ExternalOutput").ap()
        o_bcsb = nc.dram_tensor("o_bcsb", [128, 2 * TT], BF16, kind="ExternalOutput").ap()
        o_cdbg = nc.dram_tensor("o_cdbg", [128, 6 * DC], F32, kind="ExternalOutput").ap()
        o_ga_pre = nc.dram_tensor("o_ga_pre", [128, DC], F32, kind="ExternalOutput").ap()
        o_acl = nc.dram_tensor("o_acl", [128, IC * NS], F32, kind="ExternalOutput").ap()
    wout_bf = nc.dram_tensor("wout_bf", [IC, 128, D], BF16, kind="Internal").ap()

    a = nc.alloc_sbuf_tensor
    x_t = a("x_t", [128, DC, TT], F32)
    h1 = a("h1", [128, DC, TT], F32)
    h1b = a("h1b", [128, DC, TT], BF16)
    hid = a("hid", [128, DC, TT], BF16)
    x_raw = a("x_raw", [128, IC, TT + 3], BF16)
    xs_bf = a("xs_bf", [128, IC, TT], BF16)
    z_bf = a("z_bf", [128, IC, TT], BF16)
    dt_bf = a("dt_bf", [128, IC, TT], BF16)
    u_bf = a("u_bf", [128, IC, TT], BF16)
    xdbl = a("xdbl", [96, TT], BF16)
    bcr = a("bcr", [32, TT], BF16)
    dtA = a("dtA", [128, IC, TT], BF16)
    a_all = a("a_all", [128, IC, TT + 1], F32)
    b_all = a("b_all", [128, IC, TT + 1], BF16)
    h_all = a("h_all", [128, IC, TT + 1], BF16)
    y_acc = a("y_acc", [128, IC, TT], BF16)
    bc_sb = a("bc_sb", [128, 2, TT], BF16)
    state = a("state", [128, IC, NS], F32)
    winb = a("winb", [128, DC, 2 * DI], BF16)
    woutb = a("woutb", [128, 4, D], BF16)
    wxb = a("wxb", [128, IC, 96], BF16)
    wdtb = a("wdtb", [64, DI], BF16)
    acol = a("acol", [128, IC * NS], F32)
    mcols = a("mcols", [128, 48], F32)
    gacl = a("gacl", [128, DC], F32)
    gbcl = a("gbcl", [128, DC], F32)
    ca1 = a("ca1", [128, DC], F32)
    cb1 = a("cb1", [128, DC], F32)
    nwc = a("nwc", [128, DC], F32)
    cwc = a("cwc", [128, IC * 4], F32)
    cbc = a("cbc", [128, IC], F32)
    bdc = a("bdc", [128, IC], F32)
    dvc = a("dvc", [128, IC], F32)
    ccol = a("ccol", [128, DC], F32)
    idf = a("idf", [128, 128], F32)
    ones_c = a("ones_c", [128, 1], F32)
    ones_r = a("ones_r", [1, 128], F32)
    rows = a("rows", [1, 2 * TT + 16], F32)
    mxc = a("mxc", [128, 1], F32)
    tmpc = a("tmpc", [128, TT], F32)
    cvac = a("cvac", [128, TT], F32)
    csts = a("csts", [128, 2], F32)
    cdbg = a("cdbg", [128, 6 * DC], F32) if DBG else None
    nc.const_aps.register(F32, -1.0, csts[:, 0:1])
    nc.const_aps.register(F32, -0.5, csts[:, 1:2])

    ohot = woutb[:].rearrange("p a b -> p (a b)")[0:32, 0:32 * 128]

    ps_w0 = nc.alloc_psum_tensor("ps_w0", [128, 512], F32)
    ps_w1 = nc.alloc_psum_tensor("ps_w1", [128, 512], F32)
    ps_w2 = nc.alloc_psum_tensor("ps_w2", [128, 512], F32)
    ps_bc = nc.alloc_psum_tensor("ps_bc", [128, 512], F32)
    ps_row = nc.alloc_psum_tensor("ps_row", [1, 512], F32)
    ps_b2 = nc.alloc_psum_tensor("ps_b2", [128, 512], F32)
    ps_m = nc.alloc_psum_tensor("ps_m", [128, 128], F32)
    psel = [ps_w0, ps_w1, ps_w2]

    p = Prog()

    for tile, src in [(ccol, c_col), (acol, alog_col), (nwc, normw_col),
                      (cwc, convw_col), (cbc, convb_col), (bdc, bdt_col),
                      (dvc, dvec_col), (idf, identf)]:
        p.do("sync", DMA(tile[:], src), inc=16)
    ld_consts = p.ev("sync")

    p.do("dve", lambda e: e.memset(ones_c[:], 1.0))
    p.do("dve", lambda e: e.memset(ones_r[:], 1.0))
    p.do("dve", lambda e: e.memset(state[:].rearrange("p i n -> p (i n)"), 0.0))
    p.do("dve", lambda e: e.memset(mxc[:], 0.0))
    p.do("dve", lambda e: e.memset(x_raw[:, :, 0:3], 0.0))
    p.do("dve", lambda e: e.memset(csts[:, 0:1], -1.0))
    p.do("dve", lambda e: e.memset(csts[:, 1:2], -0.5))
    init_dve = p.ev("dve")
    p._pending["act"].append(init_dve)

    silu_c = p.do("act", lambda e: e.activation(out=ccol[:], in_=ccol[:],
                  func=AF.Silu), waits=[ld_consts])
    a_exp = p.do("act", lambda e: e.activation(out=acol[:], in_=acol[:],
                 func=AF.Exp), waits=[ld_consts])
    a_neg = p.do("dve", lambda e: e.tensor_scalar(out=acol[:], in0=acol[:],
                 scalar1=-1.0, scalar2=None, op0=ALU.mult), waits=[a_exp])

    af32 = a_all[:].rearrange("p i l -> p (i l)")
    last_mm = None
    mm_hist = [None, None]
    first = True
    ci = 0
    for k in range(DC):
        for qq in range(4):
            half = ci % 2
            off = 1536 * half
            ld = p.do("sync", (lambda k=k, qq=qq, off=off: DMA(
                af32[:, off:off + 1536],
                wmod[128 * k:128 * (k + 1), 1536 * qq:1536 * (qq + 1)]))(),
                waits=[silu_c, mm_hist[half]], inc=16)
            for m in range(12):
                mcol = 12 * qq + m
                last_mm = p.do("pe", (lambda m=m, mcol=mcol, k=k, off=off,
                               first=first: (lambda e: e.matmul(
                    out=ps_m[:, mcol:mcol + 1],
                    lhsT=af32[:, off + 128 * m:off + 128 * (m + 1)],
                    rhs=ccol[:, k:k + 1],
                    start=first, stop=(k == DC - 1))))(),
                    waits=[ld] if m == 0 else [])
                first = False
            mm_hist[half] = last_mm
            ci += 1
    mod_cols = p.do("dve", lambda e: e.tensor_copy(out=mcols[:], in_=ps_m[:, 0:48]),
                    waits=[last_mm])
    p.do("dve", lambda e: e.tensor_scalar(out=ca1[:], in0=mcols[:, 8:16],
         scalar1=1.0, scalar2=None, op0=ALU.add), waits=[mod_cols])
    p.do("dve", lambda e: e.tensor_scalar(out=cb1[:], in0=mcols[:, 32:40],
         scalar1=1.0, scalar2=None, op0=ALU.add), waits=[mod_cols])

    def clamp_g(dst, lo, prev=None):
        src = mcols[:, lo:lo + DC]
        sg0 = p.do("act", lambda e: e.activation(out=tmpc[:, 0:DC], in_=src,
                   func=AF.Sign), waits=[mod_cols, prev])
        if DBG and lo == 16:
            p.do("dve", lambda e: e.tensor_copy(out=cdbg[:, 0:DC],
                 in_=tmpc[:, 0:DC]), waits=[sg0])
        e1 = p.do("dve", lambda e: e.tensor_tensor(out=tmpc[:, DC:2 * DC],
             in0=tmpc[:, 0:DC], in1=tmpc[:, 0:DC], op=ALU.mult), waits=[sg0])
        e2 = p.do("dve", lambda e: e.tensor_scalar(out=tmpc[:, DC:2 * DC],
             in0=tmpc[:, DC:2 * DC], scalar1=-1.0, scalar2=1.0, op0=ALU.mult,
             op1=ALU.add), waits=[e1])
        e3 = p.do("dve", lambda e: e.tensor_tensor(out=tmpc[:, 0:DC],
             in0=tmpc[:, 0:DC], in1=tmpc[:, DC:2 * DC], op=ALU.add), waits=[e2])
        if DBG and lo == 16:
            p.do("dve", lambda e: e.tensor_copy(out=cdbg[:, DC:2 * DC],
                 in_=tmpc[:, 0:DC]))
        ab = p.do("act", lambda e: e.activation(out=dst, in_=src, func=AF.Abs),
                  waits=[mod_cols, sg0])
        if DBG and lo == 16:
            p.do("dve", lambda e: e.tensor_copy(out=cdbg[:, 2 * DC:3 * DC],
                 in_=dst), waits=[ab])
        e4 = p.do("dve", lambda e: e.tensor_scalar(out=dst, in0=dst, scalar1=0.25,
             scalar2=None, op0=ALU.max), waits=[ab])
        if DBG and lo == 16:
            p.do("dve", lambda e: e.tensor_copy(out=cdbg[:, 3 * DC:4 * DC],
                 in_=dst), waits=[e4])
        ret = p.do("dve", lambda e: e.tensor_tensor(out=dst, in0=dst,
                   in1=tmpc[:, 0:DC], op=ALU.mult), waits=[e3, e4])
        if DBG and lo == 16:
            p.do("dve", lambda e: e.tensor_copy(out=cdbg[:, 4 * DC:5 * DC],
                 in_=dst))
            p.do("dve", lambda e: e.tensor_copy(out=cdbg[:, 5 * DC:6 * DC],
                 in_=mcols[:, 16:24]))
        return ret

    ga_done = clamp_g(gacl[:], 16)
    gb_done = clamp_g(gbcl[:], 40, prev=ga_done)
    if DBG:
        p.do("sync", DMA(o_ga_pre, gacl[:]), waits=[gb_done], inc=16)
    p.do("sync", DMA(o_gb, gbcl[:]), waits=[gb_done], inc=16)
    p.do("sync", DMA(o_mc, mcols[:]), waits=[gb_done], inc=16)

    dtAf = dtA[:].rearrange("p i l -> p (i l)")
    cst_hist = [last_mm, last_mm]
    ci = 0
    for k in range(DC):
        for hh in range(2):
            half = ci % 2
            off = 2048 * half
            ld = p.do("sync", (lambda k=k, hh=hh, off=off: DMA(
                af32[:, off:off + 2048], w_in[128 * k:128 * (k + 1),
                                              2048 * hh:2048 * (hh + 1)]))(),
                waits=[cst_hist[half]], inc=16)
            cst_hist[half] = p.do("dve", (lambda k=k, hh=hh, off=off: (lambda e: e.tensor_copy(
                out=winb[:, k, 2048 * hh:2048 * (hh + 1)],
                in_=af32[:, off:off + 2048])))(), waits=[ld])
            ci += 1
    cst_prev = cst_hist[(ci + 1) % 2]
    cst_prev = cst_hist[ci % 2] if cst_hist[ci % 2] else cst_prev
    st_prev = None
    for k in range(IC):
        ld = p.do("sync", (lambda k=k: DMA(
            af32[:, 0:D], w_out[128 * k:128 * (k + 1), :]))(),
            waits=[cst_prev, st_prev], inc=16)
        cst_prev = p.do("dve", (lambda k=k: (lambda e: e.tensor_copy(
            out=dtAf[:, 0:D], in_=af32[:, 0:D])))(), waits=[ld, st_prev])
        st_prev = p.do("sync", (lambda k=k: DMA(
            wout_bf[k], dtAf[:, 0:D]))(), waits=[cst_prev], inc=16)
    for icx in range(IC):
        ld = p.do("sync", (lambda icx=icx: DMA(
            af32[:, 0:96], w_x[128 * icx:128 * (icx + 1), :]))(),
            waits=[cst_prev, st_prev], inc=16)
        st_prev = None
        cst_prev = p.do("dve", (lambda icx=icx: (lambda e: e.tensor_copy(
            out=wxb[:, icx, :], in_=af32[:, 0:96])))(), waits=[ld])
    ld = p.do("sync", DMA(af32[0:64, 0:DI], w_dt), waits=[cst_prev], inc=16)
    p.do("dve", lambda e: e.tensor_copy(out=wdtb[:], in_=af32[0:64, 0:DI]),
         waits=[ld])
    p.do("dve", lambda e: e.memset(af32, 0.0))

    p.barrier()

    for it in range(NT):
        t0 = it * TT
        p.barrier()
        ld_x = p.do("sync", (lambda t0=t0: DMA(
            x_t[:], xT.rearrange("(c p) t -> p c t", p=128)[:, :, t0:t0 + TT]))(),
            inc=16)
        oh_ld = p.do("sync", DMA(ohot, onehot), inc=16)
        sq = p.do("act", lambda e: e.activation(
            out=a_all[:, 0:DC, 1:1 + TT],
            in_=x_t[:].rearrange("p c l -> p (c l)"), func=AF.Square), waits=[ld_x])
        for dc in range(DC):
            m1 = p.do("pe", (lambda dc=dc: (lambda e: e.matmul(
                out=ps_row[0:1, 0:TT], lhsT=ones_c[:], rhs=x_t[:, dc, :],
                start=(dc == 0), stop=(dc == DC - 1))))(), waits=[ld_x])
        for dc in range(DC):
            m2 = p.do("pe", (lambda dc=dc: (lambda e: e.matmul(
                out=ps_row[0:1, TT:2 * TT], lhsT=ones_c[:], rhs=a_all[:, dc, 1:1 + TT],
                start=(dc == 0), stop=(dc == DC - 1))))(), waits=[sq])
        mean_r = rows[0:1, 0:TT]
        var_r = rows[0:1, TT:2 * TT]
        msq_r = tmpc[0:1, 0:TT]
        p.do("dve", lambda e: e.tensor_scalar(out=mean_r, in0=ps_row[0:1, 0:TT],
             scalar1=1.0 / D, scalar2=None, op0=ALU.mult), waits=[m1, m2])
        p.do("dve", lambda e: e.tensor_scalar(out=var_r, in0=ps_row[0:1, TT:2 * TT],
             scalar1=1.0 / D, scalar2=EPS, op0=ALU.mult, op1=ALU.add))
        p.do("dve", lambda e: e.tensor_tensor(out=msq_r, in0=mean_r, in1=mean_r,
             op=ALU.mult))
        sub_ev = p.do("dve", lambda e: e.tensor_tensor(out=var_r, in0=var_r,
                      in1=msq_r, op=ALU.subtract))
        p.do("act", lambda e: e.activation(out=var_r, in_=var_r, func=AF.Ln),
             waits=[sub_ev])
        rstd_ev = p.do("act", lambda e: e.activation(out=var_r, in_=var_r,
                       func=AF.Exp, scale=-0.5))
        bc1 = p.do("pe", lambda e: e.matmul(out=ps_b2[:, 0:TT], lhsT=ones_r[:],
                   rhs=mean_r, start=True, stop=True), waits=[sub_ev])
        bc2 = p.do("pe", lambda e: e.matmul(out=ps_b2[:, TT:2 * TT], lhsT=ones_r[:],
                   rhs=var_r, start=True, stop=True), waits=[rstd_ev])
        for dc in range(DC):
            p.do("dve", (lambda dc=dc: (lambda e: e.tensor_tensor(
                out=h1[:, dc, :], in0=x_t[:, dc, :], in1=ps_b2[:, 0:TT],
                op=ALU.subtract)))(), waits=[bc1, bc2] if dc == 0 else [])
            p.do("dve", (lambda dc=dc: (lambda e: e.tensor_tensor(
                out=h1[:, dc, :], in0=h1[:, dc, :], in1=ps_b2[:, TT:2 * TT],
                op=ALU.mult)))())
            h1_done = p.do("dve", (lambda dc=dc: (lambda e: e.tensor_scalar(
                out=h1[:, dc, :], in0=h1[:, dc, :], scalar1=ca1[:, dc:dc + 1],
                scalar2=mcols[:, dc:dc + 1], op0=ALU.mult, op1=ALU.add)))())
        h1b_ev = p.do("dve", lambda e: e.tensor_copy(
            out=h1b[:].rearrange("p c l -> p (c l)"),
            in_=h1[:].rearrange("p c l -> p (c l)")))

        sq2 = p.do("act", lambda e: e.activation(
            out=a_all[:, DC:2 * DC, 1:1 + TT],
            in_=h1[:].rearrange("p c l -> p (c l)"), func=AF.Square),
            waits=[h1_done])
        for dc in range(DC):
            m3 = p.do("pe", (lambda dc=dc: (lambda e: e.matmul(
                out=ps_row[0:1, 0:TT], lhsT=ones_c[:], rhs=a_all[:, DC + dc, 1:1 + TT],
                start=(dc == 0), stop=(dc == DC - 1))))(), waits=[sq2, bc2])
        mr = p.do("dve", lambda e: e.tensor_scalar(out=mean_r, in0=ps_row[0:1, 0:TT],
                  scalar1=1.0 / D, scalar2=EPS, op0=ALU.mult, op1=ALU.add),
                  waits=[m3])
        p.do("act", lambda e: e.activation(out=mean_r, in_=mean_r, func=AF.Ln),
             waits=[mr])
        r2 = p.do("act", lambda e: e.activation(out=mean_r, in_=mean_r, func=AF.Exp,
                  scale=-0.5))
        bc3 = p.do("pe", lambda e: e.matmul(out=ps_b2[:, 0:TT], lhsT=ones_r[:],
                   rhs=mean_r, start=True, stop=True), waits=[r2])
        for dc in range(DC):
            p.do("dve", (lambda dc=dc: (lambda e: e.tensor_tensor(
                out=tmpc[:], in0=h1[:, dc, :], in1=ps_b2[:, 0:TT],
                op=ALU.mult)))(), waits=[bc3] if dc == 0 else [])
            hid_done = p.do("dve", (lambda dc=dc: (lambda e: e.tensor_scalar(
                out=hid[:, dc, :], in0=tmpc[:], scalar1=nwc[:, dc:dc + 1],
                scalar2=None, op0=ALU.mult)))())

        if it > 0:
            tail_ev = p.do("dve", lambda e: e.tensor_copy(
                out=x_raw[:, :, 0:3], in_=x_raw[:, :, TT:TT + 3]))
        else:
            tail_ev = p.ev("dve")

        evac_ev = {0: None, 1: None, 2: None}
        xz_evs = []
        for m in range(32):
            bank = m % 3
            for k in range(DC):
                wd = [hid_done, evac_ev[bank]] if k == 0 else []
                mm_l = p.do("pe", (lambda m=m, k=k, bank=bank: (lambda e: e.matmul(
                    out=psel[bank][:, 0:TT],
                    lhsT=winb[:, k, 128 * m:128 * (m + 1)],
                    rhs=hid[:, k, :], start=(k == 0), stop=(k == DC - 1))))(),
                    waits=wd)
            if m < IC:
                ev = p.do("act", (lambda m=m, bank=bank: (lambda e: e.activation(
                    out=x_raw[:, m, 3:3 + TT], in_=psel[bank][:, 0:TT],
                    func=AF.Copy)))(), waits=[mm_l, tail_ev])
            else:
                ev = p.do("act", (lambda m=m, bank=bank: (lambda e: e.activation(
                    out=z_bf[:, m - IC, :], in_=psel[bank][:, 0:TT],
                    func=AF.Silu)))(), waits=[mm_l])
            evac_ev[bank] = ev
            xz_evs.append(ev)

        conv_evs = []
        for icx in range(IC):
            wd = [xz_evs[icx], tail_ev]
            if conv_evs:
                wd.append(conv_evs[-1])
            p.do("dve", (lambda icx=icx: (lambda e: e.tensor_scalar(
                out=cvac[:], in0=x_raw[:, icx, 0:TT],
                scalar1=cwc[:, 4 * icx:4 * icx + 1], scalar2=None,
                op0=ALU.mult)))(), waits=wd)
            for kk in range(1, 4):
                p.do("dve", (lambda icx=icx, kk=kk: (lambda e: e.scalar_tensor_tensor(
                    out=cvac[:], in0=x_raw[:, icx, kk:kk + TT],
                    scalar=cwc[:, 4 * icx + kk:4 * icx + kk + 1], in1=cvac[:],
                    op0=ALU.mult, op1=ALU.add)))())
            cv = p.ev("dve")
            conv_evs.append(p.do("act", (lambda icx=icx: (lambda e: e.activation(
                out=xs_bf[:, icx, :], in_=cvac[:], func=AF.Silu,
                bias=cbc[:, icx:icx + 1])))(), waits=[cv]))
        conv_done = conv_evs[-1]

        for icx in range(IC):
            mmx = p.do("pe", (lambda icx=icx: (lambda e: e.matmul(
                out=ps_w0[0:96, 0:TT], lhsT=wxb[:, icx, :], rhs=xs_bf[:, icx, :],
                start=(icx == 0), stop=(icx == IC - 1))))(),
                waits=[conv_evs[icx]] + ([evac_ev[0]] if icx == 0 else []))
        xdbl_ev = p.do("act", lambda e: e.activation(out=xdbl[:],
                       in_=ps_w0[0:96, 0:TT], func=AF.Copy), waits=[mmx])
        evac_ev[0] = xdbl_ev
        bcr_ev = p.do("act", lambda e: e.activation(out=bcr[:], in_=xdbl[64:96, :],
                      func=AF.Copy))

        dtp_last = None
        for m in range(IC):
            bank = 1 + (m % 2)
            mmd = p.do("pe", (lambda m=m, bank=bank: (lambda e: e.matmul(
                out=psel[bank][:, 0:TT], lhsT=wdtb[:, 128 * m:128 * (m + 1)],
                rhs=xdbl[0:64, :], start=True, stop=True)))(),
                waits=[xdbl_ev, evac_ev[bank]])
            dtp_last = p.do("dve", (lambda m=m, bank=bank: (lambda e: e.tensor_scalar(
                out=a_all[:, m, 1:1 + TT], in0=psel[bank][:, 0:TT],
                scalar1=bdc[:, m:m + 1], scalar2=None, op0=ALU.add)))(),
                waits=[mmd])
            evac_ev[bank] = dtp_last
        vv = a_all[:, :, 1:1 + TT]
        c1 = p.do("act", lambda e: e.activation(out=dtAf, in_=vv, func=AF.Abs),
                  waits=[dtp_last])
        p.do("act", lambda e: e.activation(out=dtAf, in_=dtAf, func=AF.Exp,
             scale=-1.0))
        c3 = p.do("act", lambda e: e.activation(out=dtAf, in_=dtAf, func=AF.Ln,
                  bias=1.0))
        p.do("dve", lambda e: e.tensor_scalar(
            out=dt_bf[:].rearrange("p i l -> p (i l)"), in0=vv, scalar1=0.0,
            scalar2=None, op0=ALU.max), waits=[c1])
        dt_done = p.do("dve", lambda e: e.tensor_tensor(
            out=dt_bf[:].rearrange("p i l -> p (i l)"),
            in0=dt_bf[:].rearrange("p i l -> p (i l)"), in1=dtAf, op=ALU.add),
            waits=[c3])
        u_done = p.do("dve", lambda e: e.tensor_tensor(
            out=u_bf[:].rearrange("p i l -> p (i l)"),
            in0=dt_bf[:].rearrange("p i l -> p (i l)"),
            in1=xs_bf[:].rearrange("p i l -> p (i l)"), op=ALU.mult))

        g_ev = None
        aH_prev = [None, None]
        stH_prev = [None, None]
        bce_prev = None
        sel2 = None
        for n in range(NS):
            seld = [bcr_ev, bce_prev, g_ev, oh_ld]
            p.do("pe", (lambda n=n: (lambda e: e.matmul(
                out=ps_bc[:, 0:TT], lhsT=ohot[:, 128 * n:128 * (n + 1)],
                rhs=bcr[:], start=True, stop=True)))(), waits=seld)
            sel2 = p.do("pe", (lambda n=n: (lambda e: e.matmul(
                out=ps_bc[:, TT:2 * TT], lhsT=ohot[:, 128 * (16 + n):128 * (17 + n)],
                rhs=bcr[:], start=True, stop=True)))())
            bce_prev = p.do("act", lambda e: e.activation(
                out=bc_sb[:].rearrange("p c l -> p (c l)"), in_=ps_bc[:, 0:2 * TT],
                func=AF.Copy), waits=[sel2])
            dtaH = [None, None]
            for H in (0, 1):
                for j in range(8):
                    icx = 8 * H + j
                    wd = []
                    if j == 0:
                        wd = ([dt_done, a_neg] if n == 0
                              else [aH_prev[H], g_ev])
                    dtaH[H] = p.do("dve", (lambda icx=icx, n=n: (lambda e: e.tensor_scalar(
                        out=dtA[:, icx, :], in0=dt_bf[:, icx, :],
                        scalar1=acol[:, NS * icx + n:NS * icx + n + 1],
                        scalar2=None, op0=ALU.mult)))(), waits=wd)
            for H in (0, 1):
                exp_w = [dtaH[H]]
                exp_w.append(stH_prev[H] if stH_prev[H] is not None else dt_done)
                aH_prev[H] = p.do("act", (lambda H=H: (lambda e: e.activation(
                    out=a_all[:, 8 * H:8 * H + 8, 1:1 + TT],
                    in_=dtA[:, 8 * H:8 * H + 8, :].rearrange("p i l -> p (i l)"),
                    func=AF.Exp)))(), waits=exp_w)
            p.do("pe", lambda e: e.matmul(out=ps_m[:, 0:64], lhsT=ohot[:, 0:128],
                 rhs=bcr[:, 0:64], start=True, stop=True), waits=[dtaH[0]])
            p.do("pe", lambda e: e.matmul(out=ps_m[:, 64:128], lhsT=ohot[:, 0:128],
                 rhs=bcr[:, 0:64], start=True, stop=True), waits=[dtaH[1]])
            yacc_ev = None
            for H in (0, 1):
                lo = 8 * H
                b_w = [bce_prev] + ([u_done] if n == 0 else [])
                p.do("dve", (lambda lo=lo: (lambda e: e.tensor_tensor(
                    out=b_all[:, lo:lo + 8, 1:1 + TT], in0=u_bf[:, lo:lo + 8, :],
                    in1=bc_sb[:, 0:1, :].broadcast_to([128, 8, TT]),
                    op=ALU.mult)))(), waits=b_w)
                rst_ev = p.do("dve", (lambda n=n, lo=lo: (lambda e: e.tensor_copy(
                    out=b_all[:, lo:lo + 8, 0], in_=state[:, lo:lo + 8, n])))())
                scn = p.do("dve", (lambda lo=lo: (lambda e: e.tensor_tensor_scan(
                    out=h_all[:, lo:lo + 8, :].rearrange("p i l -> p (i l)"),
                    data0=a_all[:, lo:lo + 8, :].rearrange("p i l -> p (i l)"),
                    data1=b_all[:, lo:lo + 8, :].rearrange("p i l -> p (i l)"),
                    initial=0.0, op0=ALU.mult, op1=ALU.add)))(),
                    waits=[aH_prev[H], rst_ev])
                stH_prev[H] = p.do("dve", (lambda n=n, lo=lo: (lambda e: e.tensor_copy(
                    out=state[:, lo:lo + 8, n], in_=h_all[:, lo:lo + 8, TT])))(),
                    waits=[scn])
                g_ev = p.do("dve", (lambda lo=lo: (lambda e: e.tensor_tensor(
                    out=dtA[:, lo:lo + 8, :].rearrange("p i l -> p (i l)"),
                    in0=h_all[:, lo:lo + 8, 1:1 + TT],
                    in1=bc_sb[:, 1:2, :].broadcast_to([128, 8, TT]),
                    op=ALU.mult)))(), waits=[aH_prev[H], dtaH[H]])
                if n == 0:
                    yacc_ev = p.do("dve", (lambda lo=lo: (lambda e: e.tensor_copy(
                        out=y_acc[:, lo:lo + 8, :].rearrange("p i l -> p (i l)"),
                        in_=dtA[:, lo:lo + 8, :].rearrange("p i l -> p (i l)"))))())
                else:
                    yacc_ev = p.do("dve", (lambda lo=lo: (lambda e: e.tensor_tensor(
                        out=y_acc[:, lo:lo + 8, :].rearrange("p i l -> p (i l)"),
                        in0=y_acc[:, lo:lo + 8, :].rearrange("p i l -> p (i l)"),
                        in1=dtA[:, lo:lo + 8, :].rearrange("p i l -> p (i l)"),
                        op=ALU.add)))())
                p.do("pe", (lambda H=H: (lambda e: e.matmul(
                    out=ps_m[:, 64 * H:64 * H + 64], lhsT=ohot[:, 0:128],
                    rhs=bcr[:, 0:64], start=True, stop=True)))(),
                    waits=[stH_prev[H]])

        for icx in range(IC):
            p.do("dve", (lambda icx=icx: (lambda e: e.scalar_tensor_tensor(
                out=dtA[:, 0, :], in0=xs_bf[:, icx, :], scalar=dvc[:, icx:icx + 1],
                in1=y_acc[:, icx, :], op0=ALU.mult, op1=ALU.add)))(),
                waits=[yacc_ev] if icx == 0 else [])
            yz_done = p.do("dve", (lambda icx=icx: (lambda e: e.tensor_tensor(
                out=x_raw[:, icx, 0:TT], in0=dtA[:, 0, :], in1=z_bf[:, icx, :],
                op=ALU.mult)))())

        accs = [(psel + [ps_bc])[m // 2][:, (m % 2) * TT:(m % 2 + 1) * TT]
                for m in range(DC)]
        mo = None
        for q in range(4):
            wo_ld = p.do("sync", (lambda q=q: DMA(
                woutb[:], wout_bf[4 * q:4 * q + 4]))(),
                waits=([yz_done, evac_ev[0], evac_ev[1], evac_ev[2], sel2,
                        bce_prev] if q == 0 else [mo]), inc=16)
            for k in range(4 * q, 4 * q + 4):
                for m in range(DC):
                    wd = [wo_ld, yz_done] if m == 0 else []
                    mo = p.do("pe", (lambda m=m, k=k: (lambda e: e.matmul(
                        out=accs[m],
                        lhsT=woutb[:, k % 4, 128 * m:128 * (m + 1)],
                        rhs=x_raw[:, k, 0:TT], start=(k == 0),
                        stop=(k == IC - 1))))(), waits=wd)
        xmid_done = None
        for m in range(DC):
            p.do("dve", (lambda m=m: (lambda e: e.tensor_tensor(
                out=tmpc[:], in0=accs[m], in1=h1b[:, m, :],
                op=ALU.add)))(), waits=[mo, h1b_ev] if m == 0 else [])
            xmid_done = p.do("dve", (lambda m=m: (lambda e: e.scalar_tensor_tensor(
                out=h1[:, m, :], in0=tmpc[:], scalar=gacl[:, m:m + 1],
                in1=x_t[:, m, :], op0=ALU.mult, op1=ALU.add)))())
        evac_ev[0] = evac_ev[1] = evac_ev[2] = xmid_done
        p.do("sync", (lambda t0=t0: DMA(
            o_xmid.rearrange("(c p) t -> p c t", p=128)[:, :, t0:t0 + TT], h1[:]))(),
            waits=[xmid_done], inc=16)

        sq3 = p.do("act", lambda e: e.activation(
            out=a_all[:, DC:2 * DC, 1:1 + TT],
            in_=h1[:].rearrange("p c l -> p (c l)"), func=AF.Square),
            waits=[xmid_done])
        for dc in range(DC):
            s1 = p.do("pe", (lambda dc=dc: (lambda e: e.matmul(
                out=ps_row[0:1, 0:TT], lhsT=ones_c[:], rhs=h1[:, dc, :],
                start=(dc == 0), stop=(dc == DC - 1))))(),
                waits=[xmid_done, bc3] if dc == 0 else [])
        for dc in range(DC):
            s2 = p.do("pe", (lambda dc=dc: (lambda e: e.matmul(
                out=ps_row[0:1, TT:2 * TT], lhsT=ones_c[:], rhs=a_all[:, DC + dc, 1:1 + TT],
                start=(dc == 0), stop=(dc == DC - 1))))(), waits=[sq3])
        p.do("dve", lambda e: e.tensor_scalar(out=mean_r, in0=ps_row[0:1, 0:TT],
             scalar1=1.0 / D, scalar2=None, op0=ALU.mult), waits=[s1, s2])
        p.do("dve", lambda e: e.tensor_scalar(out=var_r, in0=ps_row[0:1, TT:2 * TT],
             scalar1=1.0 / D, scalar2=EPS, op0=ALU.mult, op1=ALU.add))
        p.do("dve", lambda e: e.tensor_tensor(out=msq_r, in0=mean_r, in1=mean_r,
             op=ALU.mult))
        sb2 = p.do("dve", lambda e: e.tensor_tensor(out=var_r, in0=var_r,
                   in1=msq_r, op=ALU.subtract))
        p.do("act", lambda e: e.activation(out=var_r, in_=var_r, func=AF.Ln),
             waits=[sb2])
        rs2 = p.do("act", lambda e: e.activation(out=var_r, in_=var_r, func=AF.Exp,
                   scale=-0.5))
        bc4 = p.do("pe", lambda e: e.matmul(out=ps_b2[:, 0:TT], lhsT=ones_r[:],
                   rhs=mean_r, start=True, stop=True), waits=[sb2])
        bc5 = p.do("pe", lambda e: e.matmul(out=ps_b2[:, TT:2 * TT], lhsT=ones_r[:],
                   rhs=var_r, start=True, stop=True), waits=[rs2])
        mx_prev = None
        for dc in range(DC):
            p.do("dve", (lambda dc=dc: (lambda e: e.tensor_tensor(
                out=a_all[:, dc, 1:1 + TT], in0=h1[:, dc, :], in1=ps_b2[:, 0:TT],
                op=ALU.subtract)))(), waits=[bc4, bc5, sq3] if dc == 0 else [])
            p.do("dve", (lambda dc=dc: (lambda e: e.tensor_tensor(
                out=a_all[:, dc, 1:1 + TT], in0=a_all[:, dc, 1:1 + TT],
                in1=ps_b2[:, TT:2 * TT], op=ALU.mult)))())
            p.do("dve", (lambda dc=dc: (lambda e: e.tensor_scalar(
                out=a_all[:, dc, 1:1 + TT], in0=a_all[:, dc, 1:1 + TT],
                scalar1=cb1[:, dc:dc + 1],
                scalar2=mcols[:, 24 + dc:25 + dc], op0=ALU.mult, op1=ALU.add)))())
            rdev = p.do("dve", (lambda dc=dc: (lambda e: e.reduce_max(
                out=tmpc[:, dc:dc + 1], in_=a_all[:, dc, 1:1 + TT],
                axis=mybir.AxisListType.X,
                apply_absolute_value=True)))())
            mx_ev = p.do("dve", (lambda dc=dc: (lambda e: e.tensor_tensor(
                out=mxc[:], in0=mxc[:], in1=tmpc[:, dc:dc + 1], op=ALU.max)))(),
                waits=[rdev, mx_prev])
            mx_prev = mx_ev
        p.do("sync", (lambda t0=t0: DMA(
            o_h2.rearrange("(c p) t -> p c t", p=128)[:, :, t0:t0 + TT],
            a_all[:, 0:DC, 1:1 + TT]))(), waits=[mx_ev], inc=16)

    if DBG:
        allev = [p.ev("dve"), p.ev("act"), p.ev("pe")]
        p.do("sync", DMA(o_hid, hid[:].rearrange("p c l -> p (c l)")), waits=allev, inc=16)
        p.do("sync", DMA(o_xs, xs_bf[:].rearrange("p i l -> p (i l)")), waits=allev, inc=16)
        p.do("sync", DMA(o_zb, z_bf[:].rearrange("p i l -> p (i l)")), waits=allev, inc=16)
        p.do("sync", DMA(o_dtb, dt_bf[:].rearrange("p i l -> p (i l)")), waits=allev, inc=16)
        p.do("sync", DMA(o_ya, y_acc[:].rearrange("p i l -> p (i l)")), waits=allev, inc=16)
        p.do("sync", DMA(o_ga2, gacl[:]), waits=allev, inc=16)
        p.do("sync", DMA(o_ball, b_all[:].rearrange("p i l -> p (i l)")), waits=allev, inc=16)
        p.do("sync", DMA(o_hall, h_all[:].rearrange("p i l -> p (i l)")), waits=allev, inc=16)
        p.do("sync", DMA(o_bcsb, bc_sb[:].rearrange("p c l -> p (c l)")), waits=allev, inc=16)
        p.do("sync", DMA(o_cdbg, cdbg[:]), waits=allev, inc=16)
        p.do("sync", DMA(o_acl, acol[:]), waits=allev, inc=16)

    trm = p.do("pe", lambda e: e.transpose(out=ps_b2[0:1, 0:128], in_=mxc[:],
               identity=idf[:]), waits=[mx_ev, p.ev("dve")])
    mxr = p.do("dve", lambda e: e.reduce_max(out=rows[0:1, 0:1],
               in_=ps_b2[0:1, 0:128], axis=mybir.AxisListType.X), waits=[trm])
    p.do("sync", DMA(o_max, rows[0:1, 0:1]), waits=[mxr], inc=16)
    p.do("dve", lambda e: e.memset(tmpc[:, 0:1], 0.0), waits=[p.ev("sync")])

    _emit_all(nc, p)
    return nc



def build_l2(TT, NT):
    nc = bass.Bass("TRN2", target_bir_lowering=False)
    h2T = nc.dram_tensor("h2T", [D, T], F32, kind="ExternalInput").ap()
    mx8 = nc.dram_tensor("mx8", [1, 8], F32, kind="ExternalInput").ap()
    clipx = nc.dram_tensor("clipx", [1, 1], F32, kind="ExternalInput").ap()
    w_gate = nc.dram_tensor("w_gate", [I_FFN, D], F32, kind="ExternalInput").ap()
    w_up = nc.dram_tensor("w_up", [I_FFN, D], F32, kind="ExternalInput").ap()
    identb = nc.dram_tensor("identb", [128, 128], BF16, kind="ExternalInput").ap()
    o_gp = nc.dram_tensor("o_gp", [I_FFN, T], F32, kind="ExternalOutput").ap()
    o_maxg = nc.dram_tensor("o_maxg", [1, 1], F32, kind="ExternalOutput").ap()

    a = nc.alloc_sbuf_tensor
    wgT = a("wgT", [128, DC, I_FFN], BF16)
    wuT = a("wuT", [128, DC, I_FFN], BF16)
    swg = a("swg", [128, ICF], F32)
    swu = a("swu", [128, ICF], F32)
    wld = a("wld", [128, 2 * D], F32)
    wq = a("wq", [128, 2 * D], BF16)
    h2t = a("h2t", [128, DC, TT], F32)
    xq = a("xq", [128, DC, TT], BF16)
    gsil = a("gsil", [128, TT], F32)
    upsc = a("upsc", [128, TT], F32)
    gp = a("gp", [128, ICF, TT], F32)
    rowb = a("rowb", [1, 128], F32)
    colb = a("colb", [128, 4], F32)
    mxcl = a("mxcl", [128, 1], F32)
    tmp1 = a("tmp1", [128, TT], F32)
    idb = a("idb", [128, 128], BF16)
    idf2 = a("idf2", [128, 128], F32)
    onesr = a("onesr", [1, 128], F32)
    csts = a("csts", [128, 1], F32)
    nc.const_aps.register(F32, 0.5, csts[:, 0:1])

    ps0 = nc.alloc_psum_tensor("ps0", [128, 512], F32)
    ps1 = nc.alloc_psum_tensor("ps1", [128, 512], F32)
    psb = nc.alloc_psum_tensor("psb", [128, 512], F32)
    psr = nc.alloc_psum_tensor("psr", [128, 512], F32)

    p = Prog()

    ld0 = p.do("sync", DMA(rowb[0:1, 0:8], mx8), inc=16)
    ld1 = p.do("sync", DMA(rowb[0:1, 8:9], clipx), inc=16)
    ldi = p.do("sync", DMA(idb[:], identb), inc=16)
    p.do("dve", lambda e: e.memset(onesr[:], 1.0))
    p.do("dve", lambda e: e.memset(mxcl[:], 0.0))
    p.do("dve", lambda e: e.memset(csts[:, 0:1], 0.5))
    ones_ev = p.ev("dve")
    p._pending["act"].append(ones_ev)
    ic_ev = p.do("dve", lambda e: e.tensor_copy(out=idf2[:], in_=idb[:]),
                 waits=[ldi])

    gm = p.do("dve", lambda e: e.reduce_max(out=rowb[0:1, 9:10],
              in_=rowb[0:1, 0:8], axis=mybir.AxisListType.X), waits=[ld0])
    th = p.do("act", lambda e: e.activation(out=rowb[0:1, 10:11],
              in_=rowb[0:1, 8:9], func=AF.Tanh, scale=0.5), waits=[ld1])
    sg = p.do("dve", lambda e: e.tensor_scalar(out=rowb[0:1, 10:11],
              in0=rowb[0:1, 10:11], scalar1=0.5, scalar2=0.5, op0=ALU.mult,
              op1=ALU.add), waits=[th])
    pm = p.do("dve", lambda e: e.tensor_tensor(out=rowb[0:1, 11:12],
              in0=rowb[0:1, 9:10], in1=rowb[0:1, 10:11], op=ALU.mult),
              waits=[gm, sg])
    bcp = p.do("pe", lambda e: e.matmul(out=psr[:, 0:1], lhsT=onesr[:],
               rhs=rowb[0:1, 11:12], start=True, stop=True), waits=[pm, ones_ev])
    pc0 = p.do("dve", lambda e: e.tensor_copy(out=colb[:, 0:1], in_=psr[:, 0:1]),
               waits=[bcp])
    sxc = p.do("dve", lambda e: e.tensor_scalar(out=colb[:, 1:2], in0=colb[:, 0:1],
               scalar1=1.0 / 127.0, scalar2=None, op0=ALU.mult), waits=[pc0])
    rc0 = p.do("dve", lambda e: e.reciprocal(out=colb[:, 2:3], in_=colb[:, 0:1]),
               waits=[pc0])
    rsc = p.do("dve", lambda e: e.tensor_scalar(out=colb[:, 2:3], in0=colb[:, 2:3],
               scalar1=127.0, scalar2=None, op0=ALU.mult), waits=[rc0])

    def quant_w(src, dstT, swcol, first_ev):
        psbv0 = ps0[:].bitcast(BF16)
        psbv1 = ps1[:].bitcast(BF16)
        psv = [psbv0, psbv1]
        ld_h = [first_ev, None]
        q_h = [first_ev, first_ev]
        ev_h = [None, None]
        tprev = [None, None]
        ret = None
        for icx in range(ICF):
            sl = icx % 2
            ld = p.do("sync", (lambda icx=icx, sl=sl: DMA(
                wld[:, 1024 * sl:1024 * (sl + 1)],
                src[128 * icx:128 * (icx + 1), :]))(),
                waits=[q_h[sl]], inc=16)
            rm = p.do("dve", (lambda icx=icx, sl=sl: (lambda e: e.reduce_max(
                out=swcol[:, icx:icx + 1], in_=wld[:, 1024 * sl:1024 * (sl + 1)],
                axis=mybir.AxisListType.X,
                apply_absolute_value=True)))(), waits=[ld])
            rcp = p.do("dve", (lambda icx=icx, sl=sl: (lambda e: e.reciprocal(
                out=tmp1[:, sl:sl + 1], in_=swcol[:, icx:icx + 1])))(), waits=[rm])
            p.do("dve", (lambda sl=sl: (lambda e: e.tensor_scalar(
                out=wq[:, 1024 * sl:1024 * (sl + 1)],
                in0=wld[:, 1024 * sl:1024 * (sl + 1)],
                scalar1=tmp1[:, sl:sl + 1], scalar2=127.0, op0=ALU.mult,
                op1=ALU.mult)))(), waits=[rcp, ev_h[sl]])
            qv = p.do("dve", (lambda sl=sl: (lambda e: e.tensor_scalar(
                out=wq[:, 1024 * sl:1024 * (sl + 1)],
                in0=wq[:, 1024 * sl:1024 * (sl + 1)],
                scalar1=RND, scalar2=RND, op0=ALU.add, op1=ALU.subtract)))())
            q_h[sl] = qv
            for hc in range(DC):
                ps = hc % 2
                tr = p.do("pe", (lambda hc=hc, sl=sl, ps=ps: (lambda e: e.transpose(
                    out=psv[ps][:, 0:128],
                    in_=wq[:, 1024 * sl + 128 * hc:1024 * sl + 128 * (hc + 1)],
                    identity=idb[:])))(), waits=[qv, ldi, tprev[ps]])
                tprev[ps] = p.do("act", (lambda hc=hc, icx=icx, ps=ps: (lambda e: e.activation(
                    out=dstT[:, hc, 128 * icx:128 * (icx + 1)],
                    in_=psv[ps][:, 0:128],
                    func=AF.Copy)))(), waits=[tr])
            ev_h[sl] = tprev[(DC - 1) % 2]
            ret = p.ev("act")
        return ret

    qg = quant_w(w_gate, wgT, swg, None)
    qu = quant_w(w_up, wuT, swu, qg)
    cg = p.do("dve", lambda e: e.tensor_scalar(out=swg[:], in0=swg[:],
              scalar1=colb[:, 1:2], scalar2=1.0 / 127.0, op0=ALU.mult,
              op1=ALU.mult), waits=[qg, sxc])
    cu = p.do("dve", lambda e: e.tensor_scalar(out=swu[:], in0=swu[:],
              scalar1=colb[:, 1:2], scalar2=1.0 / 127.0, op0=ALU.mult,
              op1=ALU.mult), waits=[qu, sxc])

    mx_ev = None
    ldh_s = [None] * NT
    ldh_s[0] = p.do("sync", DMA(
        h2t[:], h2T.rearrange("(c p) t -> p c t", p=128)[:, :, 0:TT]), inc=16)
    gmm_prev = None
    umm_prev = None
    dgp_prev = None
    gsl = [gsil[:, 0:TT], gsil[:, TT:2 * TT]] if False else None
    for it in range(NT):
        t0 = it * TT
        q1w = [ldh_s[it], rsc]
        p.do("dve", lambda e: e.tensor_scalar(
            out=h2t[:].rearrange("p c l -> p (c l)"),
            in0=h2t[:].rearrange("p c l -> p (c l)"), scalar1=colb[:, 2:3],
            scalar2=None, op0=ALU.mult), waits=q1w)
        p.do("dve", lambda e: e.tensor_scalar(
            out=h2t[:].rearrange("p c l -> p (c l)"),
            in0=h2t[:].rearrange("p c l -> p (c l)"), scalar1=RND, scalar2=RND,
            op0=ALU.add, op1=ALU.subtract))
        q3 = p.do("dve", lambda e: e.tensor_scalar(
            out=xq[:].rearrange("p c l -> p (c l)"),
            in0=h2t[:].rearrange("p c l -> p (c l)"), scalar1=127.0,
            scalar2=-128.0, op0=ALU.min, op1=ALU.max),
            waits=[gmm_prev, umm_prev])
        if it + 1 < NT:
            ldh_s[it + 1] = p.do("sync", (lambda t0=t0: DMA(
                h2t[:], h2T.rearrange("(c p) t -> p c t", p=128)[:, :,
                        t0 + TT:t0 + 2 * TT]))(), waits=[q3], inc=16)

        gbank = [ps0, psb.bitcast(F32)[:, 0:512] if False else psb]
        gbank = [ps0, psb]
        ubank = [ps1, psr]
        evg = [None, None]
        evu = [None, None]
        for m in range(ICF):
            gb_ = gbank[m % 2]
            ub_ = ubank[m % 2]
            for k in range(DC):
                g1 = p.do("pe", (lambda m=m, k=k, gb_=gb_: (lambda e: e.matmul(
                    out=gb_[:, 0:TT], lhsT=wgT[:, k, 128 * m:128 * (m + 1)],
                    rhs=xq[:, k, :], start=(k == 0), stop=(k == DC - 1))))(),
                    waits=[q3, cg, evg[m % 2]] if k == 0 else [])
            gmm_prev = g1
            for k in range(DC):
                u1 = p.do("pe", (lambda m=m, k=k, ub_=ub_: (lambda e: e.matmul(
                    out=ub_[:, 0:TT], lhsT=wuT[:, k, 128 * m:128 * (m + 1)],
                    rhs=xq[:, k, :], start=(k == 0), stop=(k == DC - 1))))(),
                    waits=[cu, evu[m % 2]] if k == 0 else [])
            umm_prev = u1
            ge = p.do("act", (lambda m=m, gb_=gb_: (lambda e: e.activation(
                out=gsil[:], in_=gb_[:, 0:TT], func=AF.Silu,
                scale=swg[:, m:m + 1])))(), waits=[g1])
            evg[m % 2] = ge
            ue = p.do("act", (lambda m=m, ub_=ub_: (lambda e: e.activation(
                out=upsc[:], in_=ub_[:, 0:TT], func=AF.Copy,
                scale=swu[:, m:m + 1])))(), waits=[u1])
            evu[m % 2] = ue
            gpv = p.do("dve", (lambda m=m: (lambda e: e.tensor_tensor(
                out=gp[:, m, :], in0=gsil[:], in1=upsc[:], op=ALU.mult)))(),
                waits=[ge, ue, dgp_prev if m == 0 else None])
            p._pending["act"].append(gpv)
            rdev = p.do("dve", (lambda m=m: (lambda e: e.reduce_max(
                out=tmp1[:, 1:2], in_=gp[:, m, :], axis=mybir.AxisListType.X,
                apply_absolute_value=True)))(), waits=[mx_ev])
            mx_ev = p.do("dve", lambda e: e.tensor_tensor(
                out=mxcl[:], in0=mxcl[:], in1=tmp1[:, 1:2], op=ALU.max),
                waits=[rdev])
        dgp_prev = p.do("sync", (lambda t0=t0: DMA(
            o_gp.rearrange("(c p) t -> p c t", p=128)[:, :, t0:t0 + TT], gp[:]))(),
            waits=[mx_ev], inc=16)

    trm = p.do("pe", lambda e: e.transpose(out=psr[0:1, 0:128], in_=mxcl[:],
               identity=idf2[:]), waits=[mx_ev, ic_ev])
    fm = p.do("dve", lambda e: e.reduce_max(out=rowb[0:1, 12:13],
              in_=psr[0:1, 0:128], axis=mybir.AxisListType.X), waits=[trm])
    p.do("sync", DMA(o_maxg, rowb[0:1, 12:13]), waits=[fm], inc=16)
    p.do("dve", lambda e: e.memset(tmp1[:, 2:3], 0.0), waits=[p.ev("sync")])

    _emit_all(nc, p)
    return nc



def build_l3(TT, NT):
    nc = bass.Bass("TRN2", target_bir_lowering=False)
    gpT = nc.dram_tensor("gpT", [I_FFN, T], F32, kind="ExternalInput").ap()
    xmidT = nc.dram_tensor("xmidT", [D, T], F32, kind="ExternalInput").ap()
    mx8 = nc.dram_tensor("mx8", [1, 8], F32, kind="ExternalInput").ap()
    clipd = nc.dram_tensor("clipd", [1, 1], F32, kind="ExternalInput").ap()
    w_down = nc.dram_tensor("w_down", [D, I_FFN], F32, kind="ExternalInput").ap()
    gb_col = nc.dram_tensor("gb_col", [128, DC], F32, kind="ExternalInput").ap()
    identb = nc.dram_tensor("identb", [128, 128], BF16, kind="ExternalInput").ap()
    o_out = nc.dram_tensor("o_out", [D, T], F32, kind="ExternalOutput").ap()

    a = nc.alloc_sbuf_tensor
    wdT = a("wdT", [128, ICF, D], BF16)
    swd = a("swd", [128, DC], F32)
    wld = a("wld", [128, I_FFN], F32)
    wq = a("wq", [128, I_FFN], BF16)
    gpt = a("gpt", [128, 2, ICF, TT], F32)
    hq = a("hq", [128, ICF, TT], BF16)
    xmt = a("xmt", [128, 2, DC, TT], F32)
    outt = a("outt", [128, DC, TT], F32)
    rowb = a("rowb", [1, 128], F32)
    colb = a("colb", [128, 4], F32)
    gbc = a("gbc", [128, DC], F32)
    comb = a("comb", [128, DC], F32)
    csts = a("csts", [128, 1], F32)
    nc.const_aps.register(F32, 0.5, csts[:, 0:1])
    tmp1 = a("tmp1", [128, TT], F32)
    idb = a("idb", [128, 128], BF16)
    onesr = a("onesr", [1, 128], F32)

    ps0 = nc.alloc_psum_tensor("ps0", [128, 512], F32)
    ps1 = nc.alloc_psum_tensor("ps1", [128, 512], F32)
    psb = nc.alloc_psum_tensor("psb", [128, 512], F32)
    psr = nc.alloc_psum_tensor("psr", [128, 512], F32)

    p = Prog()

    ld0 = p.do("sync", DMA(rowb[0:1, 0:8], mx8), inc=16)
    ld1 = p.do("sync", DMA(rowb[0:1, 8:9], clipd), inc=16)
    ldi = p.do("sync", DMA(idb[:], identb), inc=16)
    ldg = p.do("sync", DMA(gbc[:], gb_col), inc=16)
    p.do("dve", lambda e: e.memset(onesr[:], 1.0))
    p.do("dve", lambda e: e.memset(csts[:, 0:1], 0.5))
    ones_ev = p.ev("dve")
    p._pending["act"].append(ones_ev)

    gm = p.do("dve", lambda e: e.reduce_max(out=rowb[0:1, 9:10],
              in_=rowb[0:1, 0:8], axis=mybir.AxisListType.X), waits=[ld0])
    th = p.do("act", lambda e: e.activation(out=rowb[0:1, 10:11],
              in_=rowb[0:1, 8:9], func=AF.Tanh, scale=0.5), waits=[ld1])
    sg = p.do("dve", lambda e: e.tensor_scalar(out=rowb[0:1, 10:11],
              in0=rowb[0:1, 10:11], scalar1=0.5, scalar2=0.5, op0=ALU.mult,
              op1=ALU.add), waits=[th])
    pm = p.do("dve", lambda e: e.tensor_tensor(out=rowb[0:1, 11:12],
              in0=rowb[0:1, 9:10], in1=rowb[0:1, 10:11], op=ALU.mult),
              waits=[gm, sg])
    bcp = p.do("pe", lambda e: e.matmul(out=psr[:, 0:1], lhsT=onesr[:],
               rhs=rowb[0:1, 11:12], start=True, stop=True), waits=[pm, ones_ev])
    pc0 = p.do("dve", lambda e: e.tensor_copy(out=colb[:, 0:1], in_=psr[:, 0:1]),
               waits=[bcp])
    sxc = p.do("dve", lambda e: e.tensor_scalar(out=colb[:, 1:2], in0=colb[:, 0:1],
               scalar1=1.0 / 127.0, scalar2=None, op0=ALU.mult), waits=[pc0])
    rc0 = p.do("dve", lambda e: e.reciprocal(out=colb[:, 2:3], in_=colb[:, 0:1]),
               waits=[pc0])
    rsc = p.do("dve", lambda e: e.tensor_scalar(out=colb[:, 2:3], in0=colb[:, 2:3],
               scalar1=127.0, scalar2=None, op0=ALU.mult), waits=[rc0])

    prev = None
    for hc in range(DC):
        ld = p.do("sync", (lambda hc=hc: DMA(
            wld[:], w_down[128 * hc:128 * (hc + 1), :]))(), waits=[prev], inc=16)
        rm = p.do("dve", (lambda hc=hc: (lambda e: e.reduce_max(
            out=swd[:, hc:hc + 1], in_=wld[:], axis=mybir.AxisListType.X,
            apply_absolute_value=True)))(), waits=[ld])
        rcp = p.do("dve", (lambda hc=hc: (lambda e: e.reciprocal(
            out=tmp1[:, 0:1], in_=swd[:, hc:hc + 1])))(), waits=[rm])
        p.do("dve", lambda e: e.tensor_scalar(out=wq[:], in0=wld[:],
             scalar1=tmp1[:, 0:1], scalar2=127.0, op0=ALU.mult, op1=ALU.mult),
             waits=[rcp])
        qv = p.do("dve", lambda e: e.tensor_scalar(out=wq[:], in0=wq[:],
                  scalar1=RND, scalar2=RND, op0=ALU.add, op1=ALU.subtract))
        tprev = None
        for icx in range(ICF):
            tr = p.do("pe", (lambda icx=icx: (lambda e: e.transpose(
                out=psb[:].bitcast(BF16)[:, 0:128],
                in_=wq[:, 128 * icx:128 * (icx + 1)],
                identity=idb[:])))(), waits=[qv, ldi, tprev])
            tprev = p.do("act", (lambda icx=icx, hc=hc: (lambda e: e.activation(
                out=wdT[:, icx, 128 * hc:128 * (hc + 1)],
                in_=psb[:].bitcast(BF16)[:, 0:128],
                func=AF.Copy)))(), waits=[tr])
        prev = tprev
    cmb = p.do("dve", lambda e: e.tensor_scalar(out=comb[:], in0=swd[:],
               scalar1=colb[:, 1:2], scalar2=1.0 / 127.0, op0=ALU.mult,
               op1=ALU.mult), waits=[prev, sxc])

    p.barrier()

    ldg2s = [None] * NT
    ldxs = [None] * NT
    q3s = [None] * NT
    for pre in range(min(2, NT)):
        t0 = pre * TT
        ldg2s[pre] = p.do("sync", (lambda t0=t0, pre=pre: DMA(
            gpt[:, pre % 2], gpT.rearrange("(c p) t -> p c t", p=128)[:, :, t0:t0 + TT]))(),
            inc=16)
        ldxs[pre] = p.do("sync", (lambda t0=t0, pre=pre: DMA(
            xmt[:, pre % 2], xmidT.rearrange("(c p) t -> p c t", p=128)[:, :, t0:t0 + TT]))(),
            inc=16)
    dmaout_prev = None
    mm_hist = [None, None]
    for it in range(NT):
        sl = it % 2
        gv = gpt[:, sl]
        xv = xmt[:, sl]
        p.do("dve", (lambda gv=gv: (lambda e: e.tensor_scalar(
            out=gv.rearrange("p c l -> p (c l)"),
            in0=gv.rearrange("p c l -> p (c l)"), scalar1=colb[:, 2:3],
            scalar2=None, op0=ALU.mult)))(), waits=[ldg2s[it], rsc])
        p.do("dve", (lambda gv=gv: (lambda e: e.tensor_scalar(
            out=gv.rearrange("p c l -> p (c l)"),
            in0=gv.rearrange("p c l -> p (c l)"), scalar1=RND, scalar2=RND,
            op0=ALU.add, op1=ALU.subtract)))())
        q3 = p.do("dve", (lambda gv=gv: (lambda e: e.tensor_scalar(
            out=hq[:].rearrange("p c l -> p (c l)"),
            in0=gv.rearrange("p c l -> p (c l)"), scalar1=127.0,
            scalar2=-128.0, op0=ALU.min, op1=ALU.max)))(),
            waits=[mm_hist[0], mm_hist[1]])
        q3s[it] = q3
        evac = [None, None]
        out_done = None
        for m in range(DC):
            bank = m % 2
            pst = ps0 if bank == 0 else ps1
            for k in range(ICF):
                mm = p.do("pe", (lambda m=m, k=k, pst=pst: (lambda e: e.matmul(
                    out=pst[:, 0:TT], lhsT=wdT[:, k, 128 * m:128 * (m + 1)],
                    rhs=hq[:, k, :], start=(k == 0), stop=(k == ICF - 1))))(),
                    waits=[q3, cmb, evac[bank]] if k == 0 else [])
            mm_hist[bank] = mm
            p.do("dve", (lambda m=m, pst=pst: (lambda e: e.tensor_scalar(
                out=tmp1[:], in0=pst[:, 0:TT], scalar1=comb[:, m:m + 1],
                scalar2=None, op0=ALU.mult)))(), waits=[mm])
            out_done = p.do("dve", (lambda m=m, xv=xv: (lambda e: e.scalar_tensor_tensor(
                out=outt[:, m, :], in0=tmp1[:], scalar=gbc[:, m:m + 1],
                in1=xv[:, m, :], op0=ALU.mult, op1=ALU.add)))(),
                waits=[ldxs[it], ldg, dmaout_prev] if m == 0 else [])
            evac[bank] = out_done
        dmaout_prev = p.do("sync", (lambda it=it: DMA(
            o_out.rearrange("(c p) t -> p c t", p=128)[:, :, it * TT:(it + 1) * TT],
            outt[:]))(), waits=[out_done], inc=16)
        if it + 2 < NT:
            t2 = (it + 2) * TT
            ldg2s[it + 2] = p.do("sync", (lambda t2=t2, sl=sl: DMA(
                gpt[:, sl], gpT.rearrange("(c p) t -> p c t", p=128)[:, :, t2:t2 + TT]))(),
                waits=[q3], inc=16)
            ldxs[it + 2] = p.do("sync", (lambda t2=t2, sl=sl: DMA(
                xmt[:, sl], xmidT.rearrange("(c p) t -> p c t", p=128)[:, :, t2:t2 + TT]))(),
                waits=[out_done], inc=16)

    p.do("dve", lambda e: e.memset(tmp1[:, 2:3], 0.0), waits=[p.ev("sync")])

    _emit_all(nc, p)
    return nc


_CACHE = {}


def _programs():
    if "l1" not in _CACHE:
        _CACHE["l1"] = build_l1(256, 8)
        _CACHE["l2"] = build_l2(512, 4)
        _CACHE["l3"] = build_l3(256, 8)
    return _CACHE["l1"], _CACHE["l2"], _CACHE["l3"]


def _colize(v, chunks):
    return np.ascontiguousarray(np.asarray(v, np.float32).reshape(chunks, 128).T)


def kernel(L0_attn_residual, L0_attn_norm_w, L0_attn_W_in, L0_attn_conv_w,
           L0_attn_conv_b, L0_attn_W_x, L0_attn_W_dt, L0_attn_b_dt,
           L0_attn_A_log, L0_attn_D, L0_attn_W_out, L0_ffn_clip_x,
           L0_ffn_clip_down, L0_ffn_w_gate, L0_ffn_w_up, L0_ffn_w_down,
           L0_c, L0_wmod):
    x = np.asarray(L0_attn_residual, np.float32)
    l1, l2, l3 = _programs()

    onehot = np.zeros((32, 32 * 128), BF)
    for n in range(32):
        onehot[n, 128 * n:128 * (n + 1)] = 1.0
    identf = np.eye(128, dtype=np.float32)
    identb = np.eye(128, dtype=BF)

    normw_col = _colize(L0_attn_norm_w, DC)
    convb_col = _colize(L0_attn_conv_b, IC)
    bdt_col = _colize(L0_attn_b_dt, IC)
    dvec_col = _colize(L0_attn_D, IC)
    cw = np.asarray(L0_attn_conv_w, np.float32).reshape(IC, 128, 4)
    convw_col = np.ascontiguousarray(cw.transpose(1, 0, 2).reshape(128, IC * 4))
    al = np.asarray(L0_attn_A_log, np.float32).reshape(IC, 128, NS)
    alog_col = np.ascontiguousarray(al.transpose(1, 0, 2).reshape(128, IC * NS))

    in1 = [dict(
        xT=np.ascontiguousarray(x[b].T),
        c_col=np.ascontiguousarray(np.asarray(L0_c, np.float32)[b].reshape(DC, 128).T),
        wmod=np.asarray(L0_wmod, np.float32),
        w_in=np.asarray(L0_attn_W_in, np.float32),
        w_x=np.asarray(L0_attn_W_x, np.float32),
        w_dt=np.asarray(L0_attn_W_dt, np.float32),
        w_out=np.asarray(L0_attn_W_out, np.float32),
        alog_col=alog_col, normw_col=normw_col, convw_col=convw_col,
        convb_col=convb_col, bdt_col=bdt_col, dvec_col=dvec_col,
        onehot=onehot, identf=identf,
    ) for b in range(NCORES)]
    r1 = run_bass_kernel_spmd(l1, in1, core_ids=list(range(NCORES)))

    mx8 = np.ascontiguousarray(np.concatenate(
        [r1.results[b]["o_max"].reshape(1, 1) for b in range(NCORES)], axis=1))
    in2 = [dict(
        h2T=r1.results[b]["o_h2"], mx8=mx8,
        clipx=np.asarray(L0_ffn_clip_x, np.float32).reshape(1, 1),
        w_gate=np.asarray(L0_ffn_w_gate, np.float32),
        w_up=np.asarray(L0_ffn_w_up, np.float32),
        identb=identb,
    ) for b in range(NCORES)]
    r2 = run_bass_kernel_spmd(l2, in2, core_ids=list(range(NCORES)))

    mxg8 = np.ascontiguousarray(np.concatenate(
        [r2.results[b]["o_maxg"].reshape(1, 1) for b in range(NCORES)], axis=1))
    in3 = [dict(
        gpT=r2.results[b]["o_gp"], xmidT=r1.results[b]["o_xmid"], mx8=mxg8,
        clipd=np.asarray(L0_ffn_clip_down, np.float32).reshape(1, 1),
        w_down=np.asarray(L0_ffn_w_down, np.float32),
        gb_col=r1.results[b]["o_gb"], identb=identb,
    ) for b in range(NCORES)]
    r3 = run_bass_kernel_spmd(l3, in3, core_ids=list(range(NCORES)))

    return np.stack([np.ascontiguousarray(r3.results[b]["o_out"].T)
                     for b in range(NCORES)], axis=0)
